# version 08
# baseline (speedup 1.0000x reference)
import numpy as np
from contextlib import ExitStack

import concourse.bass as bass
import concourse.mybir as mybir
from concourse.bass_utils import run_bass_kernel_spmd

F32 = mybir.dt.float32
BF16 = mybir.dt.bfloat16
AF = mybir.ActivationFunctionType
ALU = mybir.AluOpType
AX = mybir.AxisListType

COMPUTE = ("tensor", "vector", "scalar", "gpsimd")
QUEUES = ("qsync", "qgps", "qscal")
Q_ENG = {"qsync": "sync", "qgps": "gpsimd", "qscal": "scalar"}
STREAMS = ("tensor", "vector", "scalar", "gpsimd", "sync")

BIG = 1 << 50

MAMBA_FP32D = False
ATTN_PAIRED = True
KVBF = True

T = 4096
D = 1024
NPT = D // 128
SLAB = 512
NSLAB = T // SLAB


def rr(name, lo=0, hi=BIG):
    return (name, lo, hi)


class Prog:

    def __init__(self, nc, stack):
        self.nc = nc
        self.stack = stack
        self.streams = {e: [] for e in STREAMS}
        self.sems = {}
        for s in COMPUTE + QUEUES:
            self.sems[s] = stack.enter_context(nc.semaphore(name="sem_" + s))
        self.tickets = {s: 0 for s in COMPUTE + QUEUES}
        self.writes = {}
        self.reads = {}
        self.water = {e: {} for e in STREAMS}
        self.nsteps = 0
        self.sim_safe = False

    def _collect(self, keys_r, keys_w):
        need = {}
        for (n, lo, hi) in keys_r:
            for (l2, h2, s, v) in self.writes.get(n, ()):
                if l2 < hi and lo < h2 and v > need.get(s, 0):
                    need[s] = v
        for (n, lo, hi) in keys_w:
            for (l2, h2, s, v) in self.writes.get(n, ()):
                if l2 < hi and lo < h2 and v > need.get(s, 0):
                    need[s] = v
            for (l2, h2, s, v) in self.reads.get(n, ()):
                if l2 < hi and lo < h2 and v > need.get(s, 0):
                    need[s] = v
        return need

    def _emit_waits(self, stream, own_kind, need):
        wm = self.water[stream]
        if self.sim_safe:
            for s in list(need):
                if s in QUEUES:
                    need[s] = 16 * self.tickets[s]
        for s, v in sorted(need.items()):
            if s == own_kind and s == "tensor":
                continue
            if wm.get(s, 0) >= v:
                continue
            sem = self.sems[s]
            self.streams[stream].append(lambda e, sem=sem, v=v: e.wait_ge(sem, v))
            wm[s] = v

    def _record(self, keys_r, keys_w, kind, val):
        for (n, lo, hi) in keys_r:
            self.reads.setdefault(n, []).append((lo, hi, kind, val))
        for (n, lo, hi) in keys_w:
            wl = self.writes.setdefault(n, [])
            wl[:] = [r for r in wl if not (lo <= r[0] and r[1] <= hi)]
            wl.append((lo, hi, kind, val))
            rl = self.reads.get(n)
            if rl:
                rl[:] = [r for r in rl if not (lo <= r[0] and r[1] <= hi)]

    def op(self, eng, fn, reads=(), writes=(), tag=""):
        assert eng in COMPUTE
        need = self._collect(reads, writes)
        self._emit_waits(eng, eng, need)
        self.tickets[eng] += 1
        val = self.tickets[eng]
        sem = self.sems[eng]

        def thunk(e, fn=fn, sem=sem):
            r = fn(e)
            if isinstance(r, (list, tuple)):
                r = r[-1]
            r.then_inc(sem, 1)

        self.streams[eng].append(thunk)
        self._record(reads, writes, eng, val)
        self.nsteps += 1
        return val

    def dma(self, queue, out_ap, in_ap, reads=(), writes=(), tag="", **kw):
        assert queue in QUEUES
        stream = Q_ENG[queue]
        need = self._collect(reads, writes)
        self._emit_waits(stream, queue, need)
        self.tickets[queue] += 1
        val = 16 * self.tickets[queue]
        sem = self.sems[queue]

        def thunk(e, out_ap=out_ap, in_ap=in_ap, sem=sem, kw=kw):
            e.dma_start(out_ap, in_ap, **kw).then_inc(sem, 16)

        self.streams[stream].append(thunk)
        self._record(reads, writes, queue, val)
        self.nsteps += 1
        return val

    def finalize(self):
        finals = []
        for s in COMPUTE:
            if self.tickets[s]:
                finals.append((self.sems[s], self.tickets[s]))
        for q in QUEUES:
            if self.tickets[q]:
                finals.append((self.sems[q], 16 * self.tickets[q]))

        def thunk(e, finals=finals):
            for sem, v in finals:
                e.wait_ge(sem, v)

        self.streams["sync"].append(thunk)
        with self.nc.Block() as block:
            for engname in STREAMS:
                thunks = self.streams[engname]
                if not thunks:
                    continue
                dec = getattr(block, engname)

                def make(th):
                    def _(eng):
                        for t in th:
                            t(eng)
                    return _

                dec(make(thunks))

    def sbuf(self, name, shape, dtype):
        return self.stack.enter_context(self.nc.sbuf_tensor(name, shape, dtype))

    def psum(self, name, shape, dtype):
        return self.stack.enter_context(self.nc.psum_tensor(name, shape, dtype))

    def barrier(self):
        vals = [(s, self.tickets[s]) for s in COMPUTE if self.tickets[s]]
        vals += [(q, 16 * self.tickets[q]) for q in QUEUES if self.tickets[q]]
        for stream in STREAMS:
            wm = self.water[stream]
            for kind, v in vals:
                if wm.get(kind, 0) >= v:
                    continue
                sem = self.sems[kind]
                self.streams[stream].append(lambda e, sem=sem, v=v: e.wait_ge(sem, v))
                wm[kind] = v
        self.writes.clear()
        self.reads.clear()

    def phase(self):
        prog = self

        class _Phase:
            def __enter__(ph):
                ph.st = ExitStack()
                ph.old = prog.stack
                prog.stack = ph.st
                return ph

            def __exit__(ph, *a):
                prog.stack = ph.old
                prog.barrier()
                ph.st.close()
                return False

        return _Phase()



def build_consts():
    c = {}
    c["c_id_f32"] = np.eye(128, dtype=np.float32)
    c["c_id_bf16"] = np.eye(128, dtype=np.float32)
    c["c_ones_col"] = np.ones((128, 1), np.float32)
    c["c_ones_colf"] = np.ones((128, 1), np.float32)
    c["c_ones128"] = np.ones((128, 128), np.float32)
    c["c_ones_row"] = np.ones((1, 128), np.float32)
    r = np.arange(128)
    c["c_utri"] = (r[:, None] <= r[None, :]).astype(np.float32)
    c["c_ltris"] = (r[:, None] > r[None, :]).astype(np.float32)
    c["c_causmask"] = np.where(r[:, None] <= r[None, :], 0.0, 1e30).astype(np.float32)
    oh8 = np.zeros((8, 8 * 128), np.float32)
    for e in range(8):
        oh8[e, e * 128:(e + 1) * 128] = 1.0
    c["c_oh8"] = oh8
    oh16 = np.zeros((16, 16 * 128), np.float32)
    for e in range(16):
        oh16[e, e * 128:(e + 1) * 128] = 1.0
    c["c_oh16"] = oh16
    oh64 = np.zeros((64, 64 * 128), np.float32)
    for e in range(64):
        oh64[e, e * 128:(e + 1) * 128] = 1.0
    c["c_oh64"] = oh64
    half = 32
    inv_freq = 1.0 / (10000.0 ** (np.arange(0, half, dtype=np.float64) / half))
    freqs = np.arange(T, dtype=np.float64)[:, None] * inv_freq[None, :]
    cos = np.cos(freqs).T.astype(np.float32)
    sin = np.sin(freqs).T.astype(np.float32)
    c["c_rope_cos2"] = np.concatenate([cos, cos], 0)
    c["c_rope_sin2"] = np.concatenate([sin, sin], 0)
    c["c_ropec4"] = np.tile(cos, (4, 1))
    c["c_ropes4"] = np.tile(np.concatenate([-sin, sin], 0), (2, 1))
    sw = np.zeros((128, 128), np.float32)
    for j in range(128):
        sw[j, j ^ 32] = 1.0
    c["c_swap32"] = sw
    hh = np.arange(16)
    cc = np.arange(512)
    c["c_tgmask"] = (cc[None, :] // 64 == hh[:, None] // 2).astype(np.float32)
    pp = np.arange(128)
    c["c_tgsel"] = (pp[None, :] // 64 == hh[:, None] % 2).astype(np.float32)
    e2 = np.zeros((128, 1024), np.float32)
    for g in range(2):
        for fl in range(512):
            rl = (fl // 64) * 16 + (fl % 64) // 4
            e2[rl, g * 512 + fl] = 0.1
    c["c_erep2"] = e2
    hm = np.zeros((128, 32), np.float32)
    for pf in range(4):
        for hh in range(8):
            for i in range(128):
                if hh == 2 * pf + i // 64:
                    hm[i, pf * 8 + hh] = 1.0
    c["c_hmask"] = hm
    return c


class Ctx:

    CONST_SHAPES = {
        "c_id_f32": (128, 128, F32), "c_id_bf16": (128, 128, BF16),
        "c_ones_col": (128, 1, BF16), "c_ones_colf": (128, 1, F32),
        "c_ones_row": (1, 128, F32),
        "c_utri": (128, 128, F32), "c_ltris": (128, 128, F32),
        "c_causmask": (128, 128, F32),
        "c_oh8": (8, 8 * 128, BF16), "c_oh16": (16, 16 * 128, BF16),
        "c_oh64": (64, 64 * 128, BF16),
        "c_rope_cos2": (64, T, F32), "c_rope_sin2": (64, T, F32),
        "c_tgmask": (16, 512, F32), "c_tgsel": (16, 128, F32),
        "c_erep2": (128, 1024, BF16), "c_hmask": (128, 32, BF16),
        "c_swap32": (128, 128, BF16), "c_ropec4": (128, T, BF16), "c_ropes4": (128, T, BF16),
        "c_ones128": (128, 128, F32),
    }

    def __init__(self, prog):
        self.p = prog
        self.dram = {}
        self.sb = {}
        self._uc = 0
        for n in ("c_ones_col", "c_ones_colf", "c_ones_row", "c_id_f32"):
            self.sb[n] = self.use_const(n)
        self.XS = prog.sbuf("XS", [128, NPT * SLAB], F32)
        self.XNB = prog.sbuf("XNB", [128, NPT * SLAB], BF16)
        self.TMP = prog.sbuf("TMP", [128, SLAB], F32)
        self.ROWS = prog.sbuf("ROWS", [1, 4 * SLAB], F32)
        self.COLS = prog.sbuf("COLS", [128, 12], F32)
        prog.op("vector", lambda e: e.memset(self.COLS[:, 0:1], 1e-5), writes=[rr("EPSC")])
        prog.op("vector", lambda e: e.memset(self.COLS[:, 1:2], 1.0), writes=[rr("ONEC")])
        self.PS = [prog.psum("PSB%d" % i, [128, 512], F32) for i in range(2)]

    def use_const(self, name):
        r, cdim, dt = self.CONST_SHAPES[name]
        if name not in self.dram:
            self.dram[name] = self.p.nc.dram_tensor(name, [r, cdim], F32,
                                                    kind="ExternalInput")
        self._uc += 1
        t = self.p.sbuf("sbc%d_%s" % (self._uc, name), [r, cdim], dt)
        q = "qgps" if dt == BF16 else "qsync"
        self.p.dma(q, t[:, :], self.dram[name].ap(), writes=[rr(name)])
        return t


def load_x_slab(ctx, x_dram, xkey, s, stats=True, rowbufs=None, gb=None):
    p = ctx.p
    if rowbufs is None:
        mrow_ap = ctx.ROWS[0:1, 2 * SLAB:3 * SLAB]
        rrow_ap = ctx.ROWS[0:1, 3 * SLAB:4 * SLAB]
        mkey, rkey = rr("ROWS", 2 * SLAB, 3 * SLAB), rr("ROWS", 3 * SLAB, 4 * SLAB)
    else:
        mrow_ap, rrow_ap, mkey, rkey = rowbufs
    xa = x_dram.ap()[:, s * SLAB:(s + 1) * SLAB].rearrange("(pt p) t -> p pt t", p=128)
    xs3 = ctx.XS[:, :].rearrange("p (pt t) -> p pt t", t=SLAB)
    p.dma("qsync", xs3, xa, reads=[rr(xkey, s, s + 1)], writes=[rr("XS")])
    p.op("vector", lambda e: e.tensor_copy(out=ctx.XNB[:, :], in_=ctx.XS[:, :]),
         reads=[rr("XS")], writes=[rr("XNB")])
    if stats:
        onec = ctx.sb["c_ones_col"]
        psA, psB = ctx.PS[0], ctx.PS[1]
        def mk_sum(e):
            ins = []
            for pt in range(NPT):
                ins.append(e.matmul(psA[0:1, 0:SLAB], onec[:, :],
                                    ctx.XNB[:, pt * SLAB:(pt + 1) * SLAB],
                                    start=(pt == 0), stop=(pt == NPT - 1)))
            return ins
        p.op("tensor", mk_sum, reads=[rr("XNB"), rr("c_ones_col")], writes=[rr("PSB0", 0, 1)])
        p.op("vector", lambda e: e.tensor_tensor(out=ctx.XNB[:, :],
                                                 in0=ctx.XNB[:, :], in1=ctx.XNB[:, :], op=ALU.mult),
             reads=[rr("XNB")], writes=[rr("XNB")])
        def mk_sumsq(e):
            ins = []
            for pt in range(NPT):
                ins.append(e.matmul(psB[0:1, 0:SLAB], onec[:, :],
                                    ctx.XNB[:, pt * SLAB:(pt + 1) * SLAB],
                                    start=(pt == 0), stop=(pt == NPT - 1)))
            return ins
        p.op("tensor", mk_sumsq, reads=[rr("XNB"), rr("c_ones_col")], writes=[rr("PSB1", 0, 1)])
        mrow = mrow_ap
        rrow = rrow_ap
        vrow = ctx.ROWS[0:1, 0:SLAB]
        n2row = ctx.ROWS[0:1, SLAB:2 * SLAB]
        p.op("vector", lambda e: e.tensor_scalar(out=mrow, in0=psA[0:1, 0:SLAB],
                                                 scalar1=1.0 / D, scalar2=None, op0=ALU.mult),
             reads=[rr("PSB0", 0, 1)], writes=[mkey])
        p.op("vector", lambda e: e.tensor_scalar(out=vrow, in0=psB[0:1, 0:SLAB],
                                                 scalar1=1.0 / D, scalar2=None, op0=ALU.mult),
             reads=[rr("PSB1", 0, 1)], writes=[rr("ROWS", 0, SLAB)])
        p.op("vector", lambda e: e.scalar_tensor_tensor(out=n2row, in0=mrow, scalar=-1.0,
                                                        in1=mrow, op0=ALU.mult, op1=ALU.mult),
             reads=[mkey], writes=[rr("ROWS", SLAB, 2 * SLAB)])
        p.op("vector", lambda e: e.tensor_tensor(out=vrow, in0=vrow, in1=n2row, op=ALU.add),
             reads=[rr("ROWS", 0, SLAB), rr("ROWS", SLAB, 2 * SLAB)], writes=[rr("ROWS", 0, SLAB)])
        p.op("scalar", lambda e: e.activation(out=vrow, in_=vrow, func=AF.Ln,
                                              bias=ctx.COLS[0:1, 0:1]),
             reads=[rr("ROWS", 0, SLAB), rr("EPSC")], writes=[rr("ROWS", 0, SLAB)])
        p.op("scalar", lambda e: e.activation(out=rrow, in_=vrow, func=AF.Exp, scale=-0.5),
             reads=[rr("ROWS", 0, SLAB)], writes=[rkey])
    onesr = ctx.sb["c_ones_row"]
    p.op("tensor", lambda e: e.matmul(ctx.PS[0][:, :], onesr[:, :], mrow_ap, start=True, stop=True),
         reads=[mkey, rr("c_ones_row")], writes=[rr("PSB0")])
    p.op("tensor", lambda e: e.matmul(ctx.PS[1][:, :], onesr[:, :], rrow_ap, start=True, stop=True),
         reads=[rkey, rr("c_ones_row")], writes=[rr("PSB1")])
    for pt in range(NPT):
        sl = slice(pt * SLAB, (pt + 1) * SLAB)
        p.op("vector", lambda e, sl=sl: e.tensor_tensor(out=ctx.TMP[:, :], in0=ctx.XS[:, sl],
                                                        in1=ctx.PS[0][:, :], op=ALU.subtract),
             reads=[rr("XS"), rr("PSB0")], writes=[rr("TMPF")])
        p.op("vector", lambda e, sl=sl: e.tensor_tensor(out=ctx.XNB[:, sl], in0=ctx.TMP[:, :],
                                                        in1=ctx.PS[1][:, :], op=ALU.mult),
             reads=[rr("TMPF"), rr("PSB1")], writes=[rr("XNB")])
        if gb is not None:
            gbt, gbkey = gb
            p.op("vector", lambda e, sl=sl, pt=pt, gbt=gbt: e.tensor_scalar(
                out=ctx.XNB[:, sl], in0=ctx.XNB[:, sl], scalar1=gbt[:, pt:pt + 1],
                scalar2=gbt[:, 8 + pt:8 + pt + 1], op0=ALU.mult, op1=ALU.add),
                 reads=[rr("XNB"), gbkey], writes=[rr("XNB")])


def store_x_slab(ctx, x_dram, xkey, s):
    p = ctx.p
    xa = x_dram.ap()[:, s * SLAB:(s + 1) * SLAB].rearrange("(pt p) t -> p pt t", p=128)
    xs3 = ctx.XS[:, :].rearrange("p (pt t) -> p pt t", t=SLAB)
    p.dma("qscal", xa, xs3, reads=[rr("XS")], writes=[rr(xkey, s, s + 1)])



def emit_me_build(ctx, win_dram, wotT_dram, me_dram):
    p = ctx.p
    H = 4096
    WI = p.sbuf("WI", [128, (H // 128) * 1024], BF16)
    WO = p.sbuf("WO", [128, (H // 128) * 1024], BF16)
    MEV = p.sbuf("MEV", [128, 8 * 1024], BF16)
    wia = win_dram.ap().rearrange("e (ht p) k -> e p ht k", p=128)
    woa = wotT_dram.ap().rearrange("e (ht p) m -> e p ht m", p=128)
    wi3 = WI[:, :].rearrange("p (ht k) -> p ht k", k=1024)
    wo3 = WO[:, :].rearrange("p (ht m) -> p ht m", m=1024)
    PSM = [p.psum("PSM%d" % i, [128, 512], F32) for i in range(4)]
    psm_i = [0]
    for e in range(8):
        for q in range(4):
            sl = slice(q * 8, (q + 1) * 8)
            p.dma("qgps", wi3[:, sl, :], wia[e, :, sl, :], writes=[rr("WI", q, q + 1)], tag="wi%d" % e)
            p.dma("qgps", wo3[:, sl, :], woa[e, :, sl, :], writes=[rr("WO", q, q + 1)], tag="wo%d" % e)
        for kt in range(8):
            for mh in range(2):
                i = psm_i[0] % 4
                psm_i[0] += 1
                ps, pskey = PSM[i], "PSM%d" % i
                def mk(e2, kt=kt, mh=mh, ps=ps):
                    ins = []
                    for ht in range(H // 128):
                        ins.append(e2.matmul(
                            ps[:, :],
                            WI[:, ht * 1024 + kt * 128: ht * 1024 + (kt + 1) * 128],
                            WO[:, ht * 1024 + mh * 512: ht * 1024 + mh * 512 + 512],
                            start=(ht == 0), stop=(ht == H // 128 - 1)))
                    return ins
                p.op("tensor", mk, reads=[rr("WI"), rr("WO")], writes=[rr(pskey)])
                dst = MEV[:, kt * 1024 + mh * 512: kt * 1024 + mh * 512 + 512]
                p.op("scalar", lambda e2, ps=ps, dst=dst: e2.copy(out=dst, in_=ps[:, :]),
                     reads=[rr(pskey)], writes=[rr("MEV", kt * 2 + mh, kt * 2 + mh + 1)])
        p.dma("qscal", me_dram.ap()[e * 128:(e + 1) * 128, :], MEV[:, :],
              reads=[rr("MEV")], writes=[rr("me", e, e + 1)])



def emit_l0ffn(ctx, x_in, x_out, kin, kout, me_dram, uid, wg_sb):
    p = ctx.p
    GTS = p.sbuf("GTS" + uid, [128, 8], F32)
    ED = p.sbuf("ED" + uid, [128, 8], F32)
    LM = p.sbuf("LM" + uid, [128, 8], F32)
    IND = p.sbuf("IND" + uid, [128, 8], F32)
    NM1 = p.sbuf("NM1" + uid, [128, 1], F32)
    NM2 = p.sbuf("NM2" + uid, [128, 1], F32)
    RC = p.sbuf("RC" + uid, [128, 1], F32)
    E2 = p.sbuf("E2" + uid, [128, 1], F32)
    GFM = p.sbuf("GFM" + uid, [8, SLAB], BF16)
    oh8 = ctx.use_const("c_oh8")
    GB = p.sbuf("GB" + uid, [128, 8 * SLAB], BF16)
    ME = p.sbuf("ME" + uid, [128, 8 * 8 * 1024], BF16)
    XGK = p.sbuf("XGK" + uid, [128, 2 * 8 * SLAB], BF16)
    PSE = [p.psum("PSE%s%d" % (uid, i), [128, 512], F32) for i in range(6)]
    ACC = PSE + [ctx.PS[0], ctx.PS[1]]
    ACCK = ["PSE%s%d" % (uid, i) for i in range(6)] + ["PSB0", "PSB1"]
    mea = me_dram.ap().rearrange("(e p) c -> p e c", p=128)
    me3 = ME[:, :].rearrange("p (e c) -> p e c", c=8192)
    p.dma("qsync", me3, mea, reads=[rr("me", 0, 8)], writes=[rr("MEsb" + uid)])

    for s in range(NSLAB):
        load_x_slab(ctx, x_in, kin, s, stats=True)
        psL = PSE[0]
        psT = PSE[1]
        for tt in range(4):
            def mk_log(e, s=s, tt=tt, psL=psL):
                ins = []
                for pt in range(NPT):
                    ins.append(e.matmul(
                        psL[0:128, 0:8],
                        ctx.XNB[:, pt * SLAB + tt * 128: pt * SLAB + tt * 128 + 128],
                        wg_sb[:, pt * 8:(pt + 1) * 8],
                        start=(pt == 0), stop=(pt == NPT - 1)))
                return ins
            p.op("tensor", mk_log, reads=[rr("XNB"), rr("WGsb")], writes=[rr("PSE" + uid + "0")])
            p.op("vector", lambda e, psL=psL: e.tensor_reduce(out=NM1[:, :], in_=psL[0:128, 0:8],
                                                              axis=AX.X, op=ALU.max, negate=True),
                 reads=[rr("PSE" + uid + "0")], writes=[rr("NM1" + uid)])
            p.op("scalar", lambda e, psL=psL: e.activation(out=ED[:, :], in_=psL[0:128, 0:8],
                                                           func=AF.Exp, bias=NM1[:, :]),
                 reads=[rr("PSE" + uid + "0"), rr("NM1" + uid)], writes=[rr("ED" + uid)])
            p.op("vector", lambda e, psL=psL: e.tensor_scalar(out=IND[:, :], in0=psL[0:128, 0:8],
                                                              scalar1=NM1[:, :], scalar2=0.0,
                                                              op0=ALU.add, op1=ALU.is_ge),
                 reads=[rr("PSE" + uid + "0"), rr("NM1" + uid)], writes=[rr("IND" + uid)])
            p.op("vector", lambda e, psL=psL: e.scalar_tensor_tensor(out=LM[:, :], in0=IND[:, :],
                                                                     scalar=-1e30, in1=psL[0:128, 0:8],
                                                                     op0=ALU.mult, op1=ALU.add),
                 reads=[rr("IND" + uid), rr("PSE" + uid + "0")], writes=[rr("LM" + uid)])
            p.op("vector", lambda e: e.tensor_reduce(out=NM2[:, :], in_=LM[:, :],
                                                     axis=AX.X, op=ALU.max, negate=True),
                 reads=[rr("LM" + uid)], writes=[rr("NM2" + uid)])
            p.op("scalar", lambda e: e.activation(out=E2[:, :], in_=NM2[:, :],
                                                  func=AF.Exp, scale=-1.0, bias=NM1[:, :]),
                 reads=[rr("NM2" + uid), rr("NM1" + uid)], writes=[rr("E2" + uid)])
            p.op("vector", lambda e: e.tensor_scalar(out=E2[:, :], in0=E2[:, :],
                                                     scalar1=1.0, scalar2=None, op0=ALU.add),
                 reads=[rr("E2" + uid)], writes=[rr("E2" + uid)])
            p.op("vector", lambda e: e.reciprocal(out=RC[:, :], in_=E2[:, :]),
                 reads=[rr("E2" + uid)], writes=[rr("RC" + uid)])
            p.op("vector", lambda e, psL=psL: e.tensor_scalar(out=IND[:, :], in0=psL[0:128, 0:8],
                                                              scalar1=NM2[:, :], scalar2=0.0,
                                                              op0=ALU.add, op1=ALU.is_ge),
                 reads=[rr("PSE" + uid + "0"), rr("NM2" + uid)], writes=[rr("IND" + uid)])
            p.op("vector", lambda e: e.scalar_tensor_tensor(out=GTS[:, :], in0=ED[:, :],
                                                            scalar=RC[:, :], in1=IND[:, :],
                                                            op0=ALU.mult, op1=ALU.mult),
                 reads=[rr("ED" + uid), rr("RC" + uid), rr("IND" + uid)], writes=[rr("GTS" + uid)])
            p.op("tensor", lambda e, psT=psT: e.transpose(psT[0:8, 0:128], GTS[:, :],
                                                          ctx.sb["c_id_f32"][:, :]),
                 reads=[rr("GTS" + uid), rr("c_id_f32")], writes=[rr("PSE" + uid + "1")])
            dst = GFM[0:8, tt * 128:(tt + 1) * 128]
            p.op("vector", lambda e, psT=psT, dst=dst: e.tensor_copy(out=dst, in_=psT[0:8, 0:128]),
                 reads=[rr("PSE" + uid + "1")], writes=[rr("GFM" + uid, tt, tt + 1)])
        for e8 in range(8):
            p.op("tensor", lambda e, e8=e8: e.matmul(
                psL[:, :], oh8[:, e8 * 128:(e8 + 1) * 128],
                GFM[0:8, :], start=True, stop=True),
                 reads=[rr("GFM" + uid, 0, 4), rr("c_oh8")], writes=[rr("PSE" + uid + "0")])
            p.op("scalar", lambda e, e8=e8: e.copy(out=GB[:, e8 * SLAB:(e8 + 1) * SLAB],
                                                   in_=psL[:, :]),
                 reads=[rr("PSE" + uid + "0")], writes=[rr("GBsb" + uid, e8, e8 + 1)])
        for kt in range(NPT):
            for e8 in range(8):
                slot = (kt % 2) * 8 + e8
                xg = XGK[:, slot * SLAB:(slot + 1) * SLAB]
                p.op("vector", lambda e, kt=kt, e8=e8, xg=xg: e.tensor_tensor(
                    out=xg, in0=ctx.XNB[:, kt * SLAB:(kt + 1) * SLAB],
                    in1=GB[:, e8 * SLAB:(e8 + 1) * SLAB], op=ALU.mult),
                     reads=[rr("XNB"), rr("GBsb" + uid, e8, e8 + 1)],
                     writes=[rr("XGK" + uid, slot, slot + 1)])
                first = (kt == 0 and e8 == 0)
                last = (kt == NPT - 1 and e8 == 7)
                def mk_u(e, kt=kt, e8=e8, xg=xg, first=first, last=last):
                    ins = []
                    for pt in range(NPT):
                        ins.append(e.matmul(
                            ACC[pt][:, :],
                            ME[:, (e8 * 8 + kt) * 1024 + pt * 128:(e8 * 8 + kt) * 1024 + pt * 128 + 128],
                            xg, start=first, stop=last))
                    return ins
                p.op("tensor", mk_u,
                     reads=[rr("MEsb" + uid), rr("XGK" + uid, slot, slot + 1)],
                     writes=[rr(ACCK[pt]) for pt in range(NPT)])
        for pt in range(NPT):
            p.op("vector", lambda e, pt=pt: e.tensor_tensor(
                out=ctx.XS[:, pt * SLAB:(pt + 1) * SLAB],
                in0=ctx.XS[:, pt * SLAB:(pt + 1) * SLAB], in1=ACC[pt][:, :], op=ALU.add),
                 reads=[rr(ACCK[pt]), rr("XS")], writes=[rr("XS")])
        store_x_slab(ctx, x_out, kout, s)



def emit_l1ffn(ctx, x_in, x_out, kin, kout, eselT_dram, keysc_dram, valsc_dram, uid):
    p = ctx.p
    oh64 = ctx.use_const("c_oh64")
    KQ = p.sbuf("KQ" + uid, [128, 8 * 2048], BF16)
    VQ = p.sbuf("VQ" + uid, [128, 16 * 1024], BF16)
    SCOR = p.sbuf("SCOR" + uid, [128, 16 * SLAB], BF16)
    OUTA = p.sbuf("OUTA" + uid, [128, 8 * 1024], F32)
    GATES = p.sbuf("GATES" + uid, [64, 1024], BF16)
    ESEL = p.sbuf("ESEL" + uid, [128, 8 * 64], F32)
    RSC = p.sbuf("RSC" + uid, [128, 2 * SLAB], BF16)
    LBUF = p.sbuf("LBUF" + uid, [128, 64], F32)
    LOG0 = p.sbuf("LOG0" + uid, [128, 64], F32)
    CSBC = p.sbuf("CSBC" + uid, [128, 64], F32)
    MRCOL = p.sbuf("MRCOL" + uid, [128, 2], F32)
    SIG = p.sbuf("SIG" + uid, [128, 64], F32)
    IND4 = p.sbuf("IND4" + uid, [128, 64], F32)
    GTF = p.sbuf("GTF" + uid, [128, 64], F32)
    NM = p.sbuf("NM" + uid, [128, 4], F32)
    M2 = p.sbuf("M2" + uid, [1, 1024], F32)
    R2 = p.sbuf("R2" + uid, [1, 1024], F32)
    PH = p.psum("PH" + uid, [128, 512], F32)
    PG = p.psum("PG" + uid, [128, 512], F32)
    PT = p.psum("PT" + uid, [128, 512], F32)
    PO = p.psum("PO" + uid, [128, 512], F32)
    kPH, kPG, kPT, kPO = ("PH" + uid), ("PG" + uid), ("PT" + uid), ("PO" + uid)
    p.dma("qsync", ESEL[:, :].rearrange("p (pt e) -> p pt e", e=64),
          eselT_dram.ap().rearrange("(pt p) e -> p pt e", p=128), writes=[rr("ESEL" + uid)])
    def mk_cs(e):
        ins = []
        for pt in range(NPT):
            ins.append(e.matmul(PT[0:1, 0:64], ctx.sb["c_ones_colf"][:, :],
                                ESEL[:, pt * 64:(pt + 1) * 64],
                                start=(pt == 0), stop=(pt == NPT - 1)))
        return ins
    p.op("tensor", mk_cs, reads=[rr("ESEL" + uid), rr("c_ones_colf")], writes=[rr(kPT)])
    p.op("vector", lambda e: e.tensor_copy(out=ctx.ROWS[0:1, 0:64], in_=PT[0:1, 0:64]),
         reads=[rr(kPT)], writes=[rr("ROWS", 0, 64)])
    p.op("tensor", lambda e: e.matmul(PG[:, 0:64], ctx.sb["c_ones_row"][:, :],
                                      ctx.ROWS[0:1, 0:64], start=True, stop=True),
         reads=[rr("ROWS", 0, 64), rr("c_ones_row")], writes=[rr(kPG)])
    p.op("vector", lambda e: e.tensor_copy(out=CSBC[:, :], in_=PG[:, 0:64]),
         reads=[rr(kPG)], writes=[rr("CSBC" + uid)])
    kq3 = KQ[:, :].rearrange("p (kt c) -> p kt c", c=2048)
    vq3 = VQ[:, :].rearrange("p (kt m) -> p kt m", m=1024)
    keysa = keysc_dram.ap().rearrange("(kt p) c -> p kt c", p=128)
    valsa = valsc_dram.ap().rearrange("(kt p) m -> p kt m", p=128)

    for tsup in range(4):
        for keq in range(4):
            kvq = "qsync" if KVBF else "qgps"
            p.dma(kvq, kq3, keysa[:, :, keq * 2048:(keq + 1) * 2048],
                  writes=[rr("KQ" + uid)], tag="kq")
            p.dma(kvq, vq3, valsa[:, keq * 16:(keq + 1) * 16, :],
                  writes=[rr("VQ" + uid)], tag="vq")
            for tql in range(2):
                s = tsup * 2 + tql
                rows = (M2[0:1, tql * SLAB:(tql + 1) * SLAB],
                        R2[0:1, tql * SLAB:(tql + 1) * SLAB],
                        rr("M2" + uid, tql, tql + 1), rr("R2" + uid, tql, tql + 1))
                load_x_slab(ctx, x_in, kin, s, stats=(keq == 0), rowbufs=rows)
                if keq == 0:
                    for tt in range(4):
                        def mk_log(e, tt=tt):
                            ins = []
                            for pt in range(NPT):
                                ins.append(e.matmul(
                                    PH[0:128, 0:64],
                                    ctx.XS[:, pt * SLAB + tt * 128: pt * SLAB + tt * 128 + 128],
                                    ESEL[:, pt * 64:(pt + 1) * 64],
                                    start=(pt == 0), stop=(pt == NPT - 1)))
                            return ins
                        p.op("tensor", mk_log, reads=[rr("XS"), rr("ESEL" + uid)], writes=[rr(kPH)])
                        mr = rows[0][0:1, tt * 128:(tt + 1) * 128]
                        rs = rows[1][0:1, tt * 128:(tt + 1) * 128]
                        p.op("tensor", lambda e, mr=mr: e.transpose(PT[0:128, 0:1], mr,
                                                                    ctx.sb["c_id_f32"][0:1, 0:1]),
                             reads=[rows[2], rr("c_id_f32")], writes=[rr(kPT)])
                        p.op("vector", lambda e: e.tensor_scalar(out=MRCOL[:, 0:1], in0=PT[0:128, 0:1],
                                                                 scalar1=-1.0, scalar2=None, op0=ALU.mult),
                             reads=[rr(kPT)], writes=[rr("MRC" + uid, 0, 1)])
                        p.op("tensor", lambda e, rs=rs: e.transpose(PT[0:128, 0:1], rs,
                                                                    ctx.sb["c_id_f32"][0:1, 0:1]),
                             reads=[rows[3], rr("c_id_f32")], writes=[rr(kPT)])
                        p.op("vector", lambda e: e.tensor_copy(out=MRCOL[:, 1:2], in_=PT[0:128, 0:1]),
                             reads=[rr(kPT)], writes=[rr("MRC" + uid, 1, 2)])
                        p.op("vector", lambda e: e.scalar_tensor_tensor(
                            out=LOG0[:, :], in0=CSBC[:, :], scalar=MRCOL[:, 0:1],
                            in1=PH[0:128, 0:64], op0=ALU.mult, op1=ALU.add),
                             reads=[rr("CSBC" + uid), rr("MRC" + uid, 0, 1), rr(kPH)],
                             writes=[rr("LOG0" + uid)])
                        p.op("vector", lambda e: e.tensor_copy(out=LBUF[:, :], in_=LOG0[:, :]),
                             reads=[rr("LOG0" + uid)], writes=[rr("LBUF" + uid)])
                        for it in range(4):
                            nm = NM[:, it:it + 1]
                            p.op("vector", lambda e, nm=nm: e.tensor_reduce(
                                out=nm, in_=LBUF[:, :], axis=AX.X, op=ALU.max, negate=True),
                                 reads=[rr("LBUF" + uid)], writes=[rr("NM" + uid, it, it + 1)])
                            if it < 3:
                                p.op("vector", lambda e, nm=nm: e.tensor_scalar(
                                    out=IND4[:, :], in0=LBUF[:, :], scalar1=nm, scalar2=0.0,
                                    op0=ALU.add, op1=ALU.is_ge),
                                     reads=[rr("LBUF" + uid), rr("NM" + uid, it, it + 1)],
                                     writes=[rr("IND4" + uid)])
                                p.op("vector", lambda e: e.scalar_tensor_tensor(
                                    out=LBUF[:, :], in0=IND4[:, :], scalar=-1e30, in1=LBUF[:, :],
                                    op0=ALU.mult, op1=ALU.add),
                                     reads=[rr("IND4" + uid), rr("LBUF" + uid)],
                                     writes=[rr("LBUF" + uid)])
                        p.op("scalar", lambda e: e.activation(out=SIG[:, :], in_=LOG0[:, :],
                                                              func=AF.Sigmoid, scale=MRCOL[:, 1:2]),
                             reads=[rr("LOG0" + uid), rr("MRC" + uid, 1, 2)], writes=[rr("SIG" + uid)])
                        p.op("vector", lambda e: e.tensor_scalar(
                            out=IND4[:, :], in0=LOG0[:, :], scalar1=NM[:, 3:4], scalar2=0.0,
                            op0=ALU.add, op1=ALU.is_ge),
                             reads=[rr("LOG0" + uid), rr("NM" + uid, 3, 4)], writes=[rr("IND4" + uid)])
                        p.op("vector", lambda e: e.tensor_tensor(out=GTF[:, :], in0=SIG[:, :],
                                                                 in1=IND4[:, :], op=ALU.mult),
                             reads=[rr("SIG" + uid), rr("IND4" + uid)], writes=[rr("GTF" + uid)])
                        p.op("tensor", lambda e: e.transpose(PT[0:64, 0:128], GTF[:, :],
                                                             ctx.sb["c_id_f32"][:, :]),
                             reads=[rr("GTF" + uid), rr("c_id_f32")], writes=[rr(kPT)])
                        dst = GATES[0:64, tql * SLAB + tt * 128: tql * SLAB + tt * 128 + 128]
                        p.op("vector", lambda e, dst=dst: e.tensor_copy(out=dst, in_=PT[0:64, 0:128]),
                             reads=[rr(kPT)], writes=[rr("GATES" + uid, tql * 4 + tt, tql * 4 + tt + 1)])
                for kb in range(16):
                    kbg = keq * 16 + kb
                    PHb, kPHb = (PH, kPH) if kb % 2 == 0 else (PT, kPT)
                    rsc = RSC[:, (kb % 2) * SLAB:(kb % 2 + 1) * SLAB]
                    def mk_h(e, kb=kb, PHb=PHb):
                        ins = []
                        for kt in range(NPT):
                            ins.append(e.matmul(
                                PHb[:, :], KQ[:, kt * 2048 + kb * 128: kt * 2048 + kb * 128 + 128],
                                ctx.XNB[:, kt * SLAB:(kt + 1) * SLAB],
                                start=(kt == 0), stop=(kt == NPT - 1)))
                        return ins
                    p.op("tensor", mk_h, reads=[rr("KQ" + uid), rr("XNB")], writes=[rr(kPHb)])
                    p.op("tensor", lambda e, kbg=kbg, tql=tql: e.matmul(
                        PG[:, :], oh64[:, kbg * 128:(kbg + 1) * 128],
                        GATES[0:64, tql * SLAB:(tql + 1) * SLAB], start=True, stop=True),
                         reads=[rr("GATES" + uid, tql * 4, tql * 4 + 4), rr("c_oh64")],
                         writes=[rr(kPG)])
                    p.op("scalar", lambda e, PHb=PHb, rsc=rsc: e.activation(
                        out=rsc, in_=PHb[:, :], func=AF.Relu),
                         reads=[rr(kPHb)], writes=[rr("RSC" + uid, kb % 2, kb % 2 + 1)])
                    p.op("vector", lambda e, kb=kb, rsc=rsc: e.tensor_tensor(
                        out=SCOR[:, kb * SLAB:(kb + 1) * SLAB], in0=rsc, in1=PG[:, :],
                        op=ALU.mult),
                         reads=[rr("RSC" + uid, kb % 2, kb % 2 + 1), rr(kPG)],
                         writes=[rr("SCOR" + uid, kb, kb + 1)])
                for pt in range(NPT):
                    POb, kPOb = (PO, kPO) if pt % 2 == 0 else (PH, kPH)
                    def mk_o(e, pt=pt, POb=POb):
                        ins = []
                        for kt in range(16):
                            ins.append(e.matmul(
                                POb[:, :], VQ[:, kt * 1024 + pt * 128: kt * 1024 + pt * 128 + 128],
                                SCOR[:, kt * SLAB:(kt + 1) * SLAB],
                                start=(kt == 0), stop=(kt == 15)))
                        return ins
                    p.op("tensor", mk_o,
                         reads=[rr("VQ" + uid)] + [rr("SCOR" + uid, kt, kt + 1) for kt in range(16)],
                         writes=[rr(kPOb)])
                    oa = OUTA[:, pt * 1024 + tql * SLAB: pt * 1024 + (tql + 1) * SLAB]
                    okey = rr("OUTA" + uid, pt * 2 + tql, pt * 2 + tql + 1)
                    if keq == 0:
                        p.op("vector", lambda e, oa=oa, POb=POb: e.tensor_copy(out=oa, in_=POb[:, :]),
                             reads=[rr(kPOb)], writes=[okey])
                    else:
                        p.op("vector", lambda e, oa=oa, POb=POb: e.tensor_tensor(
                            out=oa, in0=oa, in1=POb[:, :], op=ALU.add),
                             reads=[rr(kPOb), okey], writes=[okey])
                    if keq == 3:
                        p.op("vector", lambda e, oa=oa, pt=pt: e.tensor_tensor(
                            out=ctx.XS[:, pt * SLAB:(pt + 1) * SLAB],
                            in0=ctx.XS[:, pt * SLAB:(pt + 1) * SLAB], in1=oa, op=ALU.add),
                             reads=[okey, rr("XS")], writes=[rr("XS")])
                if keq == 3:
                    store_x_slab(ctx, x_out, kout, s)



def emit_l1attn_a(ctx, x_in, x_out, kin, kout, w):
    p = ctx.p
    tgm = ctx.use_const("c_tgmask")
    tgs = ctx.use_const("c_tgsel")
    QKVW = p.sbuf("QKVW", [128, 8 * 3072], BF16)
    P1W = p.sbuf("P1W", [128, 8 * 1024], BF16)
    P2W = p.sbuf("P2W", [128, 8 * 1024], BF16)
    QB = p.sbuf("QBc", [128, 24], F32)
    P1B = p.sbuf("P1Bc", [128, 8], F32)
    P2B = p.sbuf("P2Bc", [128, 8], F32)
    KVBB = p.sbuf("KVBB", [128, 2048], BF16)
    GBKV = p.sbuf("GBKV", [128, 4 * 1024], BF16)
    TGRID = p.sbuf("TGRID", [128, 512], F32)
    BGRID = p.sbuf("BGRID", [128, 512], F32)
    KVSB = p.sbuf("KVSB", [128, 512], F32)
    KV2 = p.sbuf("KV2", [128, 512], BF16)
    KT = p.sbuf("KT", [128, 1024], F32)
    KNT = p.sbuf("KNT", [128, 512], BF16)
    VNT = p.sbuf("VNT", [128, 512], BF16)
    TMPB = p.sbuf("TMPB", [128, 512], F32)
    ST16 = p.sbuf("ST16", [128, 32], F32)
    QSL = p.sbuf("QSL", [128, NPT * SLAB], BF16)
    AXN = p.sbuf("AXN", [128, NPT * SLAB], BF16)
    H1 = p.sbuf("H1", [128, NPT * SLAB], BF16)
    MT = p.sbuf("MT", [1, T], F32)
    RT = p.sbuf("RT", [1, T], F32)
    PK = p.psum("PK5", [128, 512], F32)
    PKV = p.psum("PKV5", [128, 512], F32)
    PQ = p.psum("PQ5", [128, 512], F32)
    PM = p.psum("PM5", [128, 512], F32)
    kPK, kPKV, kPQ, kPM = "PK5", "PKV5", "PQ5", "PM5"

    p.dma("qgps", QKVW[:, :].rearrange("p (kt c) -> p kt c", c=3072),
          w["qkvwT"].ap().rearrange("(kt p) c -> p kt c", p=128), writes=[rr("QKVW")])
    p.dma("qgps", P1W[:, :].rearrange("p (kt c) -> p kt c", c=1024),
          w["p1wT"].ap().rearrange("(kt p) c -> p kt c", p=128), writes=[rr("P1W")])
    p.dma("qgps", P2W[:, :].rearrange("p (kt c) -> p kt c", c=1024),
          w["p2wT"].ap().rearrange("(kt p) c -> p kt c", p=128), writes=[rr("P2W")])
    p.dma("qsync", QB[:, :], w["qkvb_col"].ap(), writes=[rr("QBc")])
    p.dma("qsync", P1B[:, :], w["p1b_col"].ap(), writes=[rr("P1Bc")])
    p.dma("qsync", P2B[:, :], w["p2b_col"].ap(), writes=[rr("P2Bc")])
    p.dma("qsync", BGRID[:, :], w["biasgrid"].ap(), writes=[rr("BGRID")])
    p.dma("qsync", ctx.COLS[0:16, 8:9], w["tempcol"].ap(), writes=[rr("TEMPC")])
    p.dma("qsync", ctx.ROWS[0:1, 0:1024], w["qkvb_row"].ap()[0:1, 1024:2048],
          writes=[rr("ROWS", 0, 1024)])
    p.dma("qsync", ctx.ROWS[0:1, 1024:2048], w["qkvb_row"].ap()[0:1, 2048:3072],
          writes=[rr("ROWS", 1024, 2048)])
    for c2 in range(4):
        p.op("tensor", lambda e, c2=c2: e.matmul(PM[:, :], ctx.sb["c_ones_row"][:, :],
                                                 ctx.ROWS[0:1, c2 * 512:(c2 + 1) * 512],
                                                 start=True, stop=True),
             reads=[rr("ROWS", c2 * 512, (c2 + 1) * 512), rr("c_ones_row")], writes=[rr(kPM)])
        p.op("vector", lambda e, c2=c2: e.tensor_copy(out=KVBB[:, c2 * 512:(c2 + 1) * 512],
                                                      in_=PM[:, :]),
             reads=[rr(kPM)], writes=[rr("KVBB", c2, c2 + 1)])
    for i, nm in enumerate(("kng_row", "knb_row", "vng_row", "vnb_row")):
        p.dma("qsync", ctx.ROWS[0:1, 0:1024], w[nm].ap(), writes=[rr("ROWS", 0, 1024)])
        for c2 in range(2):
            p.op("tensor", lambda e, c2=c2: e.matmul(PM[:, :], ctx.sb["c_ones_row"][:, :],
                                                     ctx.ROWS[0:1, c2 * 512:(c2 + 1) * 512],
                                                     start=True, stop=True),
                 reads=[rr("ROWS", c2 * 512, (c2 + 1) * 512), rr("c_ones_row")],
                 writes=[rr(kPM)])
            p.op("vector", lambda e, i=i, c2=c2: e.tensor_copy(
                out=GBKV[:, i * 1024 + c2 * 512: i * 1024 + (c2 + 1) * 512], in_=PM[:, :]),
                 reads=[rr(kPM)], writes=[rr("GBKV", i * 2 + c2, i * 2 + c2 + 1)])
    p.op("vector", lambda e: e.tensor_scalar(out=ctx.TMP[0:16, 0:512], in0=tgm[:, :],
                                             scalar1=ctx.COLS[0:16, 8:9], scalar2=1.0 / T,
                                             op0=ALU.mult, op1=ALU.mult),
         reads=[rr("c_tgmask"), rr("TEMPC")], writes=[rr("TMP")])
    p.op("tensor", lambda e: e.matmul(PM[:, :], tgs[:, :], ctx.TMP[0:16, 0:512],
                                      start=True, stop=True),
         reads=[rr("TMP"), rr("c_tgsel")], writes=[rr(kPM)])
    p.op("vector", lambda e: e.tensor_copy(out=TGRID[:, :], in_=PM[:, :]),
         reads=[rr(kPM)], writes=[rr("TGRID")])
    p.op("vector", lambda e: e.memset(KVSB[:, :], 0.0), writes=[rr("KVSB")])

    for s in range(NSLAB):
        rows = (MT[0:1, s * SLAB:(s + 1) * SLAB], RT[0:1, s * SLAB:(s + 1) * SLAB],
                rr("MT", s, s + 1), rr("RT", s, s + 1))
        load_x_slab(ctx, x_in, kin, s, stats=True, rowbufs=rows)
        for tt in range(4):
            for ch in range(2):
                for which, dstT, dk in ((0, KNT, "KNT"), (1, VNT, "VNT")):
                    col0 = 1024 + which * 1024 + ch * 512
                    PKb, kPKb = (PK, kPK) if which == 0 else (PM, kPM)
                    def mk_kv(e, tt=tt, col0=col0, PKb=PKb):
                        ins = []
                        for kt in range(NPT):
                            ins.append(e.matmul(
                                PKb[:, :],
                                ctx.XNB[:, kt * SLAB + tt * 128: kt * SLAB + tt * 128 + 128],
                                QKVW[:, kt * 3072 + col0: kt * 3072 + col0 + 512],
                                start=(kt == 0), stop=(kt == NPT - 1)))
                        return ins
                    p.op("tensor", mk_kv, reads=[rr("XNB"), rr("QKVW")], writes=[rr(kPKb)])
                    kk = rr("KT", which, which + 1)
                    dst = KT[:, which * 512:(which + 1) * 512]
                    p.op("vector", lambda e, dst=dst, which=which, ch=ch, PKb=PKb: e.tensor_tensor(
                        out=dst, in0=PKb[:, :],
                        in1=KVBB[:, which * 1024 + ch * 512: which * 1024 + ch * 512 + 512],
                        op=ALU.add),
                         reads=[rr(kPKb), rr("KVBB", which * 2 + ch, which * 2 + ch + 1)],
                         writes=[kk])
                    src3 = dst.rearrange("p (h d) -> p h d", d=64)
                    p.op("vector", lambda e, src3=src3: e.tensor_reduce(
                        out=ST16[:, 0:8], in_=src3, axis=AX.X, op=ALU.add),
                         reads=[kk], writes=[rr("ST16", 0, 8)])
                    p.op("vector", lambda e: e.tensor_scalar(
                        out=ST16[:, 0:8], in0=ST16[:, 0:8], scalar1=1.0 / 64, scalar2=None,
                        op0=ALU.mult),
                         reads=[rr("ST16", 0, 8)], writes=[rr("ST16", 0, 8)])
                    p.op("vector", lambda e, dst=dst: e.tensor_tensor(
                        out=ctx.TMP[:, :], in0=dst, in1=dst, op=ALU.mult),
                         reads=[kk], writes=[rr("TMP")])
                    p.op("vector", lambda e: e.tensor_reduce(
                        out=ST16[:, 8:16], in_=ctx.TMP[:, :].rearrange("p (h d) -> p h d", d=64),
                        axis=AX.X, op=ALU.add),
                         reads=[rr("TMP")], writes=[rr("ST16", 8, 16)])
                    p.op("vector", lambda e: e.tensor_scalar(
                        out=ST16[:, 8:16], in0=ST16[:, 8:16], scalar1=1.0 / 64, scalar2=None,
                        op0=ALU.mult),
                         reads=[rr("ST16", 8, 16)], writes=[rr("ST16", 8, 16)])
                    p.op("vector", lambda e: e.scalar_tensor_tensor(
                        out=ST16[:, 16:24], in0=ST16[:, 0:8], scalar=-1.0, in1=ST16[:, 0:8],
                        op0=ALU.mult, op1=ALU.mult),
                         reads=[rr("ST16", 0, 8)], writes=[rr("ST16", 16, 24)])
                    p.op("vector", lambda e: e.tensor_tensor(
                        out=ST16[:, 16:24], in0=ST16[:, 16:24], in1=ST16[:, 8:16], op=ALU.add),
                         reads=[rr("ST16", 8, 24)], writes=[rr("ST16", 16, 24)])
                    p.op("scalar", lambda e: e.activation(
                        out=ST16[:, 16:24], in_=ST16[:, 16:24], func=AF.Ln,
                        bias=ctx.COLS[:, 0:1]),
                         reads=[rr("ST16", 16, 24), rr("EPSC")], writes=[rr("ST16", 16, 24)])
                    p.op("scalar", lambda e: e.activation(
                        out=ST16[:, 24:32], in_=ST16[:, 16:24], func=AF.Exp, scale=-0.5),
                         reads=[rr("ST16", 16, 24)], writes=[rr("ST16", 24, 32)])
                    m3 = ST16[:, 0:8][:, :, None].broadcast_to((128, 8, 64))
                    r3 = ST16[:, 24:32][:, :, None].broadcast_to((128, 8, 64))
                    p.op("vector", lambda e, src3=src3, m3=m3: e.tensor_tensor(
                        out=ctx.TMP[:, :].rearrange("p (h d) -> p h d", d=64), in0=src3,
                        in1=m3, op=ALU.subtract),
                         reads=[kk, rr("ST16", 0, 8)], writes=[rr("TMP")])
                    p.op("vector", lambda e, r3=r3, dstT=dstT: e.tensor_tensor(
                        out=dstT[:, :].rearrange("p (h d) -> p h d", d=64),
                        in0=ctx.TMP[:, :].rearrange("p (h d) -> p h d", d=64),
                        in1=r3, op=ALU.mult),
                         reads=[rr("TMP"), rr("ST16", 24, 32)], writes=[rr(dk)])
                    g0 = (0 if which == 0 else 2) * 1024 + ch * 512
                    b0 = (1 if which == 0 else 3) * 1024 + ch * 512
                    p.op("vector", lambda e, dstT=dstT, g0=g0: e.tensor_tensor(
                        out=dstT[:, :], in0=dstT[:, :], in1=GBKV[:, g0:g0 + 512], op=ALU.mult),
                         reads=[rr(dk), rr("GBKV")], writes=[rr(dk)])
                    p.op("vector", lambda e, dstT=dstT, b0=b0: e.tensor_tensor(
                        out=dstT[:, :], in0=dstT[:, :], in1=GBKV[:, b0:b0 + 512], op=ALU.add),
                         reads=[rr(dk), rr("GBKV")], writes=[rr(dk)])
                for hl in range(8):
                    h = ch * 8 + hl
                    prow = (h % 2) * 64
                    pcol = (h // 2) * 64
                    p.op("tensor", lambda e, hl=hl, prow=prow, pcol=pcol: e.matmul(
                        PKV[prow:prow + 64, pcol:pcol + 64],
                        KNT[:, hl * 64:(hl + 1) * 64],
                        VNT[:, hl * 64:(hl + 1) * 64], start=True, stop=True),
                         reads=[rr("KNT"), rr("VNT")], writes=[rr(kPKV)])
                    p.op("vector", lambda e, prow=prow, pcol=pcol: e.tensor_tensor(
                        out=KVSB[prow:prow + 64, pcol:pcol + 64],
                        in0=KVSB[prow:prow + 64, pcol:pcol + 64],
                        in1=PKV[prow:prow + 64, pcol:pcol + 64], op=ALU.add),
                         reads=[rr(kPKV), rr("KVSB")], writes=[rr("KVSB")])
    p.op("vector", lambda e: e.tensor_tensor(out=KVSB[:, :], in0=KVSB[:, :], in1=TGRID[:, :],
                                             op=ALU.mult),
         reads=[rr("KVSB"), rr("TGRID")], writes=[rr("KVSB")])
    p.op("vector", lambda e: e.tensor_tensor(out=KV2[:, :], in0=KVSB[:, :], in1=BGRID[:, :],
                                             op=ALU.add),
         reads=[rr("KVSB"), rr("BGRID")], writes=[rr("KV2")])

    for s in range(NSLAB):
        rows = (MT[0:1, s * SLAB:(s + 1) * SLAB], RT[0:1, s * SLAB:(s + 1) * SLAB],
                rr("MT", s, s + 1), rr("RT", s, s + 1))
        load_x_slab(ctx, x_in, kin, s, stats=False, rowbufs=rows)
        for pt in range(NPT):
            def mk_q(e, pt=pt):
                ins = []
                for kt in range(NPT):
                    ins.append(e.matmul(
                        PQ[:, :], QKVW[:, kt * 3072 + pt * 128: kt * 3072 + pt * 128 + 128],
                        ctx.XNB[:, kt * SLAB:(kt + 1) * SLAB],
                        start=(kt == 0), stop=(kt == NPT - 1)))
                return ins
            p.op("tensor", mk_q, reads=[rr("XNB"), rr("QKVW")], writes=[rr(kPQ)])
            p.op("scalar", lambda e, pt=pt: e.activation(
                out=QSL[:, pt * SLAB:(pt + 1) * SLAB], in_=PQ[:, :], func=AF.Identity,
                bias=QB[:, pt:pt + 1]),
                 reads=[rr(kPQ), rr("QBc")], writes=[rr("QSL", pt, pt + 1)])
        for h in range(16):
            prow = (h % 2) * 64
            pcol = (h // 2) * 64
            qsl = QSL[prow:prow + 64, (h // 2) * SLAB:(h // 2 + 1) * SLAB]
            p.op("tensor", lambda e, prow=prow, pcol=pcol, qsl=qsl: e.matmul(
                PM[prow:prow + 64, :], KV2[prow:prow + 64, pcol:pcol + 64], qsl,
                start=True, stop=True),
                 reads=[rr("KV2"), rr("QSL", h // 2, h // 2 + 1)], writes=[rr(kPM)])
            xsl = ctx.XNB[prow:prow + 64, (h // 2) * SLAB:(h // 2 + 1) * SLAB]
            asl = AXN[prow:prow + 64, (h // 2) * SLAB:(h // 2 + 1) * SLAB]
            p.op("vector", lambda e, prow=prow, xsl=xsl, asl=asl: e.tensor_tensor(
                out=asl, in0=PM[prow:prow + 64, :], in1=xsl, op=ALU.add),
                 reads=[rr(kPM), rr("XNB")], writes=[rr("AXN", h, h + 1)])
        for pt in range(NPT):
            def mk_p1(e, pt=pt):
                ins = []
                for kt in range(NPT):
                    ins.append(e.matmul(
                        PQ[:, :], P1W[:, kt * 1024 + pt * 128: kt * 1024 + pt * 128 + 128],
                        AXN[:, kt * SLAB:(kt + 1) * SLAB],
                        start=(kt == 0), stop=(kt == NPT - 1)))
                return ins
            p.op("tensor", mk_p1, reads=[rr("AXN", 0, 16), rr("P1W")], writes=[rr(kPQ)])
            p.op("scalar", lambda e, pt=pt: e.activation(
                out=H1[:, pt * SLAB:(pt + 1) * SLAB], in_=PQ[:, :], func=AF.Gelu,
                bias=P1B[:, pt:pt + 1]),
                 reads=[rr(kPQ), rr("P1Bc")], writes=[rr("H1", pt, pt + 1)])
        for pt in range(NPT):
            def mk_p2(e, pt=pt):
                ins = []
                for kt in range(NPT):
                    ins.append(e.matmul(
                        PQ[:, :], P2W[:, kt * 1024 + pt * 128: kt * 1024 + pt * 128 + 128],
                        H1[:, kt * SLAB:(kt + 1) * SLAB],
                        start=(kt == 0), stop=(kt == NPT - 1)))
                return ins
            p.op("tensor", mk_p2, reads=[rr("H1", 0, 8), rr("P2W")], writes=[rr(kPQ)])
            xsp = ctx.XS[:, pt * SLAB:(pt + 1) * SLAB]
            p.op("vector", lambda e, pt=pt, xsp=xsp: e.scalar_tensor_tensor(
                out=xsp, in0=PQ[:, :], scalar=P2B[:, pt:pt + 1], in1=xsp,
                op0=ALU.add, op1=ALU.add),
                 reads=[rr(kPQ), rr("P2Bc"), rr("XS")], writes=[rr("XS")])
            p.op("vector", lambda e, pt=pt, xsp=xsp: e.tensor_tensor(
                out=xsp, in0=xsp, in1=ctx.XNB[:, pt * SLAB:(pt + 1) * SLAB], op=ALU.add),
                 reads=[rr("XS"), rr("XNB")], writes=[rr("XS")])
        store_x_slab(ctx, x_out, kout, s)



def emit_attn_passB_slab(p, ctx, s, names, KAR, QAR, VAR, PTB, OUTS, CTXE, RL,
                         BCOLS, PSQ, PCTX, PP):
    for pf in range(4):
        banks = {}
        for hp in range(2):
            h = 2 * pf + hp
            pr = hp * 64
            PC, kPC = names["pctx"][hp]
            PBC, kPBC = names["pbc"][hp]
            banks[hp] = (h, pr, PC, kPC, PBC, kPBC)

        def emit_qk(kq, hps=(0, 1)):
            par = kq % 2
            for hp in hps:
                h, pr, PC, kPC, PBC, kPBC = banks[hp]
                PSQb, kPSQ = names["psq"][hp * 2 + par]
                def mk_qk(e, kq=kq, pr=pr, PSQb=PSQb, pf=pf, s=s):
                    return e.matmul(
                        PSQb[:, :],
                        KAR[pr:pr + 64, pf * T + kq * 128: pf * T + kq * 128 + 128],
                        QAR[pr:pr + 64, pf * T + s * SLAB: pf * T + (s + 1) * SLAB],
                        start=True, stop=True)
                p.op("tensor", mk_qk,
                     reads=[rr(names["kar"], pf * 8, pf * 8 + 8),
                            rr(names["qar"], pf * 8 + s, pf * 8 + s + 1)],
                     writes=[rr(kPSQ)])
            for hp in hps:
                h, pr, PC, kPC, PBC, kPBC = banks[hp]
                PSQb, kPSQ = names["psq"][hp * 2 + par]
                dstp = PTB[:, (hp * 2 + par) * 512:(hp * 2 + par) * 512 + 512]
                p.op("scalar", lambda e, h=h, PSQb=PSQb, dstp=dstp: e.activation(
                    out=dstp, in_=PSQb[:, :], func=AF.Exp, scale=0.125,
                    bias=BCOLS[:, h:h + 1]),
                     reads=[rr(kPSQ), rr(names["bcols"])],
                     writes=[rr(names["ptb"], hp * 2 + par, hp * 2 + par + 1)])

        def emit_pv(kq, hps=(0, 1)):
            par = kq % 2
            for hp in hps:
                h, pr, PC, kPC, PBC, kPBC = banks[hp]
                dstp = PTB[:, (hp * 2 + par) * 512:(hp * 2 + par) * 512 + 512]
                def mk_pv(e, kq=kq, h=h, PC=PC, dstp=dstp):
                    return e.matmul(
                        PC[0:65, :], VAR[:, kq * 520 + h * 65: kq * 520 + h * 65 + 65],
                        dstp, start=(kq == 0), stop=(kq == T // 128 - 1))
                p.op("tensor", mk_pv,
                     reads=[rr(names["ptb"], hp * 2 + par, hp * 2 + par + 1),
                            rr(names["varv"], 0, 32), rr(names["varone"], h, h + 1)],
                     writes=[rr(kPC)])

        nkq = T // 128
        if ATTN_PAIRED:
            for kq in range(nkq):
                emit_qk(kq)
                if kq > 0:
                    emit_pv(kq - 1)
            emit_pv(nkq - 1)
        else:
            for kq in range(nkq):
                emit_qk(kq, (0,))
                if kq > 0:
                    emit_pv(kq - 1, (0,))
            emit_pv(nkq - 1, (0,))
            for kq in range(nkq):
                emit_qk(kq, (1,))
                if kq > 0:
                    emit_pv(kq - 1, (1,))
            emit_pv(nkq - 1, (1,))
        for hp in range(2):
            h, pr, PC, kPC, PBC, kPBC = banks[hp]
            ctxe = CTXE[:, hp * 512:(hp + 1) * 512]
            rl = RL[0:1, 0:512] if RL.shape[1] < 1024 else RL[0:1, hp * 512:(hp + 1) * 512]
            rlk = rr(names["rl"], hp, hp + 1) if RL.shape[1] >= 1024 else rr(names["rl"])
            p.op("vector", lambda e, PC=PC, rl=rl: e.reciprocal(out=rl, in_=PC[64:65, 0:512]),
                 reads=[rr(kPC)], writes=[rlk])
            p.op("tensor", lambda e, PBC=PBC, rl=rl: e.matmul(
                PBC[0:64, :], ctx.sb["c_ones_row"][0:1, 0:64], rl, start=True, stop=True),
                 reads=[rlk, rr("c_ones_row")], writes=[rr(kPBC)])
            p.op("scalar", lambda e, PC=PC, ctxe=ctxe: e.copy(out=ctxe, in_=PC[0:64, 0:512]),
                 reads=[rr(kPC)], writes=[rr(names["ctxe"], hp, hp + 1)])
            dsto = OUTS[pr:pr + 64, pf * 512:(pf + 1) * 512]
            p.op("vector", lambda e, dsto=dsto, ctxe=ctxe, PBC=PBC: e.tensor_tensor(
                out=dsto, in0=ctxe, in1=PBC[0:64, :], op=ALU.mult),
                 reads=[rr(names["ctxe"], hp, hp + 1), rr(kPBC)],
                 writes=[rr(names["outs"], h, h + 1)])


def emit_l0attn_b(ctx, x_in, x_out, kin, kout, w, attout):
    p = ctx.p
    erep = ctx.use_const("c_erep2")
    hmask = ctx.use_const("c_hmask")
    WQG = p.sbuf("WQG", [128, 8 * 512], BF16)
    WKG = p.sbuf("WKG", [128, 8 * 512], BF16)
    WVG = p.sbuf("WVG", [128, 8 * 512], BF16)
    WSG = p.sbuf("WSG", [128, 8 * 128], BF16)
    BQ = p.sbuf("BQ3", [128, 8], F32)
    BK = p.sbuf("BK3", [128, 8], F32)
    BS = p.sbuf("BS3", [128, 2], F32)
    VBCG = p.sbuf("VBCG", [128, 512], BF16)
    QAR = p.sbuf("QAR", [128, 4 * T], BF16)
    KAR = p.sbuf("KAR", [128, 4 * T], BF16)
    VAR = p.sbuf("VAR", [128, 32 * 520], BF16)
    SIGG = p.sbuf("SIGG", [128, 512], BF16)
    SEXPB = p.sbuf("SEXPB", [128, 512], BF16)
    SQT = p.sbuf("SQT", [128, 4 * 512], BF16)
    PTB = p.sbuf("PTB", [128, 2 * 1024], BF16)
    OUTS = p.sbuf("OUTS", [128, 4 * 512], BF16)
    XGRP = p.sbuf("XGRP", [128, 4 * 512], F32)
    CTXE = p.sbuf("CTXE", [64, 2 * 512], BF16)
    NRM = p.sbuf("NRM", [8, 4], F32)
    BROW = p.sbuf("BROW", [1, 8], F32)
    BCOLS = p.sbuf("BCOLS", [128, 8], F32)
    RL = p.sbuf("RL", [1, 2 * 512], F32)
    PSQ = [p.psum("PSQ3%d" % i, [128, 512], F32) for i in range(4)]
    PP = p.psum("PP3", [128, 512], F32)
    PCTX = p.psum("PCTX3", [128, 512], F32)
    kPP, kPCTX = "PP3", "PCTX3"
    p.dma("qsync", BQ[:, :], w["bq_col"].ap(), writes=[rr("BQ3")])
    p.dma("qsync", BK[:, :], w["bk_col"].ap(), writes=[rr("BK3")])
    p.dma("qsync", BS[:, :], w["bs_col"].ap(), writes=[rr("BS3")])

    for g in range(2):
        fc = g * 512
        p.dma("qgps", WQG[:, :].rearrange("p (kt c) -> p kt c", c=512),
              w["wqT"].ap()[:, fc:fc + 512].rearrange("(kt p) c -> p kt c", p=128),
              writes=[rr("WQG")])
        p.dma("qgps", WKG[:, :].rearrange("p (kt c) -> p kt c", c=512),
              w["wkT"].ap()[:, fc:fc + 512].rearrange("(kt p) c -> p kt c", p=128),
              writes=[rr("WKG")])
        p.dma("qgps", WVG[:, :].rearrange("p (kt c) -> p kt c", c=512),
              w["wvT"].ap()[:, fc:fc + 512].rearrange("(kt p) c -> p kt c", p=128),
              writes=[rr("WVG")])
        p.dma("qgps", WSG[:, :].rearrange("p (kt c) -> p kt c", c=128),
              w["wsT"].ap()[:, g * 128:(g + 1) * 128].rearrange("(kt p) c -> p kt c", p=128),
              writes=[rr("WSG")])
        p.dma("qsync", ctx.ROWS[0:1, 0:512], w["bv_row"].ap()[0:1, fc:fc + 512],
              writes=[rr("ROWS", 0, 512)])
        p.op("tensor", lambda e: e.matmul(PP[:, :], ctx.sb["c_ones_row"][:, :],
                                          ctx.ROWS[0:1, 0:512], start=True, stop=True),
             reads=[rr("ROWS", 0, 512), rr("c_ones_row")], writes=[rr(kPP)])
        p.op("vector", lambda e: e.tensor_copy(out=VBCG[:, :], in_=PP[:, :]),
             reads=[rr(kPP)], writes=[rr("VBCG")])
        for hh in range(8):
            ap1 = VAR[:, :].rearrange("p (kt hc) -> p kt hc", hc=520)[:, :, hh * 65 + 64: hh * 65 + 65]
            p.op("vector", lambda e, ap1=ap1: e.memset(ap1, 1.0), writes=[rr("VARONE", hh, hh + 1)])

        for s in range(NSLAB):
            load_x_slab(ctx, x_in, kin, s, stats=True)
            def mk_sig(e):
                ins = []
                for kt in range(NPT):
                    ins.append(e.matmul(PCTX[:, :], WSG[:, kt * 128:(kt + 1) * 128],
                                        ctx.XNB[:, kt * SLAB:(kt + 1) * SLAB],
                                        start=(kt == 0), stop=(kt == NPT - 1)))
                return ins
            p.op("tensor", mk_sig, reads=[rr("XNB"), rr("WSG")], writes=[rr(kPCTX)])
            p.op("scalar", lambda e, g=g: e.activation(out=SIGG[:, :], in_=PCTX[:, :],
                                                       func=AF.Sigmoid, bias=BS[:, g:g + 1]),
                 reads=[rr(kPCTX), rr("BS3")], writes=[rr("SIGG")])
            for pf in range(4):
                def mk_q(e, pf=pf):
                    ins = []
                    for kt in range(NPT):
                        ins.append(e.matmul(
                            PP[:, :], WQG[:, kt * 512 + pf * 128: kt * 512 + pf * 128 + 128],
                            ctx.XNB[:, kt * SLAB:(kt + 1) * SLAB],
                            start=(kt == 0), stop=(kt == NPT - 1)))
                    return ins
                p.op("tensor", mk_q, reads=[rr("XNB"), rr("WQG")], writes=[rr(kPP)])
                p.op("vector", lambda e, g=g, pf=pf: e.tensor_scalar(
                    out=ctx.TMP[:, :], in0=PP[:, :], scalar1=BQ[:, g * 4 + pf: g * 4 + pf + 1],
                    scalar2=None, op0=ALU.add),
                     reads=[rr(kPP), rr("BQ3")], writes=[rr("TMP")])
                p.op("tensor", lambda e, g=g, pf=pf: e.matmul(
                    PCTX[:, :], erep[:, g * 512 + pf * 128: g * 512 + pf * 128 + 128],
                    SIGG[:, :], start=True, stop=True),
                     reads=[rr("SIGG"), rr("c_erep2")], writes=[rr(kPCTX)])
                p.op("vector", lambda e: e.tensor_scalar(out=SEXPB[:, :], in0=PCTX[:, :],
                                                         scalar1=0.95, scalar2=None, op0=ALU.add),
                     reads=[rr(kPCTX)], writes=[rr("SEXPB")])
                dq = QAR[:, pf * T + s * SLAB: pf * T + (s + 1) * SLAB]
                p.op("vector", lambda e, dq=dq: e.tensor_tensor(out=dq, in0=ctx.TMP[:, :],
                                                                in1=SEXPB[:, :], op=ALU.mult),
                     reads=[rr("TMP"), rr("SEXPB")], writes=[rr("QAR", pf * 8 + s, pf * 8 + s + 1)])
                def mk_k(e, pf=pf):
                    ins = []
                    for kt in range(NPT):
                        ins.append(e.matmul(
                            PP[:, :], WKG[:, kt * 512 + pf * 128: kt * 512 + pf * 128 + 128],
                            ctx.XNB[:, kt * SLAB:(kt + 1) * SLAB],
                            start=(kt == 0), stop=(kt == NPT - 1)))
                    return ins
                p.op("tensor", mk_k, reads=[rr("XNB"), rr("WKG")], writes=[rr(kPP)])
                dk = KAR[:, pf * T + s * SLAB: pf * T + (s + 1) * SLAB]
                p.op("vector", lambda e, g=g, pf=pf, dk=dk: e.tensor_scalar(
                    out=dk, in0=PP[:, :], scalar1=BK[:, g * 4 + pf: g * 4 + pf + 1],
                    scalar2=None, op0=ALU.add),
                     reads=[rr(kPP), rr("BK3")], writes=[rr("KAR", pf * 8 + s, pf * 8 + s + 1)])
            for tt in range(4):
                def mk_v(e, tt=tt):
                    ins = []
                    for kt in range(NPT):
                        ins.append(e.matmul(
                            PP[:, :], ctx.XNB[:, kt * SLAB + tt * 128: kt * SLAB + tt * 128 + 128],
                            WVG[:, kt * 512:(kt + 1) * 512],
                            start=(kt == 0), stop=(kt == NPT - 1)))
                    return ins
                p.op("tensor", mk_v, reads=[rr("XNB"), rr("WVG")], writes=[rr(kPP)])
                ktg = s * 4 + tt
                dv = VAR[:, ktg * 520: ktg * 520 + 520].rearrange("p (h c) -> p h c", c=65)[:, :, 0:64]
                p.op("vector", lambda e, dv=dv: e.tensor_tensor(out=dv, in0=PP[:, :],
                                                                in1=VBCG[:, :], op=ALU.add),
                     reads=[rr(kPP), rr("VBCG")], writes=[rr("VARV", ktg, ktg + 1)])
            for which, AR, kk in ((0, QAR, "QAR"), (1, KAR, "KAR")):
                for pf in range(4):
                    src = AR[:, pf * T + s * SLAB: pf * T + (s + 1) * SLAB]
                    p.op("vector", lambda e, src=src, pf=pf: e.tensor_tensor(
                        out=SQT[:, pf * 512:(pf + 1) * 512], in0=src, in1=src, op=ALU.mult),
                         reads=[rr(kk, pf * 8 + s, pf * 8 + s + 1)],
                         writes=[rr("SQT", pf, pf + 1)])
                def mk_n(e):
                    ins = []
                    for pf in range(4):
                        ins.append(e.matmul(PSQ[1][0:8, 0:512], hmask[:, pf * 8:(pf + 1) * 8],
                                            SQT[:, pf * 512:(pf + 1) * 512],
                                            start=(pf == 0), stop=(pf == 3)))
                    return ins
                p.op("tensor", mk_n, reads=[rr("SQT", 0, 4), rr("c_hmask")], writes=[rr("PSQ31")])
                p.op("vector", lambda e: e.tensor_reduce(out=NRM[0:8, 2:3], in_=PSQ[1][0:8, 0:512],
                                                         axis=AX.X, op=ALU.max),
                     reads=[rr("PSQ31")], writes=[rr("NRM", 2, 3)])
                if s == 0:
                    p.op("vector", lambda e, which=which: e.tensor_copy(
                        out=NRM[0:8, which:which + 1], in_=NRM[0:8, 2:3]),
                         reads=[rr("NRM", 2, 3)], writes=[rr("NRM", which, which + 1)])
                else:
                    p.op("vector", lambda e, which=which: e.tensor_tensor(
                        out=NRM[0:8, which:which + 1], in0=NRM[0:8, which:which + 1],
                        in1=NRM[0:8, 2:3], op=ALU.max),
                         reads=[rr("NRM", 2, 3), rr("NRM", which, which + 1)],
                         writes=[rr("NRM", which, which + 1)])
        p.op("scalar", lambda e: e.activation(out=NRM[0:8, 0:1], in_=NRM[0:8, 0:1], func=AF.Ln),
             reads=[rr("NRM", 0, 1)], writes=[rr("NRM", 0, 1)])
        p.op("scalar", lambda e: e.activation(out=NRM[0:8, 1:2], in_=NRM[0:8, 1:2], func=AF.Ln),
             reads=[rr("NRM", 1, 2)], writes=[rr("NRM", 1, 2)])
        p.op("vector", lambda e: e.tensor_tensor(out=NRM[0:8, 3:4], in0=NRM[0:8, 0:1],
                                                 in1=NRM[0:8, 1:2], op=ALU.add),
             reads=[rr("NRM", 0, 2)], writes=[rr("NRM", 3, 4)])
        p.op("scalar", lambda e: e.activation(out=NRM[0:8, 3:4], in_=NRM[0:8, 3:4],
                                              func=AF.Exp, scale=0.25),
             reads=[rr("NRM", 3, 4)], writes=[rr("NRM", 3, 4)])
        p.op("vector", lambda e: e.tensor_scalar(out=NRM[0:8, 3:4], in0=NRM[0:8, 3:4],
                                                 scalar1=-0.125, scalar2=None, op0=ALU.mult),
             reads=[rr("NRM", 3, 4)], writes=[rr("NRM", 3, 4)])
        p.op("tensor", lambda e: e.transpose(PP[0:1, 0:8], NRM[0:8, 3:4],
                                             ctx.sb["c_id_f32"][0:8, 0:8]),
             reads=[rr("NRM", 3, 4), rr("c_id_f32")], writes=[rr(kPP)])
        p.op("vector", lambda e: e.tensor_copy(out=BROW[0:1, 0:8], in_=PP[0:1, 0:8]),
             reads=[rr(kPP)], writes=[rr("BROW")])
        for hh in range(8):
            p.op("tensor", lambda e, hh=hh: e.matmul(PP[:, hh:hh + 1],
                                                     ctx.sb["c_ones_row"][:, :],
                                                     BROW[0:1, hh:hh + 1], start=True, stop=True),
                 reads=[rr("BROW"), rr("c_ones_row")], writes=[rr(kPP)])
        p.op("vector", lambda e: e.tensor_copy(out=BCOLS[:, 0:8], in_=PP[:, 0:8]),
             reads=[rr(kPP)], writes=[rr("BCOLS")])

        names3 = {"kar": "KAR", "qar": "QAR", "varv": "VARV", "varone": "VARONE",
                  "ptb": "PTB", "outs": "OUTS", "ctxe": "CTXE", "rl": "RL",
                  "bcols": "BCOLS",
                  "psq": [(PSQ[i], "PSQ3%d" % i) for i in range(4)],
                  "pctx": [(PCTX, "PCTX3"), (ctx.PS[0], "PSB0")],
                  "pbc": [(PP, "PP3"), (ctx.PS[1], "PSB1")]}
        for s in range(NSLAB):
            emit_attn_passB_slab(p, ctx, s, names3, KAR, QAR, VAR, PTB, OUTS, CTXE,
                                 RL, BCOLS, PSQ, PCTX, PP)
            ao = attout.ap()[fc:fc + 512, s * SLAB:(s + 1) * SLAB].rearrange(
                "(pt p) t -> p pt t", p=128)
            o3 = OUTS[:, :].rearrange("p (pt t) -> p pt t", t=SLAB)
            p.dma("qscal", ao, o3, reads=[rr("OUTS", 0, 8)],
                  writes=[rr("attout", g * 8 + s, g * 8 + s + 1)])

    AOS = p.sbuf("AOS", [128, NPT * SLAB], BF16)
    for s in range(NSLAB):
        xa = x_in.ap()[:, s * SLAB:(s + 1) * SLAB].rearrange("(pt p) t -> p pt t", p=128)
        xs3 = ctx.XS[:, :].rearrange("p (pt t) -> p pt t", t=SLAB)
        p.dma("qsync", xs3, xa, reads=[rr(kin, s, s + 1)], writes=[rr("XS")])
        aoa = attout.ap()[:, s * SLAB:(s + 1) * SLAB].rearrange("(pt p) t -> p pt t", p=128)
        ao3 = AOS[:, :].rearrange("p (pt t) -> p pt t", t=SLAB)
        p.dma("qsync", ao3, aoa, reads=[rr("attout", s, s + 1), rr("attout", 8 + s, 9 + s)],
              writes=[rr("AOS")])
        for pt in range(NPT):
            p.op("vector", lambda e, pt=pt: e.tensor_tensor(
                out=ctx.XS[:, pt * SLAB:(pt + 1) * SLAB],
                in0=ctx.XS[:, pt * SLAB:(pt + 1) * SLAB],
                in1=AOS[:, pt * SLAB:(pt + 1) * SLAB], op=ALU.add),
                 reads=[rr("XS"), rr("AOS")], writes=[rr("XS")])
        store_x_slab(ctx, x_out, kout, s)



def emit_l1attn_b(ctx, x_in, x_out, kin, kout, w, attout):
    p = ctx.p
    hmask = ctx.use_const("c_hmask")
    swap = ctx.use_const("c_swap32")
    ropec = ctx.use_const("c_ropec4")
    ropes = ctx.use_const("c_ropes4")
    WQG = p.sbuf("WQG7", [128, 8 * 512], BF16)
    WKG = p.sbuf("WKG7", [128, 8 * 512], BF16)
    WVG = p.sbuf("WVG7", [128, 8 * 512], BF16)
    WOG = p.sbuf("WOG7", [128, 4 * 1024], BF16)
    BO = p.sbuf("BO7", [128, 8], F32)
    GBC = p.sbuf("GBC7", [128, 16], F32)
    QAR = p.sbuf("QAR7", [128, 4 * T], BF16)
    KAR = p.sbuf("KAR7", [128, 4 * T], BF16)
    VAR = p.sbuf("VAR7", [128, 32 * 520], BF16)
    SQT = p.sbuf("SQT7", [128, 4 * 512], BF16)
    PTB = p.sbuf("PTB7", [128, 2 * 1024], BF16)
    OUTS = p.sbuf("OUTS7", [128, 4 * 512], BF16)
    AOS7 = p.sbuf("AOS7", [128, NPT * SLAB], BF16)
    CTXE = p.sbuf("CTXE7", [64, 2 * 512], BF16)
    RT1 = p.sbuf("RT17", [128, 512], BF16)
    RT2 = p.sbuf("RT27", [128, 512], BF16)
    NRM = p.sbuf("NRM7", [8, 4], F32)
    BROW = p.sbuf("BROW7", [1, 8], F32)
    BCOLS = p.sbuf("BCOLS7", [128, 8], F32)
    RL = p.sbuf("RL7", [1, 512], F32)
    PSQ = [p.psum("PSQ7%d" % i, [128, 512], F32) for i in range(4)]
    PP = p.psum("PP7", [128, 512], F32)
    PCTX = p.psum("PCTX7", [128, 512], F32)
    kPP, kPCTX = "PP7", "PCTX7"
    p.dma("qsync", BO[:, :], w["bo_col"].ap(), writes=[rr("BO7")])
    p.dma("qsync", GBC[:, :], w["gbcols"].ap(), writes=[rr("GBC7")])

    for g in range(2):
        fc = g * 512
        for which, Wt, wk in ((0, WQG, "WQG7"), (1, WKG, "WKG7"), (2, WVG, "WVG7")):
            p.dma("qgps", Wt[:, :].rearrange("p (kt c) -> p kt c", c=512),
                  w["wqkvT"].ap()[:, which * 1024 + fc: which * 1024 + fc + 512]
                  .rearrange("(kt p) c -> p kt c", p=128), writes=[rr(wk)])
        p.dma("qgps", WOG[:, :].rearrange("p (kt c) -> p kt c", c=1024),
              w["woT"].ap()[fc:fc + 512, :].rearrange("(kt p) c -> p kt c", p=128),
              writes=[rr("WOG7")])
        for hh in range(8):
            ap1 = VAR[:, :].rearrange("p (kt hc) -> p kt hc", hc=520)[:, :, hh * 65 + 64: hh * 65 + 65]
            p.op("vector", lambda e, ap1=ap1: e.memset(ap1, 1.0), writes=[rr("VARONE7", hh, hh + 1)])

        for s in range(NSLAB):
            load_x_slab(ctx, x_in, kin, s, stats=True, gb=(GBC, rr("GBC7")))
            for which, Wt, wk, AR, ak in ((0, WQG, "WQG7", QAR, "QAR7"),
                                          (1, WKG, "WKG7", KAR, "KAR7")):
                for pf in range(4):
                    def mk_p(e, pf=pf, Wt=Wt):
                        ins = []
                        for kt in range(NPT):
                            ins.append(e.matmul(
                                PP[:, :], Wt[:, kt * 512 + pf * 128: kt * 512 + pf * 128 + 128],
                                ctx.XNB[:, kt * SLAB:(kt + 1) * SLAB],
                                start=(kt == 0), stop=(kt == NPT - 1)))
                        return ins
                    p.op("tensor", mk_p, reads=[rr("XNB"), rr(wk)], writes=[rr(kPP)])
                    p.op("vector", lambda e: e.tensor_copy(out=RT1[:, :], in_=PP[:, :]),
                         reads=[rr(kPP)], writes=[rr("RT17")])
                    p.op("tensor", lambda e: e.matmul(PCTX[:, :], swap[:, :], RT1[:, :],
                                                      start=True, stop=True),
                         reads=[rr("RT17"), rr("c_swap32")], writes=[rr(kPCTX)])
                    cs = ropec[:, s * SLAB:(s + 1) * SLAB]
                    sn = ropes[:, s * SLAB:(s + 1) * SLAB]
                    p.op("vector", lambda e, sn=sn: e.tensor_tensor(
                        out=RT2[:, :], in0=PCTX[:, :], in1=sn, op=ALU.mult),
                         reads=[rr(kPCTX), rr("c_ropes4")], writes=[rr("RT27")])
                    p.op("vector", lambda e, cs=cs: e.tensor_tensor(
                        out=RT1[:, :], in0=RT1[:, :], in1=cs, op=ALU.mult),
                         reads=[rr("RT17"), rr("c_ropec4")], writes=[rr("RT17")])
                    dst = AR[:, pf * T + s * SLAB: pf * T + (s + 1) * SLAB]
                    p.op("vector", lambda e, dst=dst: e.tensor_tensor(
                        out=dst, in0=RT1[:, :], in1=RT2[:, :], op=ALU.add),
                         reads=[rr("RT17"), rr("RT27")], writes=[rr(ak, pf * 8 + s, pf * 8 + s + 1)])
            for tt in range(4):
                def mk_v(e, tt=tt):
                    ins = []
                    for kt in range(NPT):
                        ins.append(e.matmul(
                            PP[:, :], ctx.XNB[:, kt * SLAB + tt * 128: kt * SLAB + tt * 128 + 128],
                            WVG[:, kt * 512:(kt + 1) * 512],
                            start=(kt == 0), stop=(kt == NPT - 1)))
                    return ins
                p.op("tensor", mk_v, reads=[rr("XNB"), rr("WVG7")], writes=[rr(kPP)])
                ktg = s * 4 + tt
                dv = VAR[:, ktg * 520: ktg * 520 + 520].rearrange("p (h c) -> p h c", c=65)[:, :, 0:64]
                p.op("vector", lambda e, dv=dv: e.tensor_copy(out=dv, in_=PP[:, :]),
                     reads=[rr(kPP)], writes=[rr("VARV7", ktg, ktg + 1)])
            for which, AR, kk in ((0, QAR, "QAR7"), (1, KAR, "KAR7")):
                for pf in range(4):
                    src = AR[:, pf * T + s * SLAB: pf * T + (s + 1) * SLAB]
                    p.op("vector", lambda e, src=src, pf=pf: e.tensor_tensor(
                        out=SQT[:, pf * 512:(pf + 1) * 512], in0=src, in1=src, op=ALU.mult),
                         reads=[rr(kk, pf * 8 + s, pf * 8 + s + 1)],
                         writes=[rr("SQT7", pf, pf + 1)])
                def mk_n(e):
                    ins = []
                    for pf in range(4):
                        ins.append(e.matmul(PSQ[1][0:8, 0:512], hmask[:, pf * 8:(pf + 1) * 8],
                                            SQT[:, pf * 512:(pf + 1) * 512],
                                            start=(pf == 0), stop=(pf == 3)))
                    return ins
                p.op("tensor", mk_n, reads=[rr("SQT7", 0, 4), rr("c_hmask")], writes=[rr("PSQ71")])
                p.op("vector", lambda e: e.tensor_reduce(out=NRM[0:8, 2:3], in_=PSQ[1][0:8, 0:512],
                                                         axis=AX.X, op=ALU.max),
                     reads=[rr("PSQ71")], writes=[rr("NRM7", 2, 3)])
                if s == 0:
                    p.op("vector", lambda e, which=which: e.tensor_copy(
                        out=NRM[0:8, which:which + 1], in_=NRM[0:8, 2:3]),
                         reads=[rr("NRM7", 2, 3)], writes=[rr("NRM7", which, which + 1)])
                else:
                    p.op("vector", lambda e, which=which: e.tensor_tensor(
                        out=NRM[0:8, which:which + 1], in0=NRM[0:8, which:which + 1],
                        in1=NRM[0:8, 2:3], op=ALU.max),
                         reads=[rr("NRM7", 2, 3), rr("NRM7", which, which + 1)],
                         writes=[rr("NRM7", which, which + 1)])
        p.op("scalar", lambda e: e.activation(out=NRM[0:8, 0:1], in_=NRM[0:8, 0:1], func=AF.Ln),
             reads=[rr("NRM7", 0, 1)], writes=[rr("NRM7", 0, 1)])
        p.op("scalar", lambda e: e.activation(out=NRM[0:8, 1:2], in_=NRM[0:8, 1:2], func=AF.Ln),
             reads=[rr("NRM7", 1, 2)], writes=[rr("NRM7", 1, 2)])
        p.op("vector", lambda e: e.tensor_tensor(out=NRM[0:8, 3:4], in0=NRM[0:8, 0:1],
                                                 in1=NRM[0:8, 1:2], op=ALU.add),
             reads=[rr("NRM7", 0, 2)], writes=[rr("NRM7", 3, 4)])
        p.op("scalar", lambda e: e.activation(out=NRM[0:8, 3:4], in_=NRM[0:8, 3:4],
                                              func=AF.Exp, scale=0.25),
             reads=[rr("NRM7", 3, 4)], writes=[rr("NRM7", 3, 4)])
        p.op("vector", lambda e: e.tensor_scalar(out=NRM[0:8, 3:4], in0=NRM[0:8, 3:4],
                                                 scalar1=-0.125, scalar2=None, op0=ALU.mult),
             reads=[rr("NRM7", 3, 4)], writes=[rr("NRM7", 3, 4)])
        p.op("tensor", lambda e: e.transpose(PP[0:1, 0:8], NRM[0:8, 3:4],
                                             ctx.sb["c_id_f32"][0:8, 0:8]),
             reads=[rr("NRM7", 3, 4), rr("c_id_f32")], writes=[rr(kPP)])
        p.op("vector", lambda e: e.tensor_copy(out=BROW[0:1, 0:8], in_=PP[0:1, 0:8]),
             reads=[rr(kPP)], writes=[rr("BROW7")])
        for hh in range(8):
            p.op("tensor", lambda e, hh=hh: e.matmul(PP[:, hh:hh + 1],
                                                     ctx.sb["c_ones_row"][:, :],
                                                     BROW[0:1, hh:hh + 1], start=True, stop=True),
                 reads=[rr("BROW7"), rr("c_ones_row")], writes=[rr(kPP)])
        p.op("vector", lambda e: e.tensor_copy(out=BCOLS[:, 0:8], in_=PP[:, 0:8]),
             reads=[rr(kPP)], writes=[rr("BCOLS7")])

        names7 = {"kar": "KAR7", "qar": "QAR7", "varv": "VARV7", "varone": "VARONE7",
                  "ptb": "PTB7", "outs": "OUTS7", "ctxe": "CTXE7", "rl": "RL7",
                  "bcols": "BCOLS7",
                  "psq": [(PSQ[i], "PSQ7%d" % i) for i in range(4)],
                  "pctx": [(PCTX, "PCTX7"), (ctx.PS[0], "PSB0")],
                  "pbc": [(PP, "PP7"), (ctx.PS[1], "PSB1")]}
        for s in range(NSLAB):
            emit_attn_passB_slab(p, ctx, s, names7, KAR, QAR, VAR, PTB, OUTS, CTXE,
                                 RL, BCOLS, PSQ, PCTX, PP)
            if g == 0:
                for pt in range(NPT):
                    def mk_o(e, pt=pt):
                        ins = []
                        for kt in range(4):
                            ins.append(e.matmul(
                                PP[:, :], WOG[:, kt * 1024 + pt * 128: kt * 1024 + pt * 128 + 128],
                                OUTS[:, kt * 512:(kt + 1) * 512],
                                start=(kt == 0), stop=(kt == 3)))
                        return ins
                    p.op("tensor", mk_o, reads=[rr("OUTS7", 0, 8), rr("WOG7")], writes=[rr(kPP)])
                    p.op("scalar", lambda e, pt=pt: e.activation(
                        out=AOS7[:, pt * SLAB:(pt + 1) * SLAB], in_=PP[:, :],
                        func=AF.Identity, bias=BO[:, pt:pt + 1]),
                         reads=[rr(kPP), rr("BO7")], writes=[rr("AOS7s", pt, pt + 1)])
                ao = attout.ap()[:, s * SLAB:(s + 1) * SLAB].rearrange("(pt p) t -> p pt t", p=128)
                a3 = AOS7[:, :].rearrange("p (pt t) -> p pt t", t=SLAB)
                p.dma("qscal", ao, a3, reads=[rr("AOS7s", 0, 8)], writes=[rr("attout7", s, s + 1)])
            else:
                ao = attout.ap()[:, s * SLAB:(s + 1) * SLAB].rearrange("(pt p) t -> p pt t", p=128)
                a3 = AOS7[:, :].rearrange("p (pt t) -> p pt t", t=SLAB)
                p.dma("qsync", a3, ao, reads=[rr("attout7", s, s + 1)], writes=[rr("AOS7s", 0, 8)])
                xa = x_in.ap()[:, s * SLAB:(s + 1) * SLAB].rearrange("(pt p) t -> p pt t", p=128)
                xs3 = ctx.XS[:, :].rearrange("p (pt t) -> p pt t", t=SLAB)
                p.dma("qsync", xs3, xa, reads=[rr(kin, s, s + 1)], writes=[rr("XS")])
                for pt in range(NPT):
                    def mk_o(e, pt=pt):
                        ins = []
                        for kt in range(4):
                            ins.append(e.matmul(
                                PP[:, :], WOG[:, kt * 1024 + pt * 128: kt * 1024 + pt * 128 + 128],
                                OUTS[:, kt * 512:(kt + 1) * 512],
                                start=(kt == 0), stop=(kt == 3)))
                        return ins
                    p.op("tensor", mk_o, reads=[rr("OUTS7", 0, 8), rr("WOG7")], writes=[rr(kPP)])
                    sl = slice(pt * SLAB, (pt + 1) * SLAB)
                    p.op("vector", lambda e, sl=sl: e.tensor_tensor(
                        out=ctx.XS[:, sl], in0=ctx.XS[:, sl], in1=AOS7[:, sl], op=ALU.add),
                         reads=[rr("AOS7s", 0, 8), rr("XS")], writes=[rr("XS")])
                    p.op("vector", lambda e, sl=sl: e.tensor_tensor(
                        out=ctx.XS[:, sl], in0=ctx.XS[:, sl], in1=PP[:, :], op=ALU.add),
                         reads=[rr(kPP), rr("XS")], writes=[rr("XS")])
                store_x_slab(ctx, x_out, kout, s)


def emit_precast(ctx, src_dram, dst_dram, rows, cols, uid, BB=None):
    p = ctx.p
    rchunk = max(128, rows // 8)
    sa = src_dram.ap()
    da = dst_dram.ap()
    for i, r0 in enumerate(range(0, rows, rchunk)):
        r1 = min(rows, r0 + rchunk)
        p.dma("qgps", da[r0:r1, :], sa[r0:r1, :],
              writes=[rr("pcdst" + uid, i, i + 1)], tag="pc")



def emit_mamba(ctx, x_in, x_out, kin, kout, w):
    p = ctx.p
    oh16 = ctx.use_const("c_oh16")
    idb = ctx.use_const("c_id_bf16")
    ones128 = ctx.use_const("c_ones128")
    utri = ctx.use_const("c_utri")
    ltris = ctx.use_const("c_ltris")
    cmask = ctx.use_const("c_causmask")
    WXC = p.sbuf("WXC1", [128, 8 * 2048], BF16)
    WZ = p.sbuf("WZ1", [128, 8 * 2048], BF16)
    WX = p.sbuf("WX1", [128, 16 * 33], BF16)
    WOUT = p.sbuf("WOUT1", [128, 16 * 1024], BF16)
    CVW = p.sbuf("CVW1", [128, 64], F32)
    CVB = p.sbuf("CVB1", [128, 16], F32)
    DCS = p.sbuf("DCS1", [128, 16], F32)
    ALC = p.sbuf("ALC1", [16, 1], F32)
    ANC = p.sbuf("ANC1", [16, 1], F32)
    ABC = p.sbuf("ABC1", [128, 16], F32)
    XCB = p.sbuf("XCB1", [128, 515], BF16)
    if MAMBA_FP32D:
        WXDF = p.sbuf("WXDF1", [128, 16], F32)
        XSF32T = p.sbuf("XSF32T", [128, 128], F32)
    HALO = p.sbuf("HALO1", [128, 48], BF16)
    XSFM = p.sbuf("XSFM1", [128, 16 * 512], BF16)
    XST = p.sbuf("XST1", [128, 2048], BF16)
    ZS = p.sbuf("ZS1", [128, 16 * 512], BF16)
    YB = p.sbuf("YB1", [128, 16 * 512], BF16)
    HF = p.sbuf("HF1", [16, 2048], F32)
    HBF = p.sbuf("HBF1", [16, 2048], BF16)
    DELT = p.sbuf("DELT1", [128, 8], F32)
    T16 = p.sbuf("T16A", [128, 16], F32)
    E1 = p.sbuf("E1B", [128, 16], F32)
    EB = p.sbuf("EBB", [128, 16], F32)
    CEX = p.sbuf("CEX1", [128, 16], BF16)
    BEX = p.sbuf("BEX1", [128, 16], BF16)
    CEXT = p.sbuf("CEXT1", [16, 128], BF16)
    BBAR = p.sbuf("BBAR1", [128, 16], BF16)
    CPB = p.sbuf("CPB1", [128, 16], BF16)
    CPT = p.sbuf("CPT1", [16, 128], BF16)
    DEC = p.sbuf("DEC1", [16, 1], F32)
    TDA = p.sbuf("TDA1", [128, 128], F32)
    TDB = p.sbuf("TDB1", [128, 128], F32)
    GB1 = p.sbuf("GB1", [128, 128], BF16)
    MTA = p.sbuf("MTA1", [128, 128], F32)
    MTB = p.sbuf("MTB1", [128, 128], BF16)
    PA = p.psum("PA1", [128, 512], F32)
    PB = p.psum("PB1", [128, 512], F32)
    PD = p.psum("PD1", [128, 512], F32)
    PST = p.psum("PST1", [128, 512], F32)
    PT1 = p.psum("PT1", [128, 512], F32)
    kPA, kPB, kPD, kPST, kPT = "PA1", "PB1", "PD1", "PST1", "PT1"

    for q in range(2):
        sl = slice(q * 4, (q + 1) * 4)
        p.dma("qgps", WXC[:, :].rearrange("p (kt c) -> p kt c", c=2048)[:, sl, :],
              w["winxc"].ap().rearrange("(kt p) c -> p kt c", p=128)[:, sl, :],
              writes=[rr("WXC1", q, q + 1)])
        p.dma("qgps", WZ[:, :].rearrange("p (kt c) -> p kt c", c=2048)[:, sl, :],
              w["winz"].ap().rearrange("(kt p) c -> p kt c", p=128)[:, sl, :],
              writes=[rr("WZ1", q, q + 1)])
    p.dma("qgps", WX[:, :].rearrange("p (kt c) -> p kt c", c=33),
          w["wx"].ap().rearrange("(kt p) c -> p kt c", p=128), writes=[rr("WX1")])
    for q in range(4):
        sl = slice(q * 4, (q + 1) * 4)
        p.dma("qgps", WOUT[:, :].rearrange("p (kt c) -> p kt c", c=1024)[:, sl, :],
              w["wout"].ap().rearrange("(kt p) c -> p kt c", p=128)[:, sl, :],
              writes=[rr("WOUT1", q, q + 1)])
    p.dma("qsync", CVW[:, :], w["convcols"].ap(), writes=[rr("CVW1")])
    if MAMBA_FP32D:
        p.dma("qsync", WXDF[:, :], w["wxd_cols"].ap(), writes=[rr("WXDF1")])
    p.dma("qsync", CVB[:, :], w["convbcol"].ap(), writes=[rr("CVB1")])
    p.dma("qsync", DCS[:, :], w["dcols"].ap(), writes=[rr("DCS1")])
    p.dma("qsync", ALC[:, :], w["alogcol"].ap(), writes=[rr("ALC1")])
    p.op("scalar", lambda e: e.activation(out=ANC[:, :], in_=ALC[:, :], func=AF.Exp),
         reads=[rr("ALC1")], writes=[rr("ANC1")])
    p.op("vector", lambda e: e.tensor_scalar(out=ANC[:, :], in0=ANC[:, :], scalar1=-1.0,
                                             scalar2=None, op0=ALU.mult),
         reads=[rr("ANC1")], writes=[rr("ANC1")])
    p.op("tensor", lambda e: e.transpose(PT1[0:1, 0:16], ANC[:, :],
                                         ctx.sb["c_id_f32"][0:16, 0:16]),
         reads=[rr("ANC1"), rr("c_id_f32")], writes=[rr(kPT)])
    p.op("vector", lambda e: e.tensor_copy(out=ctx.ROWS[0:1, 0:16], in_=PT1[0:1, 0:16]),
         reads=[rr(kPT)], writes=[rr("ROWS", 0, 16)])
    p.op("tensor", lambda e: e.matmul(PT1[:, 0:16], ctx.sb["c_ones_row"][:, :],
                                      ctx.ROWS[0:1, 0:16], start=True, stop=True),
         reads=[rr("ROWS", 0, 16), rr("c_ones_row")], writes=[rr(kPT)])
    p.op("vector", lambda e: e.tensor_copy(out=ABC[:, :], in_=PT1[:, 0:16]),
         reads=[rr(kPT)], writes=[rr("ABC1")])
    p.op("vector", lambda e: e.memset(HF[:, :], 0.0), writes=[rr("HF1")])
    p.op("vector", lambda e: e.memset(HBF[:, :], 0.0), writes=[rr("HBF1")])
    p.op("vector", lambda e: e.memset(HALO[:, :], 0.0), writes=[rr("HALO1", 0, 16)])

    DPS = [(PD, rr(kPD)), (ctx.PS[1], rr("PSB1")), (PT1, rr(kPT)),
           (ctx.PS[0], rr("PSB0"))]
    for s in range(NSLAB):
        load_x_slab(ctx, x_in, kin, s, stats=True)
        for ft in range(16):
            xb = 0
            p.op("vector", lambda e, ft=ft, xb=xb: e.tensor_copy(
                out=XCB[:, xb: xb + 3], in_=HALO[:, ft * 3: ft * 3 + 3]),
                 reads=[rr("HALO1", ft, ft + 1)], writes=[rr("XCB1", ft % 2, ft % 2 + 1)])
            PXCb, kXCb = (PA, kPA) if ft % 2 == 0 else (PD, kPD)
            def mk_xc(e, ft=ft, PXCb=PXCb):
                ins = []
                for kt in range(NPT):
                    ins.append(e.matmul(
                        PXCb[:, :], WXC[:, kt * 2048 + ft * 128: kt * 2048 + ft * 128 + 128],
                        ctx.XNB[:, kt * SLAB:(kt + 1) * SLAB],
                        start=(kt == 0), stop=(kt == NPT - 1)))
                return ins
            p.op("tensor", mk_xc, reads=[rr("XNB"), rr("WXC1")], writes=[rr(kXCb)])
            p.op("vector", lambda e, xb=xb, PXCb=PXCb: e.tensor_copy(
                out=XCB[:, xb + 3: xb + 515], in_=PXCb[:, :]),
                 reads=[rr(kXCb)], writes=[rr("XCB1", ft % 2, ft % 2 + 1)])
            p.op("vector", lambda e, ft=ft, xb=xb: e.tensor_copy(
                out=HALO[:, ft * 3: ft * 3 + 3], in_=XCB[:, xb + 512: xb + 515]),
                 reads=[rr("XCB1", ft % 2, ft % 2 + 1)], writes=[rr("HALO1", ft, ft + 1)])
            p.op("vector", lambda e, ft=ft, xb=xb: e.tensor_scalar(
                out=ctx.TMP[:, :], in0=XCB[:, xb + 0: xb + 512],
                scalar1=CVW[:, ft * 4: ft * 4 + 1], scalar2=None, op0=ALU.mult),
                 reads=[rr("XCB1", ft % 2, ft % 2 + 1), rr("CVW1")], writes=[rr("TMP")])
            for j in range(1, 4):
                p.op("vector", lambda e, ft=ft, j=j, xb=xb: e.scalar_tensor_tensor(
                    out=ctx.TMP[:, :], in0=XCB[:, xb + j: xb + j + 512],
                    scalar=CVW[:, ft * 4 + j: ft * 4 + j + 1], in1=ctx.TMP[:, :],
                    op0=ALU.mult, op1=ALU.add),
                     reads=[rr("XCB1", ft % 2, ft % 2 + 1), rr("CVW1"), rr("TMP")],
                     writes=[rr("TMP")])
            p.op("scalar", lambda e, ft=ft: e.activation(
                out=XSFM[:, ft * 512:(ft + 1) * 512], in_=ctx.TMP[:, :], func=AF.Silu,
                bias=CVB[:, ft:ft + 1]),
                 reads=[rr("TMP"), rr("CVB1")], writes=[rr("XSFM1", ft, ft + 1)])
            for cc in (range(4) if MAMBA_FP32D else []):
                dps, dkey = DPS[cc]
                p.op("scalar", lambda e, cc=cc: e.activation(
                    out=XSF32T[:, :], in_=ctx.TMP[:, cc * 128:(cc + 1) * 128], func=AF.Silu,
                    bias=CVB[:, ft:ft + 1]),
                     reads=[rr("TMP"), rr("CVB1")], writes=[rr("XSF32T1")])
                p.op("tensor", lambda e, ft=ft, dps=dps: e.matmul(
                    dps[:, 384:385], XSF32T[:, :], WXDF[:, ft:ft + 1],
                    start=(ft == 0), stop=(ft == 15)),
                     reads=[rr("XSF32T1"), rr("WXDF1")], writes=[dkey])
            PZb, kZb = (PB, kPB) if ft % 2 == 0 else (PT1, kPT)
            def mk_z(e, ft=ft, PZb=PZb):
                ins = []
                for kt in range(NPT):
                    ins.append(e.matmul(
                        PZb[:, :], WZ[:, kt * 2048 + ft * 128: kt * 2048 + ft * 128 + 128],
                        ctx.XNB[:, kt * SLAB:(kt + 1) * SLAB],
                        start=(kt == 0), stop=(kt == NPT - 1)))
                return ins
            p.op("tensor", mk_z, reads=[rr("XNB"), rr("WZ1")], writes=[rr(kZb)])
            p.op("scalar", lambda e, ft=ft, PZb=PZb: e.activation(
                out=ZS[:, ft * 512:(ft + 1) * 512], in_=PZb[:, :], func=AF.Silu),
                 reads=[rr(kZb)], writes=[rr("ZS1", ft, ft + 1)])
        for c in range(4):
            par = c % 2
            for ft in range(16):
                pbf = PT1[:, :].bitcast(BF16)
                p.op("tensor", lambda e, ft=ft, c=c, pbf=pbf: e.transpose(
                    pbf[:, 0:128], XSFM[:, ft * 512 + c * 128: ft * 512 + c * 128 + 128],
                    idb[:, :]),
                     reads=[rr("XSFM1", ft, ft + 1), rr("c_id_bf16")], writes=[rr(kPT)])
                p.op("vector", lambda e, ft=ft, pbf=pbf: e.tensor_copy(
                    out=XST[:, ft * 128: ft * 128 + 128], in_=pbf[:, 0:128]),
                     reads=[rr(kPT)], writes=[rr("XST1", ft, ft + 1)])
            def mk_xd(e, c=c):
                ins = []
                for ft in range(16):
                    ins.append(e.matmul(
                        PD[0:128, 0:33], XSFM[:, ft * 512 + c * 128: ft * 512 + c * 128 + 128],
                        WX[:, ft * 33:(ft + 1) * 33],
                        start=(ft == 0), stop=(ft == 15)))
                return ins
            p.op("tensor", mk_xd, reads=[rr("XSFM1", 0, 16), rr("WX1")], writes=[rr(kPD)])
            if MAMBA_FP32D:
                dps, dkey = DPS[c]
                dsrc = dps[:, 384:385]
            else:
                dps, dkey = PD, rr(kPD)
                dsrc = PD[:, 0:1]
            p.op("scalar", lambda e, dsrc=dsrc: e.activation(out=DELT[:, 5:6], in_=dsrc,
                                                  func=AF.Abs),
                 reads=[dkey], writes=[rr("DELT1", 5, 6)])
            p.op("scalar", lambda e: e.activation(out=DELT[:, 6:7], in_=DELT[:, 5:6],
                                                  func=AF.Exp, scale=-1.0),
                 reads=[rr("DELT1", 5, 6)], writes=[rr("DELT1", 6, 7)])
            p.op("scalar", lambda e: e.activation(out=DELT[:, 6:7], in_=DELT[:, 6:7],
                                                  func=AF.Ln, bias=ctx.COLS[:, 1:2]),
                 reads=[rr("DELT1", 6, 7), rr("ONEC")], writes=[rr("DELT1", 6, 7)])
            p.op("scalar", lambda e, dsrc=dsrc: e.activation(out=DELT[:, 7:8], in_=dsrc,
                                                  func=AF.Relu),
                 reads=[dkey], writes=[rr("DELT1", 7, 8)])
            p.op("vector", lambda e: e.tensor_tensor(out=DELT[:, 0:1], in0=DELT[:, 7:8],
                                                     in1=DELT[:, 6:7], op=ALU.add),
                 reads=[rr("DELT1", 6, 8)], writes=[rr("DELT1", 0, 1)])
            p.op("vector", lambda e: e.tensor_scalar(out=DELT[:, 1:2], in0=DELT[:, 0:1],
                                                     scalar1=-1.0, scalar2=None, op0=ALU.mult),
                 reads=[rr("DELT1", 0, 1)], writes=[rr("DELT1", 1, 2)])
            p.op("tensor", lambda e: e.matmul(PT1[:, 0:1], utri[:, :], DELT[:, 0:1],
                                              start=True, stop=True),
                 reads=[rr("DELT1", 0, 1), rr("c_utri")], writes=[rr(kPT)])
            p.op("vector", lambda e: e.tensor_copy(out=DELT[:, 2:3], in_=PT1[:, 0:1]),
                 reads=[rr(kPT)], writes=[rr("DELT1", 2, 3)])
            p.op("tensor", lambda e: e.matmul(PT1[:, 0:1], ltris[:, :], DELT[:, 0:1],
                                              start=True, stop=True),
                 reads=[rr("DELT1", 0, 1), rr("c_ltris")], writes=[rr(kPT)])
            p.op("vector", lambda e: e.tensor_copy(out=DELT[:, 3:4], in_=PT1[:, 0:1]),
                 reads=[rr(kPT)], writes=[rr("DELT1", 3, 4)])
            p.op("vector", lambda e: e.tensor_tensor(out=DELT[:, 4:5], in0=DELT[:, 2:3],
                                                     in1=DELT[:, 3:4], op=ALU.add),
                 reads=[rr("DELT1", 2, 4)], writes=[rr("DELT1", 4, 5)])
            p.op("vector", lambda e: e.tensor_scalar(out=T16[:, :], in0=ABC[:, :],
                                                     scalar1=DELT[:, 2:3], scalar2=None,
                                                     op0=ALU.mult),
                 reads=[rr("ABC1"), rr("DELT1", 2, 3)], writes=[rr("T16A")])
            p.op("scalar", lambda e: e.activation(out=E1[:, :], in_=T16[:, :], func=AF.Exp),
                 reads=[rr("T16A")], writes=[rr("E1B")])
            p.op("vector", lambda e: e.tensor_tensor(out=CEX[:, :], in0=PD[:, 17:33],
                                                     in1=E1[:, :], op=ALU.mult),
                 reads=[rr(kPD), rr("E1B")], writes=[rr("CEX1")])
            p.op("vector", lambda e: e.tensor_scalar(out=T16[:, :], in0=ABC[:, :],
                                                     scalar1=DELT[:, 3:4], scalar2=None,
                                                     op0=ALU.mult),
                 reads=[rr("ABC1"), rr("DELT1", 3, 4)], writes=[rr("T16A")])
            p.op("scalar", lambda e: e.activation(out=EB[:, :], in_=T16[:, :], func=AF.Exp),
                 reads=[rr("T16A")], writes=[rr("EBB")])
            p.op("vector", lambda e: e.tensor_scalar(out=T16[:, :], in0=PD[:, 1:17],
                                                     scalar1=DELT[:, 0:1], scalar2=None,
                                                     op0=ALU.mult),
                 reads=[rr(kPD), rr("DELT1", 0, 1)], writes=[rr("T16A")])
            p.op("vector", lambda e: e.tensor_copy(out=BBAR[:, :], in_=T16[:, :]),
                 reads=[rr("T16A")], writes=[rr("BBAR1")])
            p.op("vector", lambda e: e.tensor_tensor(out=BEX[:, :], in0=T16[:, :],
                                                     in1=EB[:, :], op=ALU.mult),
                 reads=[rr("T16A"), rr("EBB")], writes=[rr("BEX1")])
            p.op("vector", lambda e: e.tensor_scalar(out=DEC[:, :], in0=ANC[:, :],
                                                     scalar1=DELT[0:16, 4:5], scalar2=None,
                                                     op0=ALU.mult),
                 reads=[rr("ANC1"), rr("DELT1", 4, 5)], writes=[rr("DEC1")])
            p.op("scalar", lambda e: e.activation(out=DEC[:, :], in_=DEC[:, :], func=AF.Exp),
                 reads=[rr("DEC1")], writes=[rr("DEC1")])
            pbf = PT1[:, :].bitcast(BF16)
            p.op("tensor", lambda e, pbf=pbf: e.transpose(pbf[0:16, 0:128], CEX[:, :],
                                                          idb[:, :]),
                 reads=[rr("CEX1"), rr("c_id_bf16")], writes=[rr(kPT)])
            p.op("vector", lambda e, pbf=pbf: e.tensor_copy(out=CEXT[:, :], in_=pbf[0:16, 0:128]),
                 reads=[rr(kPT)], writes=[rr("CEXT1")])
            p.op("vector", lambda e: e.tensor_copy(out=CPB[:, :], in_=PD[:, 17:33]),
                 reads=[rr(kPD)], writes=[rr("CPB1")])
            p.op("tensor", lambda e, pbf=pbf: e.transpose(pbf[0:16, 0:128], CPB[:, :],
                                                          idb[:, :]),
                 reads=[rr("CPB1"), rr("c_id_bf16")], writes=[rr(kPT)])
            p.op("vector", lambda e, pbf=pbf: e.tensor_copy(out=CPT[:, :], in_=pbf[0:16, 0:128]),
                 reads=[rr(kPT)], writes=[rr("CPT1")])
            p.op("vector", lambda e: e.tensor_scalar(out=TDA[:, :], in0=utri[:, :],
                                                     scalar1=DELT[:, 1:2], scalar2=None,
                                                     op0=ALU.mult),
                 reads=[rr("c_utri"), rr("DELT1", 1, 2)], writes=[rr("TDA1")])
            p.op("vector", lambda e: e.tensor_scalar(out=TDB[:, :], in0=ones128[:, :],
                                                     scalar1=DELT[:, 0:1], scalar2=None,
                                                     op0=ALU.mult),
                 reads=[rr("c_ones128"), rr("DELT1", 0, 1)], writes=[rr("TDB1")])
            def mk_td(e):
                i1 = e.matmul(PD[:, 128:256], TDA[:, :], ones128[:, :], start=True, stop=False)
                i2 = e.matmul(PD[:, 128:256], TDB[:, :], utri[:, :], start=False, stop=True)
                return [i1, i2]
            p.op("tensor", mk_td, reads=[rr("TDA1"), rr("TDB1"), rr("c_ones128"), rr("c_utri")],
                 writes=[rr(kPD)])
            p.op("vector", lambda e: e.tensor_tensor(out=PST[:, 0:128], in0=PD[:, 128:256],
                                                     in1=cmask[:, :], op=ALU.add),
                 reads=[rr(kPD), rr("c_causmask")], writes=[rr(kPST)])
            for n in range(16):
                p.op("scalar", lambda e, n=n: e.activation(out=GB1[:, :], in_=PST[:, 0:128],
                                                           func=AF.Exp, scale=ABC[:, n:n + 1]),
                     reads=[rr(kPST), rr("ABC1")], writes=[rr("GB1k")])
                p.op("tensor", lambda e, n=n: e.matmul(
                    PT1[:, 0:128], oh16[:, n * 128:(n + 1) * 128], CPT[:, :],
                    start=True, stop=True),
                     reads=[rr("CPT1"), rr("c_oh16")], writes=[rr(kPT)])
                if n == 0:
                    p.op("vector", lambda e, n=n: e.scalar_tensor_tensor(
                        out=MTA[:, :], in0=GB1[:, :], scalar=BBAR[:, n:n + 1],
                        in1=PT1[:, 0:128], op0=ALU.mult, op1=ALU.mult),
                         reads=[rr("GB1k"), rr("BBAR1"), rr(kPT)], writes=[rr("MTA1")])
                else:
                    p.op("vector", lambda e, n=n: e.scalar_tensor_tensor(
                        out=PD[:, 0:128], in0=GB1[:, :], scalar=BBAR[:, n:n + 1],
                        in1=PT1[:, 0:128], op0=ALU.mult, op1=ALU.mult),
                         reads=[rr("GB1k"), rr("BBAR1"), rr(kPT)], writes=[rr(kPD)])
                    p.op("vector", lambda e: e.tensor_tensor(out=MTA[:, :], in0=MTA[:, :],
                                                             in1=PD[:, 0:128], op=ALU.add),
                         reads=[rr(kPD), rr("MTA1")], writes=[rr("MTA1")])
            p.op("vector", lambda e: e.tensor_copy(out=MTB[:, :], in_=MTA[:, :]),
                 reads=[rr("MTA1")], writes=[rr("MTB1")])
            for ft in range(16):
                pw = PA if ft % 2 == 0 else PB
                kpw = kPA if ft % 2 == 0 else kPB
                def mk_y(e, ft=ft, pw=pw):
                    i1 = e.matmul(pw[:, 0:128],
                                  XST[:, ft * 128: ft * 128 + 128],
                                  MTB[:, :], start=True, stop=False)
                    i2 = e.matmul(pw[:, 0:128], HBF[:, ft * 128:(ft + 1) * 128],
                                  CEXT[:, :], start=False, stop=True)
                    return [i1, i2]
                p.op("tensor", mk_y,
                     reads=[rr("XST1", ft, ft + 1), rr("MTB1"), rr("HBF1"), rr("CEXT1")],
                     writes=[rr(kpw)])
                p.op("vector", lambda e, ft=ft, c=c, pw=pw: e.scalar_tensor_tensor(
                    out=PD[:, 0:128], in0=XSFM[:, ft * 512 + c * 128: ft * 512 + c * 128 + 128],
                    scalar=DCS[:, ft:ft + 1], in1=pw[:, 0:128], op0=ALU.mult, op1=ALU.add),
                     reads=[rr(kpw), rr("XSFM1", ft, ft + 1), rr("DCS1")], writes=[rr(kPD)])
                p.op("vector", lambda e, ft=ft, c=c: e.tensor_tensor(
                    out=YB[:, ft * 512 + c * 128: ft * 512 + c * 128 + 128], in0=PD[:, 0:128],
                    in1=ZS[:, ft * 512 + c * 128: ft * 512 + c * 128 + 128], op=ALU.mult),
                     reads=[rr(kPD), rr("ZS1", ft, ft + 1)], writes=[rr("YB1", ft, ft + 1)])
            for dc in range(4):
                def mk_h(e, dc=dc):
                    return e.matmul(PST[0:16, :], BEX[:, :],
                                    XST[:, dc * 512:(dc + 1) * 512],
                                    start=True, stop=True)
                p.op("tensor", mk_h,
                     reads=[rr("BEX1"), rr("XST1", 0, 16)], writes=[rr(kPST)])
                p.op("vector", lambda e, dc=dc: e.scalar_tensor_tensor(
                    out=HF[:, dc * 512:(dc + 1) * 512], in0=HF[:, dc * 512:(dc + 1) * 512],
                    scalar=DEC[:, 0:1], in1=PST[0:16, :], op0=ALU.mult, op1=ALU.add),
                     reads=[rr(kPST), rr("HF1"), rr("DEC1")], writes=[rr("HF1")])
                p.op("vector", lambda e, dc=dc: e.tensor_copy(
                    out=HBF[:, dc * 512:(dc + 1) * 512], in_=HF[:, dc * 512:(dc + 1) * 512]),
                     reads=[rr("HF1")], writes=[rr("HBF1")])
        for dot in range(NPT):
            def mk_o(e, dot=dot):
                ins = []
                for kt in range(16):
                    ins.append(e.matmul(
                        PA[:, :], WOUT[:, kt * 1024 + dot * 128: kt * 1024 + dot * 128 + 128],
                        YB[:, kt * 512:(kt + 1) * 512],
                        start=(kt == 0), stop=(kt == 15)))
                return ins
            p.op("tensor", mk_o, reads=[rr("YB1", 0, 16), rr("WOUT1")], writes=[rr(kPA)])
            p.op("vector", lambda e, dot=dot: e.tensor_tensor(
                out=ctx.XS[:, dot * SLAB:(dot + 1) * SLAB],
                in0=ctx.XS[:, dot * SLAB:(dot + 1) * SLAB], in1=PA[:, :], op=ALU.add),
                 reads=[rr(kPA), rr("XS")], writes=[rr("XS")])
        store_x_slab(ctx, x_out, kout, s)



def build_l0ffn_test():
    nc = bass.Bass("TRN2", target_bir_lowering=False)
    xin = nc.dram_tensor("xin", [D, T], F32, kind="ExternalInput")
    xout = nc.dram_tensor("xout", [D, T], F32, kind="ExternalOutput")
    win = nc.dram_tensor("w_in", [8, 4096, 1024], F32, kind="ExternalInput")
    wotT = nc.dram_tensor("w_outT", [8, 4096, 1024], F32, kind="ExternalInput")
    wgT = nc.dram_tensor("w_gateT", [D, 8], F32, kind="ExternalInput")
    me = nc.dram_tensor("me", [1024, 8192], BF16, kind="Internal")
    stack = ExitStack()
    p = Prog(nc, stack)
    ctx = Ctx(p)
    wg_sb = p.sbuf("WGsb", [128, NPT * 8], BF16)
    p.dma("qgps", wg_sb[:, :].rearrange("p (pt e8) -> p pt e8", e8=8),
          wgT.ap().rearrange("(pt p) e8 -> p pt e8", p=128),
          writes=[rr("WGsb")])
    keyscb = nc.dram_tensor("keyscb", [D, 8192], BF16, kind="Internal")
    valscb = nc.dram_tensor("valscb", [8192, D], BF16, kind="Internal")
    with p.phase():
        emit_me_build(ctx, win, wotT, me)
        emit_precast(ctx, keysc, keyscb, D, 8192, "K")
        emit_precast(ctx, valsc, valscb, 8192, D, "V")
    with p.phase():
        emit_l0ffn(ctx, xin, xout, "xin", "xout", me, "A", wg_sb)
    p.finalize()
    stack.close()
    return nc


S5_W_SHAPES = [("qkvwT", (1024, 3072)), ("p1wT", (1024, 1024)), ("p2wT", (1024, 1024)),
               ("qkvb_col", (128, 24)), ("qkvb_row", (1, 3072)), ("p1b_col", (128, 8)),
               ("p2b_col", (128, 8)), ("kng_row", (1, 1024)), ("knb_row", (1, 1024)),
               ("vng_row", (1, 1024)), ("vnb_row", (1, 1024)), ("tempcol", (16, 1)),
               ("biasgrid", (128, 512))]


def prep_s5_weights(qkv_w, qkv_b, kn_g, kn_b, vn_g, vn_b, temperature, attn_bias,
                    p1_w, p1_b, p2_w, p2_b, prefix=""):
    d = {}
    d[prefix + "qkvwT"] = np.ascontiguousarray(qkv_w.T)
    d[prefix + "p1wT"] = np.ascontiguousarray(p1_w.T)
    d[prefix + "p2wT"] = np.ascontiguousarray(p2_w.T)
    d[prefix + "qkvb_col"] = np.ascontiguousarray(qkv_b.reshape(24, 128).T)
    d[prefix + "qkvb_row"] = np.ascontiguousarray(qkv_b.reshape(1, 3072))
    d[prefix + "p1b_col"] = np.ascontiguousarray(p1_b.reshape(8, 128).T)
    d[prefix + "p2b_col"] = np.ascontiguousarray(p2_b.reshape(8, 128).T)
    d[prefix + "kng_row"] = np.tile(kn_g, 16).reshape(1, 1024)
    d[prefix + "knb_row"] = np.tile(kn_b, 16).reshape(1, 1024)
    d[prefix + "vng_row"] = np.tile(vn_g, 16).reshape(1, 1024)
    d[prefix + "vnb_row"] = np.tile(vn_b, 16).reshape(1, 1024)
    d[prefix + "tempcol"] = temperature.reshape(16, 1)
    bg = np.zeros((128, 512), np.float32)
    for h in range(16):
        bg[(h % 2) * 64:(h % 2) * 64 + 64, (h // 2) * 64:(h // 2) * 64 + 64] = attn_bias[h]
    d[prefix + "biasgrid"] = bg
    return {k: np.ascontiguousarray(v).astype(np.float32) for k, v in d.items()}


def build_l1attn_a_test():
    nc = bass.Bass("TRN2", target_bir_lowering=False)
    xin = nc.dram_tensor("xin", [D, T], F32, kind="ExternalInput")
    xout = nc.dram_tensor("xout", [D, T], F32, kind="ExternalOutput")
    w = {}
    for n, shp in S5_W_SHAPES:
        w[n] = nc.dram_tensor(n, list(shp), F32, kind="ExternalInput")
    stack = ExitStack()
    p = Prog(nc, stack)
    ctx = Ctx(p)
    with p.phase():
        emit_l1attn_a(ctx, xin, xout, "xin", "xout", w)
    p.finalize()
    stack.close()
    return nc


S3_W_SHAPES = [("wqT", (1024, 1024)), ("wkT", (1024, 1024)), ("wvT", (1024, 1024)),
               ("wsT", (1024, 256)), ("bq_col", (128, 8)), ("bk_col", (128, 8)),
               ("bv_row", (1, 1024)), ("bs_col", (128, 2))]


def prep_s3_weights(Wq, bq, Wk, bk, Wv, bv, Ws, bs, prefix=""):
    d = {}
    d[prefix + "wqT"] = Wq.T
    d[prefix + "wkT"] = Wk.T
    d[prefix + "wvT"] = Wv.T
    d[prefix + "wsT"] = Ws.T
    d[prefix + "bq_col"] = bq.reshape(8, 128).T
    d[prefix + "bk_col"] = bk.reshape(8, 128).T
    d[prefix + "bv_row"] = bv.reshape(1, 1024)
    d[prefix + "bs_col"] = bs.reshape(2, 128).T
    return {k: np.ascontiguousarray(v).astype(np.float32) for k, v in d.items()}


def build_l0attn_b_test():
    nc = bass.Bass("TRN2", target_bir_lowering=False)
    xin = nc.dram_tensor("xin", [D, T], F32, kind="ExternalInput")
    xout = nc.dram_tensor("xout", [D, T], F32, kind="ExternalOutput")
    attout = nc.dram_tensor("attout", [D, T], BF16, kind="Internal")
    w = {}
    for n, shp in S3_W_SHAPES:
        w[n] = nc.dram_tensor(n, list(shp), F32, kind="ExternalInput")
    stack = ExitStack()
    p = Prog(nc, stack)
    ctx = Ctx(p)
    with p.phase():
        emit_l0attn_b(ctx, xin, xout, "xin", "xout", w, attout)
    p.finalize()
    stack.close()
    return nc


S7_W_SHAPES = [("wqkvT", (1024, 3072)), ("woT", (1024, 1024)),
               ("bo_col", (128, 8)), ("gbcols", (128, 16))]


def prep_s7_weights(ln_g, ln_b, w_qkv, w_out, b_out, prefix=""):
    d = {}
    d[prefix + "wqkvT"] = w_qkv.T
    d[prefix + "woT"] = w_out.T
    d[prefix + "bo_col"] = b_out.reshape(8, 128).T
    d[prefix + "gbcols"] = np.concatenate(
        [ln_g.reshape(8, 128).T, ln_b.reshape(8, 128).T], axis=1)
    return {k: np.ascontiguousarray(v).astype(np.float32) for k, v in d.items()}


def build_l1attn_b_test():
    nc = bass.Bass("TRN2", target_bir_lowering=False)
    xin = nc.dram_tensor("xin", [D, T], F32, kind="ExternalInput")
    xout = nc.dram_tensor("xout", [D, T], F32, kind="ExternalOutput")
    attout = nc.dram_tensor("attout", [D, T], BF16, kind="Internal")
    w = {}
    for n, shp in S7_W_SHAPES:
        w[n] = nc.dram_tensor(n, list(shp), F32, kind="ExternalInput")
    stack = ExitStack()
    p = Prog(nc, stack)
    ctx = Ctx(p)
    with p.phase():
        emit_l1attn_b(ctx, xin, xout, "xin", "xout", w, attout)
    p.finalize()
    stack.close()
    return nc


S1_W_SHAPES = [("winxc", (1024, 2048)), ("winz", (1024, 2048)), ("wx", (2048, 33)),
               ("wxd_cols", (128, 16)),
               ("wout", (2048, 1024)), ("convcols", (128, 64)), ("convbcol", (128, 16)),
               ("dcols", (128, 16)), ("alogcol", (16, 1))]


def prep_s1_weights(W_in, conv_w, conv_b, W_x, A_log, D_, W_out, prefix=""):
    d = {}
    d[prefix + "winxc"] = W_in[:, 0:2048]
    d[prefix + "winz"] = W_in[:, 2048:4096]
    d[prefix + "wx"] = W_x
    d[prefix + "wxd_cols"] = W_x[:, 0].reshape(16, 128).T
    d[prefix + "wout"] = W_out
    d[prefix + "convcols"] = conv_w.reshape(16, 128, 4).transpose(1, 0, 2).reshape(128, 64)
    d[prefix + "convbcol"] = conv_b.reshape(16, 128).T
    d[prefix + "dcols"] = D_.reshape(16, 128).T
    d[prefix + "alogcol"] = A_log.reshape(16, 1)
    return {k: np.ascontiguousarray(v).astype(np.float32) for k, v in d.items()}


def build_mamba_test():
    nc = bass.Bass("TRN2", target_bir_lowering=False)
    xin = nc.dram_tensor("xin", [D, T], F32, kind="ExternalInput")
    xout = nc.dram_tensor("xout", [D, T], F32, kind="ExternalOutput")
    w = {}
    for n, shp in S1_W_SHAPES:
        w[n] = nc.dram_tensor(n, list(shp), F32, kind="ExternalInput")
    stack = ExitStack()
    p = Prog(nc, stack)
    ctx = Ctx(p)
    with p.phase():
        emit_mamba(ctx, xin, xout, "xin", "xout", w)
    p.finalize()
    stack.close()
    return nc


def build_l1ffn_test():
    nc = bass.Bass("TRN2", target_bir_lowering=False)
    xin = nc.dram_tensor("xin", [D, T], F32, kind="ExternalInput")
    xout = nc.dram_tensor("xout", [D, T], F32, kind="ExternalOutput")
    eselT = nc.dram_tensor("eselT", [D, 64], F32, kind="ExternalInput")
    keysc = nc.dram_tensor("keysc", [D, 8192], F32, kind="ExternalInput")
    valsc = nc.dram_tensor("valsc", [8192, D], F32, kind="ExternalInput")
    stack = ExitStack()
    p = Prog(nc, stack)
    ctx = Ctx(p)
    with p.phase():
        emit_l1ffn(ctx, xin, xout, "xin", "xout", eselT, keysc, valsc, "A")
    p.finalize()
    stack.close()
    return nc



S2_CONSTS = ("c_ones_col", "c_ones_colf", "c_ones_row", "c_id_f32", "c_oh8")
S1_CONSTS = ("c_ones_col", "c_ones_colf", "c_ones_row", "c_id_f32", "c_id_bf16",
             "c_oh16", "c_ones128", "c_utri", "c_ltris", "c_causmask")
S3_CONSTS = ("c_ones_col", "c_ones_colf", "c_ones_row", "c_id_f32", "c_erep2", "c_hmask")
S5_CONSTS = ("c_ones_col", "c_ones_colf", "c_ones_row", "c_id_f32", "c_tgmask", "c_tgsel")
S6_CONSTS = ("c_ones_col", "c_ones_colf", "c_ones_row", "c_id_f32", "c_oh64")
S7_CONSTS = ("c_ones_col", "c_ones_colf", "c_ones_row", "c_id_f32", "c_hmask",
             "c_swap32", "c_ropec4", "c_ropes4")


def _build_s1():
    return build_mamba_test(), S1_CONSTS


def _build_s2():
    return build_l0ffn_test(), S2_CONSTS


def _build_s3():
    return build_l0attn_b_test(), S3_CONSTS


def _build_s5():
    return build_l1attn_a_test(), S5_CONSTS


def _build_s6():
    return build_l1ffn_test(), S6_CONSTS


def _build_s7():
    return build_l1attn_b_test(), S7_CONSTS


_PROG_CACHE = {}


def _get_prog(name):
    if name not in _PROG_CACHE:
        _PROG_CACHE[name] = {"s1": _build_s1, "s2": _build_s2, "s3": _build_s3,
                             "s5": _build_s5, "s6": _build_s6, "s7": _build_s7}[name]()
    return _PROG_CACHE[name]


FUSED = True
_FUSED_CACHE = []


def build_fused():
    nc = bass.Bass("TRN2", target_bir_lowering=False)
    xin = nc.dram_tensor("xin", [D, T], F32, kind="ExternalInput")
    xout = nc.dram_tensor("xout", [D, T], F32, kind="ExternalOutput")
    xm = nc.dram_tensor("xm", [D, T], F32, kind="Internal")
    att3 = nc.dram_tensor("att3", [D, T], BF16, kind="Internal")
    att7 = nc.dram_tensor("att7", [D, T], BF16, kind="Internal")
    me = nc.dram_tensor("me", [1024, 8192], BF16, kind="Internal")
    w1 = {n: nc.dram_tensor(n, list(s), F32, kind="ExternalInput") for n, s in S1_W_SHAPES}
    win = nc.dram_tensor("w_in", [8, 4096, 1024], F32, kind="ExternalInput")
    wotT = nc.dram_tensor("w_outT", [8, 4096, 1024], F32, kind="ExternalInput")
    wgT = nc.dram_tensor("w_gateT", [D, 8], F32, kind="ExternalInput")
    w3 = {n: nc.dram_tensor(n, list(s), F32, kind="ExternalInput") for n, s in S3_W_SHAPES}
    w5 = {n: nc.dram_tensor(n, list(s), F32, kind="ExternalInput") for n, s in S5_W_SHAPES}
    eselT = nc.dram_tensor("eselT", [D, 64], F32, kind="ExternalInput")
    keysc = nc.dram_tensor("keysc", [D, 8192], F32, kind="ExternalInput")
    valsc = nc.dram_tensor("valsc", [8192, D], F32, kind="ExternalInput")
    w7 = {n: nc.dram_tensor(n, list(s), F32, kind="ExternalInput") for n, s in S7_W_SHAPES}
    stack = ExitStack()
    p = Prog(nc, stack)
    ctx = Ctx(p)
    wg_sb = p.sbuf("WGsb", [128, NPT * 8], BF16)
    p.dma("qgps", wg_sb[:, :].rearrange("p (pt e8) -> p pt e8", e8=8),
          wgT.ap().rearrange("(pt p) e8 -> p pt e8", p=128), writes=[rr("WGsb")])
    with p.phase():
        emit_mamba(ctx, xin, xm, "xin", "xm", w1)
    keyscb = nc.dram_tensor("keyscb", [D, 8192], BF16, kind="Internal")
    valscb = nc.dram_tensor("valscb", [8192, D], BF16, kind="Internal")
    with p.phase():
        emit_me_build(ctx, win, wotT, me)
        emit_precast(ctx, keysc, keyscb, D, 8192, "K")
        emit_precast(ctx, valsc, valscb, 8192, D, "V")
    with p.phase():
        emit_l0ffn(ctx, xm, xm, "xm", "xm", me, "A", wg_sb)
    with p.phase():
        emit_l0attn_b(ctx, xm, xm, "xm", "xm", w3, att3)
    with p.phase():
        emit_l0ffn(ctx, xm, xm, "xm", "xm", me, "B", wg_sb)
    with p.phase():
        emit_l1attn_a(ctx, xm, xm, "xm", "xm", w5)
    with p.phase():
        emit_l1ffn(ctx, xm, xm, "xm", "xm", eselT, keyscb, valscb, "C")
    with p.phase():
        emit_l1attn_b(ctx, xm, xm, "xm", "xm", w7, att7)
    with p.phase():
        emit_l1ffn(ctx, xm, xout, "xm", "xout", eselT, keyscb, valscb, "D")
    p.finalize()
    stack.close()
    return nc


ALL_CONSTS = ("c_ones_col", "c_ones_colf", "c_ones_row", "c_id_f32", "c_id_bf16",
              "c_oh8", "c_oh16", "c_oh64", "c_ones128", "c_utri", "c_ltris",
              "c_causmask", "c_erep2", "c_hmask", "c_tgmask", "c_tgsel",
              "c_swap32", "c_ropec4", "c_ropes4")


def kernel(**inputs):
    consts = build_consts()
    w1 = prep_s1_weights(inputs["L0_attn_a_W_in"], inputs["L0_attn_a_conv_w"],
                         inputs["L0_attn_a_conv_b"], inputs["L0_attn_a_W_x"],
                         inputs["L0_attn_a_A_log"], inputs["L0_attn_a_D"],
                         inputs["L0_attn_a_W_out"])
    w2 = {"w_in": np.ascontiguousarray(inputs["L0_ffn_w_in"], np.float32),
          "w_outT": np.ascontiguousarray(np.transpose(inputs["L0_ffn_w_out"], (0, 2, 1)).astype(np.float32)),
          "w_gateT": np.ascontiguousarray(np.asarray(inputs["L0_ffn_w_gate"]).T.astype(np.float32))}
    w3 = prep_s3_weights(inputs["L0_attn_b_Wq"], inputs["L0_attn_b_bq"],
                         inputs["L0_attn_b_Wk"], inputs["L0_attn_b_bk"],
                         inputs["L0_attn_b_Wv"], inputs["L0_attn_b_bv"],
                         inputs["L0_attn_b_Ws"], inputs["L0_attn_b_bs"])
    w5 = prep_s5_weights(inputs["L1_attn_a_qkv_w"], inputs["L1_attn_a_qkv_b"],
                         inputs["L1_attn_a_kn_g"], inputs["L1_attn_a_kn_b"],
                         inputs["L1_attn_a_vn_g"], inputs["L1_attn_a_vn_b"],
                         inputs["L1_attn_a_temperature"], inputs["L1_attn_a_attn_bias"],
                         inputs["L1_attn_a_p1_w"], inputs["L1_attn_a_p1_b"],
                         inputs["L1_attn_a_p2_w"], inputs["L1_attn_a_p2_b"])
    keys = np.asarray(inputs["L1_ffn_keys"], np.float32)
    vals = np.asarray(inputs["L1_ffn_values"], np.float32)
    w6 = {"eselT": np.ascontiguousarray(np.asarray(inputs["L1_ffn_expert_sel"], np.float32).T),
          "keysc": np.ascontiguousarray(keys.transpose(1, 0, 2).reshape(D, 8192)),
          "valsc": np.ascontiguousarray(vals.reshape(8192, D))}
    w7 = prep_s7_weights(inputs["L1_attn_b_ln_g"], inputs["L1_attn_b_ln_b"],
                         inputs["L1_attn_b_w_qkv"], inputs["L1_attn_b_w_out"],
                         inputs["L1_attn_b_b_out"])

    x = np.asarray(inputs["L0_attn_a_x"], np.float32)
    xs = [np.ascontiguousarray(x[c].T) for c in range(8)]

    if FUSED:
        if not _FUSED_CACHE:
            _FUSED_CACHE.append(build_fused())
        nc = _FUSED_CACHE[0]
        wall = {}
        for wd in (w1, w2, w3, w5, w6, w7):
            wall.update(wd)
        for n in ALL_CONSTS:
            wall[n] = consts[n]
        in_maps = []
        for c in range(8):
            m = dict(wall)
            m["xin"] = xs[c]
            in_maps.append(m)
        res = run_bass_kernel_spmd(nc, in_maps, core_ids=list(range(8)))
        xs = [res.results[c]["xout"] for c in range(8)]
        return np.stack([xs[c].T for c in range(8)]).astype(np.float32)

    seq = [("s1", w1), ("s2", w2), ("s3", w3), ("s2", w2),
           ("s5", w5), ("s6", w6), ("s7", w7), ("s6", w6)]
    for name, wd in seq:
        nc, cns = _get_prog(name)
        in_maps = []
        for c in range(8):
            m = dict(wd)
            m["xin"] = xs[c]
            for n in cns:
                m[n] = consts[n]
            in_maps.append(m)
        res = run_bass_kernel_spmd(nc, in_maps, core_ids=list(range(8)))
        xs = [np.ascontiguousarray(res.results[c]["xout"]) for c in range(8)]
    out = np.stack([xs[c].T for c in range(8)]).astype(np.float32)
    return out





# version 09
# speedup vs baseline: 1.0504x; 1.0504x over previous
import numpy as np
from contextlib import ExitStack

import concourse.bass as bass
import concourse.mybir as mybir
from concourse.bass_utils import run_bass_kernel_spmd

F32 = mybir.dt.float32
BF16 = mybir.dt.bfloat16
AF = mybir.ActivationFunctionType
ALU = mybir.AluOpType
AX = mybir.AxisListType

COMPUTE = ("tensor", "vector", "scalar", "gpsimd")
QUEUES = ("qsync", "qgps", "qscal")
Q_ENG = {"qsync": "sync", "qgps": "gpsimd", "qscal": "scalar"}
STREAMS = ("tensor", "vector", "scalar", "gpsimd", "sync")

BIG = 1 << 50

MAMBA_FP32D = False
ATTN_PAIRED = True
KVBF = True

T = 4096
D = 1024
NPT = D // 128
SLAB = 512
NSLAB = T // SLAB


def rr(name, lo=0, hi=BIG):
    return (name, lo, hi)


class Prog:

    def __init__(self, nc, stack):
        self.nc = nc
        self.stack = stack
        self.streams = {e: [] for e in STREAMS}
        self.sems = {}
        for s in COMPUTE + QUEUES:
            self.sems[s] = stack.enter_context(nc.semaphore(name="sem_" + s))
        self.tickets = {s: 0 for s in COMPUTE + QUEUES}
        self.writes = {}
        self.reads = {}
        self.water = {e: {} for e in STREAMS}
        self.nsteps = 0
        self.sim_safe = False

    def _collect(self, keys_r, keys_w):
        need = {}
        for (n, lo, hi) in keys_r:
            for (l2, h2, s, v) in self.writes.get(n, ()):
                if l2 < hi and lo < h2 and v > need.get(s, 0):
                    need[s] = v
        for (n, lo, hi) in keys_w:
            for (l2, h2, s, v) in self.writes.get(n, ()):
                if l2 < hi and lo < h2 and v > need.get(s, 0):
                    need[s] = v
            for (l2, h2, s, v) in self.reads.get(n, ()):
                if l2 < hi and lo < h2 and v > need.get(s, 0):
                    need[s] = v
        return need

    def _emit_waits(self, stream, own_kind, need):
        wm = self.water[stream]
        if self.sim_safe:
            for s in list(need):
                if s in QUEUES:
                    need[s] = 16 * self.tickets[s]
        for s, v in sorted(need.items()):
            if s == own_kind and s == "tensor":
                continue
            if wm.get(s, 0) >= v:
                continue
            sem = self.sems[s]
            self.streams[stream].append(lambda e, sem=sem, v=v: e.wait_ge(sem, v))
            wm[s] = v

    def _record(self, keys_r, keys_w, kind, val):
        for (n, lo, hi) in keys_r:
            self.reads.setdefault(n, []).append((lo, hi, kind, val))
        for (n, lo, hi) in keys_w:
            wl = self.writes.setdefault(n, [])
            wl[:] = [r for r in wl if not (lo <= r[0] and r[1] <= hi)]
            wl.append((lo, hi, kind, val))
            rl = self.reads.get(n)
            if rl:
                rl[:] = [r for r in rl if not (lo <= r[0] and r[1] <= hi)]

    def op(self, eng, fn, reads=(), writes=(), tag=""):
        assert eng in COMPUTE
        need = self._collect(reads, writes)
        self._emit_waits(eng, eng, need)
        self.tickets[eng] += 1
        val = self.tickets[eng]
        sem = self.sems[eng]

        def thunk(e, fn=fn, sem=sem):
            r = fn(e)
            if isinstance(r, (list, tuple)):
                r = r[-1]
            r.then_inc(sem, 1)

        self.streams[eng].append(thunk)
        self._record(reads, writes, eng, val)
        self.nsteps += 1
        return val

    def dma(self, queue, out_ap, in_ap, reads=(), writes=(), tag="", **kw):
        assert queue in QUEUES
        stream = Q_ENG[queue]
        need = self._collect(reads, writes)
        self._emit_waits(stream, queue, need)
        self.tickets[queue] += 1
        val = 16 * self.tickets[queue]
        sem = self.sems[queue]

        def thunk(e, out_ap=out_ap, in_ap=in_ap, sem=sem, kw=kw):
            e.dma_start(out_ap, in_ap, **kw).then_inc(sem, 16)

        self.streams[stream].append(thunk)
        self._record(reads, writes, queue, val)
        self.nsteps += 1
        return val

    def finalize(self):
        finals = []
        for s in COMPUTE:
            if self.tickets[s]:
                finals.append((self.sems[s], self.tickets[s]))
        for q in QUEUES:
            if self.tickets[q]:
                finals.append((self.sems[q], 16 * self.tickets[q]))

        def thunk(e, finals=finals):
            for sem, v in finals:
                e.wait_ge(sem, v)

        self.streams["sync"].append(thunk)
        with self.nc.Block() as block:
            for engname in STREAMS:
                thunks = self.streams[engname]
                if not thunks:
                    continue
                dec = getattr(block, engname)

                def make(th):
                    def _(eng):
                        for t in th:
                            t(eng)
                    return _

                dec(make(thunks))

    def sbuf(self, name, shape, dtype):
        return self.stack.enter_context(self.nc.sbuf_tensor(name, shape, dtype))

    def psum(self, name, shape, dtype):
        return self.stack.enter_context(self.nc.psum_tensor(name, shape, dtype))

    def barrier(self):
        vals = [(s, self.tickets[s]) for s in COMPUTE if self.tickets[s]]
        vals += [(q, 16 * self.tickets[q]) for q in QUEUES if self.tickets[q]]
        for stream in STREAMS:
            wm = self.water[stream]
            for kind, v in vals:
                if wm.get(kind, 0) >= v:
                    continue
                sem = self.sems[kind]
                self.streams[stream].append(lambda e, sem=sem, v=v: e.wait_ge(sem, v))
                wm[kind] = v
        self.writes.clear()
        self.reads.clear()

    def phase(self):
        prog = self

        class _Phase:
            def __enter__(ph):
                ph.st = ExitStack()
                ph.old = prog.stack
                prog.stack = ph.st
                return ph

            def __exit__(ph, *a):
                prog.stack = ph.old
                prog.barrier()
                ph.st.close()
                return False

        return _Phase()



def build_consts():
    c = {}
    c["c_id_f32"] = np.eye(128, dtype=np.float32)
    c["c_id_bf16"] = np.eye(128, dtype=np.float32)
    c["c_ones_col"] = np.ones((128, 1), np.float32)
    c["c_ones_colf"] = np.ones((128, 1), np.float32)
    c["c_ones128"] = np.ones((128, 128), np.float32)
    c["c_ones_row"] = np.ones((1, 128), np.float32)
    r = np.arange(128)
    c["c_utri"] = (r[:, None] <= r[None, :]).astype(np.float32)
    c["c_ltris"] = (r[:, None] > r[None, :]).astype(np.float32)
    c["c_causmask"] = np.where(r[:, None] <= r[None, :], 0.0, 1e30).astype(np.float32)
    oh8 = np.zeros((8, 8 * 128), np.float32)
    for e in range(8):
        oh8[e, e * 128:(e + 1) * 128] = 1.0
    c["c_oh8"] = oh8
    oh16 = np.zeros((16, 16 * 128), np.float32)
    for e in range(16):
        oh16[e, e * 128:(e + 1) * 128] = 1.0
    c["c_oh16"] = oh16
    oh64 = np.zeros((64, 64 * 128), np.float32)
    for e in range(64):
        oh64[e, e * 128:(e + 1) * 128] = 1.0
    c["c_oh64"] = oh64
    half = 32
    inv_freq = 1.0 / (10000.0 ** (np.arange(0, half, dtype=np.float64) / half))
    freqs = np.arange(T, dtype=np.float64)[:, None] * inv_freq[None, :]
    cos = np.cos(freqs).T.astype(np.float32)
    sin = np.sin(freqs).T.astype(np.float32)
    c["c_rope_cos2"] = np.concatenate([cos, cos], 0)
    c["c_rope_sin2"] = np.concatenate([sin, sin], 0)
    c["c_ropec4"] = np.tile(cos, (4, 1))
    c["c_ropes4"] = np.tile(np.concatenate([-sin, sin], 0), (2, 1))
    sw = np.zeros((128, 128), np.float32)
    for j in range(128):
        sw[j, j ^ 32] = 1.0
    c["c_swap32"] = sw
    hh = np.arange(16)
    cc = np.arange(512)
    c["c_tgmask"] = (cc[None, :] // 64 == hh[:, None] // 2).astype(np.float32)
    pp = np.arange(128)
    c["c_tgsel"] = (pp[None, :] // 64 == hh[:, None] % 2).astype(np.float32)
    e2 = np.zeros((128, 1024), np.float32)
    for g in range(2):
        for fl in range(512):
            rl = (fl // 64) * 16 + (fl % 64) // 4
            e2[rl, g * 512 + fl] = 0.1
    c["c_erep2"] = e2
    hm = np.zeros((128, 32), np.float32)
    for pf in range(4):
        for hh in range(8):
            for i in range(128):
                if hh == 2 * pf + i // 64:
                    hm[i, pf * 8 + hh] = 1.0
    c["c_hmask"] = hm
    return c


class Ctx:

    CONST_SHAPES = {
        "c_id_f32": (128, 128, F32), "c_id_bf16": (128, 128, BF16),
        "c_ones_col": (128, 1, BF16), "c_ones_colf": (128, 1, F32),
        "c_ones_row": (1, 128, F32),
        "c_utri": (128, 128, F32), "c_ltris": (128, 128, F32),
        "c_causmask": (128, 128, F32),
        "c_oh8": (8, 8 * 128, BF16), "c_oh16": (16, 16 * 128, BF16),
        "c_oh64": (64, 64 * 128, BF16),
        "c_rope_cos2": (64, T, F32), "c_rope_sin2": (64, T, F32),
        "c_tgmask": (16, 512, F32), "c_tgsel": (16, 128, F32),
        "c_erep2": (128, 1024, BF16), "c_hmask": (128, 32, BF16),
        "c_swap32": (128, 128, BF16), "c_ropec4": (128, T, BF16), "c_ropes4": (128, T, BF16),
        "c_ones128": (128, 128, F32),
    }

    def __init__(self, prog):
        self.p = prog
        self.dram = {}
        self.sb = {}
        self._uc = 0
        for n in ("c_ones_col", "c_ones_colf", "c_ones_row", "c_id_f32"):
            self.sb[n] = self.use_const(n)
        self.XS = prog.sbuf("XS", [128, NPT * SLAB], F32)
        self.XNB = prog.sbuf("XNB", [128, NPT * SLAB], BF16)
        self.TMP = prog.sbuf("TMP", [128, SLAB], F32)
        self.ROWS = prog.sbuf("ROWS", [1, 4 * SLAB], F32)
        self.COLS = prog.sbuf("COLS", [128, 12], F32)
        prog.op("vector", lambda e: e.memset(self.COLS[:, 0:1], 1e-5), writes=[rr("EPSC")])
        prog.op("vector", lambda e: e.memset(self.COLS[:, 1:2], 1.0), writes=[rr("ONEC")])
        self.PS = [prog.psum("PSB%d" % i, [128, 512], F32) for i in range(2)]

    def use_const(self, name):
        r, cdim, dt = self.CONST_SHAPES[name]
        if name not in self.dram:
            self.dram[name] = self.p.nc.dram_tensor(name, [r, cdim], F32,
                                                    kind="ExternalInput")
        self._uc += 1
        t = self.p.sbuf("sbc%d_%s" % (self._uc, name), [r, cdim], dt)
        q = "qgps" if dt == BF16 else "qsync"
        self.p.dma(q, t[:, :], self.dram[name].ap(), writes=[rr(name)])
        return t


def load_x_slab(ctx, x_dram, xkey, s, stats=True, rowbufs=None, gb=None,
                raw_only=False):
    p = ctx.p
    if rowbufs is None:
        mrow_ap = ctx.ROWS[0:1, 2 * SLAB:3 * SLAB]
        rrow_ap = ctx.ROWS[0:1, 3 * SLAB:4 * SLAB]
        mkey, rkey = rr("ROWS", 2 * SLAB, 3 * SLAB), rr("ROWS", 3 * SLAB, 4 * SLAB)
    else:
        mrow_ap, rrow_ap, mkey, rkey = rowbufs
    xa = x_dram.ap()[:, s * SLAB:(s + 1) * SLAB].rearrange("(pt p) t -> p pt t", p=128)
    xs3 = ctx.XS[:, :].rearrange("p (pt t) -> p pt t", t=SLAB)
    p.dma("qsync", xs3, xa, reads=[rr(xkey, s, s + 1)], writes=[rr("XS")])
    if raw_only:
        return
    p.op("vector", lambda e: e.tensor_copy(out=ctx.XNB[:, :], in_=ctx.XS[:, :]),
         reads=[rr("XS")], writes=[rr("XNB")])
    if stats:
        onec = ctx.sb["c_ones_col"]
        psA, psB = ctx.PS[0], ctx.PS[1]
        def mk_sum(e):
            ins = []
            for pt in range(NPT):
                ins.append(e.matmul(psA[0:1, 0:SLAB], onec[:, :],
                                    ctx.XNB[:, pt * SLAB:(pt + 1) * SLAB],
                                    start=(pt == 0), stop=(pt == NPT - 1)))
            return ins
        p.op("tensor", mk_sum, reads=[rr("XNB"), rr("c_ones_col")], writes=[rr("PSB0", 0, 1)])
        p.op("vector", lambda e: e.tensor_tensor(out=ctx.XNB[:, :],
                                                 in0=ctx.XNB[:, :], in1=ctx.XNB[:, :], op=ALU.mult),
             reads=[rr("XNB")], writes=[rr("XNB")])
        def mk_sumsq(e):
            ins = []
            for pt in range(NPT):
                ins.append(e.matmul(psB[0:1, 0:SLAB], onec[:, :],
                                    ctx.XNB[:, pt * SLAB:(pt + 1) * SLAB],
                                    start=(pt == 0), stop=(pt == NPT - 1)))
            return ins
        p.op("tensor", mk_sumsq, reads=[rr("XNB"), rr("c_ones_col")], writes=[rr("PSB1", 0, 1)])
        mrow = mrow_ap
        rrow = rrow_ap
        vrow = ctx.ROWS[0:1, 0:SLAB]
        n2row = ctx.ROWS[0:1, SLAB:2 * SLAB]
        p.op("vector", lambda e: e.tensor_scalar(out=mrow, in0=psA[0:1, 0:SLAB],
                                                 scalar1=1.0 / D, scalar2=None, op0=ALU.mult),
             reads=[rr("PSB0", 0, 1)], writes=[mkey])
        p.op("vector", lambda e: e.tensor_scalar(out=vrow, in0=psB[0:1, 0:SLAB],
                                                 scalar1=1.0 / D, scalar2=None, op0=ALU.mult),
             reads=[rr("PSB1", 0, 1)], writes=[rr("ROWS", 0, SLAB)])
        p.op("vector", lambda e: e.scalar_tensor_tensor(out=n2row, in0=mrow, scalar=-1.0,
                                                        in1=mrow, op0=ALU.mult, op1=ALU.mult),
             reads=[mkey], writes=[rr("ROWS", SLAB, 2 * SLAB)])
        p.op("vector", lambda e: e.tensor_tensor(out=vrow, in0=vrow, in1=n2row, op=ALU.add),
             reads=[rr("ROWS", 0, SLAB), rr("ROWS", SLAB, 2 * SLAB)], writes=[rr("ROWS", 0, SLAB)])
        p.op("scalar", lambda e: e.activation(out=vrow, in_=vrow, func=AF.Ln,
                                              bias=ctx.COLS[0:1, 0:1]),
             reads=[rr("ROWS", 0, SLAB), rr("EPSC")], writes=[rr("ROWS", 0, SLAB)])
        p.op("scalar", lambda e: e.activation(out=rrow, in_=vrow, func=AF.Exp, scale=-0.5),
             reads=[rr("ROWS", 0, SLAB)], writes=[rkey])
    onesr = ctx.sb["c_ones_row"]
    p.op("tensor", lambda e: e.matmul(ctx.PS[0][:, :], onesr[:, :], mrow_ap, start=True, stop=True),
         reads=[mkey, rr("c_ones_row")], writes=[rr("PSB0")])
    p.op("tensor", lambda e: e.matmul(ctx.PS[1][:, :], onesr[:, :], rrow_ap, start=True, stop=True),
         reads=[rkey, rr("c_ones_row")], writes=[rr("PSB1")])
    for pt in range(NPT):
        sl = slice(pt * SLAB, (pt + 1) * SLAB)
        p.op("vector", lambda e, sl=sl: e.tensor_tensor(out=ctx.TMP[:, :], in0=ctx.XS[:, sl],
                                                        in1=ctx.PS[0][:, :], op=ALU.subtract),
             reads=[rr("XS"), rr("PSB0")], writes=[rr("TMPF")])
        p.op("vector", lambda e, sl=sl: e.tensor_tensor(out=ctx.XNB[:, sl], in0=ctx.TMP[:, :],
                                                        in1=ctx.PS[1][:, :], op=ALU.mult),
             reads=[rr("TMPF"), rr("PSB1")], writes=[rr("XNB")])
        if gb is not None:
            gbt, gbkey = gb
            p.op("vector", lambda e, sl=sl, pt=pt, gbt=gbt: e.tensor_scalar(
                out=ctx.XNB[:, sl], in0=ctx.XNB[:, sl], scalar1=gbt[:, pt:pt + 1],
                scalar2=gbt[:, 8 + pt:8 + pt + 1], op0=ALU.mult, op1=ALU.add),
                 reads=[rr("XNB"), gbkey], writes=[rr("XNB")])


def store_x_slab(ctx, x_dram, xkey, s):
    p = ctx.p
    xa = x_dram.ap()[:, s * SLAB:(s + 1) * SLAB].rearrange("(pt p) t -> p pt t", p=128)
    xs3 = ctx.XS[:, :].rearrange("p (pt t) -> p pt t", t=SLAB)
    p.dma("qscal", xa, xs3, reads=[rr("XS")], writes=[rr(xkey, s, s + 1)])



def emit_me_build(ctx, win_dram, wotT_dram, me_dram):
    p = ctx.p
    H = 4096
    WI = p.sbuf("WI", [128, (H // 128) * 1024], BF16)
    WO = p.sbuf("WO", [128, (H // 128) * 1024], BF16)
    MEV = p.sbuf("MEV", [128, 8 * 1024], BF16)
    wia = win_dram.ap().rearrange("e (ht p) k -> e p ht k", p=128)
    woa = wotT_dram.ap().rearrange("e (ht p) m -> e p ht m", p=128)
    wi3 = WI[:, :].rearrange("p (ht k) -> p ht k", k=1024)
    wo3 = WO[:, :].rearrange("p (ht m) -> p ht m", m=1024)
    PSM = [p.psum("PSM%d" % i, [128, 512], F32) for i in range(4)]
    psm_i = [0]
    for e in range(8):
        for q in range(4):
            sl = slice(q * 8, (q + 1) * 8)
            p.dma("qgps", wi3[:, sl, :], wia[e, :, sl, :], writes=[rr("WI", q, q + 1)], tag="wi%d" % e)
            p.dma("qgps", wo3[:, sl, :], woa[e, :, sl, :], writes=[rr("WO", q, q + 1)], tag="wo%d" % e)
        for kt in range(8):
            for mh in range(2):
                i = psm_i[0] % 4
                psm_i[0] += 1
                ps, pskey = PSM[i], "PSM%d" % i
                def mk(e2, kt=kt, mh=mh, ps=ps):
                    ins = []
                    for ht in range(H // 128):
                        ins.append(e2.matmul(
                            ps[:, :],
                            WI[:, ht * 1024 + kt * 128: ht * 1024 + (kt + 1) * 128],
                            WO[:, ht * 1024 + mh * 512: ht * 1024 + mh * 512 + 512],
                            start=(ht == 0), stop=(ht == H // 128 - 1)))
                    return ins
                p.op("tensor", mk, reads=[rr("WI"), rr("WO")], writes=[rr(pskey)])
                dst = MEV[:, kt * 1024 + mh * 512: kt * 1024 + mh * 512 + 512]
                p.op("scalar", lambda e2, ps=ps, dst=dst: e2.copy(out=dst, in_=ps[:, :]),
                     reads=[rr(pskey)], writes=[rr("MEV", kt * 2 + mh, kt * 2 + mh + 1)])
        p.dma("qscal", me_dram.ap()[e * 128:(e + 1) * 128, :], MEV[:, :],
              reads=[rr("MEV")], writes=[rr("me", e, e + 1)])



def emit_l0ffn(ctx, x_in, x_out, kin, kout, me_dram, uid, wg_sb):
    p = ctx.p
    GTS = p.sbuf("GTS" + uid, [128, 8], F32)
    ED = p.sbuf("ED" + uid, [128, 8], F32)
    LM = p.sbuf("LM" + uid, [128, 8], F32)
    IND = p.sbuf("IND" + uid, [128, 8], F32)
    NM1 = p.sbuf("NM1" + uid, [128, 1], F32)
    NM2 = p.sbuf("NM2" + uid, [128, 1], F32)
    RC = p.sbuf("RC" + uid, [128, 1], F32)
    E2 = p.sbuf("E2" + uid, [128, 1], F32)
    GFM = p.sbuf("GFM" + uid, [8, SLAB], BF16)
    oh8 = ctx.use_const("c_oh8")
    GB = p.sbuf("GB" + uid, [128, 8 * SLAB], BF16)
    ME = p.sbuf("ME" + uid, [128, 8 * 8 * 1024], BF16)
    XGK = p.sbuf("XGK" + uid, [128, 2 * 8 * SLAB], BF16)
    PSE = [p.psum("PSE%s%d" % (uid, i), [128, 512], F32) for i in range(6)]
    ACC = PSE + [ctx.PS[0], ctx.PS[1]]
    ACCK = ["PSE%s%d" % (uid, i) for i in range(6)] + ["PSB0", "PSB1"]
    mea = me_dram.ap().rearrange("(e p) c -> p e c", p=128)
    me3 = ME[:, :].rearrange("p (e c) -> p e c", c=8192)
    p.dma("qsync", me3, mea, reads=[rr("me", 0, 8)], writes=[rr("MEsb" + uid)])

    for s in range(NSLAB):
        load_x_slab(ctx, x_in, kin, s, stats=True)
        psL = PSE[0]
        psT = PSE[1]
        for tt in range(4):
            def mk_log(e, s=s, tt=tt, psL=psL):
                ins = []
                for pt in range(NPT):
                    ins.append(e.matmul(
                        psL[0:128, 0:8],
                        ctx.XNB[:, pt * SLAB + tt * 128: pt * SLAB + tt * 128 + 128],
                        wg_sb[:, pt * 8:(pt + 1) * 8],
                        start=(pt == 0), stop=(pt == NPT - 1)))
                return ins
            p.op("tensor", mk_log, reads=[rr("XNB"), rr("WGsb")], writes=[rr("PSE" + uid + "0")])
            p.op("vector", lambda e, psL=psL: e.tensor_reduce(out=NM1[:, :], in_=psL[0:128, 0:8],
                                                              axis=AX.X, op=ALU.max, negate=True),
                 reads=[rr("PSE" + uid + "0")], writes=[rr("NM1" + uid)])
            p.op("scalar", lambda e, psL=psL: e.activation(out=ED[:, :], in_=psL[0:128, 0:8],
                                                           func=AF.Exp, bias=NM1[:, :]),
                 reads=[rr("PSE" + uid + "0"), rr("NM1" + uid)], writes=[rr("ED" + uid)])
            p.op("vector", lambda e, psL=psL: e.tensor_scalar(out=IND[:, :], in0=psL[0:128, 0:8],
                                                              scalar1=NM1[:, :], scalar2=0.0,
                                                              op0=ALU.add, op1=ALU.is_ge),
                 reads=[rr("PSE" + uid + "0"), rr("NM1" + uid)], writes=[rr("IND" + uid)])
            p.op("vector", lambda e, psL=psL: e.scalar_tensor_tensor(out=LM[:, :], in0=IND[:, :],
                                                                     scalar=-1e30, in1=psL[0:128, 0:8],
                                                                     op0=ALU.mult, op1=ALU.add),
                 reads=[rr("IND" + uid), rr("PSE" + uid + "0")], writes=[rr("LM" + uid)])
            p.op("vector", lambda e: e.tensor_reduce(out=NM2[:, :], in_=LM[:, :],
                                                     axis=AX.X, op=ALU.max, negate=True),
                 reads=[rr("LM" + uid)], writes=[rr("NM2" + uid)])
            p.op("scalar", lambda e: e.activation(out=E2[:, :], in_=NM2[:, :],
                                                  func=AF.Exp, scale=-1.0, bias=NM1[:, :]),
                 reads=[rr("NM2" + uid), rr("NM1" + uid)], writes=[rr("E2" + uid)])
            p.op("vector", lambda e: e.tensor_scalar(out=E2[:, :], in0=E2[:, :],
                                                     scalar1=1.0, scalar2=None, op0=ALU.add),
                 reads=[rr("E2" + uid)], writes=[rr("E2" + uid)])
            p.op("vector", lambda e: e.reciprocal(out=RC[:, :], in_=E2[:, :]),
                 reads=[rr("E2" + uid)], writes=[rr("RC" + uid)])
            p.op("vector", lambda e, psL=psL: e.tensor_scalar(out=IND[:, :], in0=psL[0:128, 0:8],
                                                              scalar1=NM2[:, :], scalar2=0.0,
                                                              op0=ALU.add, op1=ALU.is_ge),
                 reads=[rr("PSE" + uid + "0"), rr("NM2" + uid)], writes=[rr("IND" + uid)])
            p.op("vector", lambda e: e.scalar_tensor_tensor(out=GTS[:, :], in0=ED[:, :],
                                                            scalar=RC[:, :], in1=IND[:, :],
                                                            op0=ALU.mult, op1=ALU.mult),
                 reads=[rr("ED" + uid), rr("RC" + uid), rr("IND" + uid)], writes=[rr("GTS" + uid)])
            p.op("tensor", lambda e, psT=psT: e.transpose(psT[0:8, 0:128], GTS[:, :],
                                                          ctx.sb["c_id_f32"][:, :]),
                 reads=[rr("GTS" + uid), rr("c_id_f32")], writes=[rr("PSE" + uid + "1")])
            dst = GFM[0:8, tt * 128:(tt + 1) * 128]
            p.op("vector", lambda e, psT=psT, dst=dst: e.tensor_copy(out=dst, in_=psT[0:8, 0:128]),
                 reads=[rr("PSE" + uid + "1")], writes=[rr("GFM" + uid, tt, tt + 1)])
        for e8 in range(8):
            p.op("tensor", lambda e, e8=e8: e.matmul(
                psL[:, :], oh8[:, e8 * 128:(e8 + 1) * 128],
                GFM[0:8, :], start=True, stop=True),
                 reads=[rr("GFM" + uid, 0, 4), rr("c_oh8")], writes=[rr("PSE" + uid + "0")])
            p.op("scalar", lambda e, e8=e8: e.copy(out=GB[:, e8 * SLAB:(e8 + 1) * SLAB],
                                                   in_=psL[:, :]),
                 reads=[rr("PSE" + uid + "0")], writes=[rr("GBsb" + uid, e8, e8 + 1)])
        for kt in range(NPT):
            for e8 in range(8):
                slot = (kt % 2) * 8 + e8
                xg = XGK[:, slot * SLAB:(slot + 1) * SLAB]
                p.op("vector", lambda e, kt=kt, e8=e8, xg=xg: e.tensor_tensor(
                    out=xg, in0=ctx.XNB[:, kt * SLAB:(kt + 1) * SLAB],
                    in1=GB[:, e8 * SLAB:(e8 + 1) * SLAB], op=ALU.mult),
                     reads=[rr("XNB"), rr("GBsb" + uid, e8, e8 + 1)],
                     writes=[rr("XGK" + uid, slot, slot + 1)])
                first = (kt == 0 and e8 == 0)
                last = (kt == NPT - 1 and e8 == 7)
                def mk_u(e, kt=kt, e8=e8, xg=xg, first=first, last=last):
                    ins = []
                    for pt in range(NPT):
                        ins.append(e.matmul(
                            ACC[pt][:, :],
                            ME[:, (e8 * 8 + kt) * 1024 + pt * 128:(e8 * 8 + kt) * 1024 + pt * 128 + 128],
                            xg, start=first, stop=last))
                    return ins
                p.op("tensor", mk_u,
                     reads=[rr("MEsb" + uid), rr("XGK" + uid, slot, slot + 1)],
                     writes=[rr(ACCK[pt]) for pt in range(NPT)])
        for pt in range(NPT):
            p.op("vector", lambda e, pt=pt: e.tensor_tensor(
                out=ctx.XS[:, pt * SLAB:(pt + 1) * SLAB],
                in0=ctx.XS[:, pt * SLAB:(pt + 1) * SLAB], in1=ACC[pt][:, :], op=ALU.add),
                 reads=[rr(ACCK[pt]), rr("XS")], writes=[rr("XS")])
        store_x_slab(ctx, x_out, kout, s)



def emit_l1ffn(ctx, x_in, x_out, kin, kout, eselT_dram, keysc_dram, valsc_dram, uid):
    p = ctx.p
    oh64 = ctx.use_const("c_oh64")
    KQ = p.sbuf("KQ" + uid, [128, 8 * 2048], BF16)
    VQ = p.sbuf("VQ" + uid, [128, 16 * 1024], BF16)
    SCOR = p.sbuf("SCOR" + uid, [128, 16 * SLAB], BF16)
    OUTA = p.sbuf("OUTA" + uid, [128, 8 * 1024], F32)
    GATES = p.sbuf("GATES" + uid, [64, 1024], BF16)
    ESEL = p.sbuf("ESEL" + uid, [128, 8 * 64], F32)
    RSC = p.sbuf("RSC" + uid, [128, 2 * SLAB], BF16)
    LBUF = p.sbuf("LBUF" + uid, [128, 64], F32)
    LOG0 = p.sbuf("LOG0" + uid, [128, 64], F32)
    CSBC = p.sbuf("CSBC" + uid, [128, 64], F32)
    MRCOL = p.sbuf("MRCOL" + uid, [128, 2], F32)
    SIG = p.sbuf("SIG" + uid, [128, 64], F32)
    IND4 = p.sbuf("IND4" + uid, [128, 64], F32)
    GTF = p.sbuf("GTF" + uid, [128, 64], F32)
    NM = p.sbuf("NM" + uid, [128, 4], F32)
    M2 = p.sbuf("M2" + uid, [1, 1024], F32)
    R2 = p.sbuf("R2" + uid, [1, 1024], F32)
    XNT = p.sbuf("XNT" + uid, [128, 2 * NPT * SLAB], BF16)
    PH = p.psum("PH" + uid, [128, 512], F32)
    PG = p.psum("PG" + uid, [128, 512], F32)
    PT = p.psum("PT" + uid, [128, 512], F32)
    PO = p.psum("PO" + uid, [128, 512], F32)
    kPH, kPG, kPT, kPO = ("PH" + uid), ("PG" + uid), ("PT" + uid), ("PO" + uid)
    p.dma("qsync", ESEL[:, :].rearrange("p (pt e) -> p pt e", e=64),
          eselT_dram.ap().rearrange("(pt p) e -> p pt e", p=128), writes=[rr("ESEL" + uid)])
    def mk_cs(e):
        ins = []
        for pt in range(NPT):
            ins.append(e.matmul(PT[0:1, 0:64], ctx.sb["c_ones_colf"][:, :],
                                ESEL[:, pt * 64:(pt + 1) * 64],
                                start=(pt == 0), stop=(pt == NPT - 1)))
        return ins
    p.op("tensor", mk_cs, reads=[rr("ESEL" + uid), rr("c_ones_colf")], writes=[rr(kPT)])
    p.op("vector", lambda e: e.tensor_copy(out=ctx.ROWS[0:1, 0:64], in_=PT[0:1, 0:64]),
         reads=[rr(kPT)], writes=[rr("ROWS", 0, 64)])
    p.op("tensor", lambda e: e.matmul(PG[:, 0:64], ctx.sb["c_ones_row"][:, :],
                                      ctx.ROWS[0:1, 0:64], start=True, stop=True),
         reads=[rr("ROWS", 0, 64), rr("c_ones_row")], writes=[rr(kPG)])
    p.op("vector", lambda e: e.tensor_copy(out=CSBC[:, :], in_=PG[:, 0:64]),
         reads=[rr(kPG)], writes=[rr("CSBC" + uid)])
    kq3 = KQ[:, :].rearrange("p (kt c) -> p kt c", c=2048)
    vq3 = VQ[:, :].rearrange("p (kt m) -> p kt m", m=1024)
    keysa = keysc_dram.ap().rearrange("(kt p) c -> p kt c", p=128)
    valsa = valsc_dram.ap().rearrange("(kt p) m -> p kt m", p=128)

    for tsup in range(4):
        for keq in range(4):
            kvq = "qsync" if KVBF else "qgps"
            p.dma(kvq, kq3, keysa[:, :, keq * 2048:(keq + 1) * 2048],
                  writes=[rr("KQ" + uid)], tag="kq")
            p.dma(kvq, vq3, valsa[:, keq * 16:(keq + 1) * 16, :],
                  writes=[rr("VQ" + uid)], tag="vq")
            for tql in range(2):
                s = tsup * 2 + tql
                rows = (M2[0:1, tql * SLAB:(tql + 1) * SLAB],
                        R2[0:1, tql * SLAB:(tql + 1) * SLAB],
                        rr("M2" + uid, tql, tql + 1), rr("R2" + uid, tql, tql + 1))
                if keq == 0:
                    load_x_slab(ctx, x_in, kin, s, stats=True, rowbufs=rows)
                    p.op("vector", lambda e, tql=tql: e.tensor_copy(
                        out=XNT[:, tql * NPT * SLAB:(tql + 1) * NPT * SLAB],
                        in_=ctx.XNB[:, :]),
                         reads=[rr("XNB")], writes=[rr("XNT" + uid, tql, tql + 1)])
                elif keq == 3:
                    load_x_slab(ctx, x_in, kin, s, raw_only=True)
                if keq == 0:
                    for tt in range(4):
                        def mk_log(e, tt=tt):
                            ins = []
                            for pt in range(NPT):
                                ins.append(e.matmul(
                                    PH[0:128, 0:64],
                                    ctx.XS[:, pt * SLAB + tt * 128: pt * SLAB + tt * 128 + 128],
                                    ESEL[:, pt * 64:(pt + 1) * 64],
                                    start=(pt == 0), stop=(pt == NPT - 1)))
                            return ins
                        p.op("tensor", mk_log, reads=[rr("XS"), rr("ESEL" + uid)], writes=[rr(kPH)])
                        mr = rows[0][0:1, tt * 128:(tt + 1) * 128]
                        rs = rows[1][0:1, tt * 128:(tt + 1) * 128]
                        p.op("tensor", lambda e, mr=mr: e.transpose(PT[0:128, 0:1], mr,
                                                                    ctx.sb["c_id_f32"][0:1, 0:1]),
                             reads=[rows[2], rr("c_id_f32")], writes=[rr(kPT)])
                        p.op("vector", lambda e: e.tensor_scalar(out=MRCOL[:, 0:1], in0=PT[0:128, 0:1],
                                                                 scalar1=-1.0, scalar2=None, op0=ALU.mult),
                             reads=[rr(kPT)], writes=[rr("MRC" + uid, 0, 1)])
                        p.op("tensor", lambda e, rs=rs: e.transpose(PT[0:128, 0:1], rs,
                                                                    ctx.sb["c_id_f32"][0:1, 0:1]),
                             reads=[rows[3], rr("c_id_f32")], writes=[rr(kPT)])
                        p.op("vector", lambda e: e.tensor_copy(out=MRCOL[:, 1:2], in_=PT[0:128, 0:1]),
                             reads=[rr(kPT)], writes=[rr("MRC" + uid, 1, 2)])
                        p.op("vector", lambda e: e.scalar_tensor_tensor(
                            out=LOG0[:, :], in0=CSBC[:, :], scalar=MRCOL[:, 0:1],
                            in1=PH[0:128, 0:64], op0=ALU.mult, op1=ALU.add),
                             reads=[rr("CSBC" + uid), rr("MRC" + uid, 0, 1), rr(kPH)],
                             writes=[rr("LOG0" + uid)])
                        p.op("vector", lambda e: e.tensor_copy(out=LBUF[:, :], in_=LOG0[:, :]),
                             reads=[rr("LOG0" + uid)], writes=[rr("LBUF" + uid)])
                        for it in range(4):
                            nm = NM[:, it:it + 1]
                            p.op("vector", lambda e, nm=nm: e.tensor_reduce(
                                out=nm, in_=LBUF[:, :], axis=AX.X, op=ALU.max, negate=True),
                                 reads=[rr("LBUF" + uid)], writes=[rr("NM" + uid, it, it + 1)])
                            if it < 3:
                                p.op("vector", lambda e, nm=nm: e.tensor_scalar(
                                    out=IND4[:, :], in0=LBUF[:, :], scalar1=nm, scalar2=0.0,
                                    op0=ALU.add, op1=ALU.is_ge),
                                     reads=[rr("LBUF" + uid), rr("NM" + uid, it, it + 1)],
                                     writes=[rr("IND4" + uid)])
                                p.op("vector", lambda e: e.scalar_tensor_tensor(
                                    out=LBUF[:, :], in0=IND4[:, :], scalar=-1e30, in1=LBUF[:, :],
                                    op0=ALU.mult, op1=ALU.add),
                                     reads=[rr("IND4" + uid), rr("LBUF" + uid)],
                                     writes=[rr("LBUF" + uid)])
                        p.op("scalar", lambda e: e.activation(out=SIG[:, :], in_=LOG0[:, :],
                                                              func=AF.Sigmoid, scale=MRCOL[:, 1:2]),
                             reads=[rr("LOG0" + uid), rr("MRC" + uid, 1, 2)], writes=[rr("SIG" + uid)])
                        p.op("vector", lambda e: e.tensor_scalar(
                            out=IND4[:, :], in0=LOG0[:, :], scalar1=NM[:, 3:4], scalar2=0.0,
                            op0=ALU.add, op1=ALU.is_ge),
                             reads=[rr("LOG0" + uid), rr("NM" + uid, 3, 4)], writes=[rr("IND4" + uid)])
                        p.op("vector", lambda e: e.tensor_tensor(out=GTF[:, :], in0=SIG[:, :],
                                                                 in1=IND4[:, :], op=ALU.mult),
                             reads=[rr("SIG" + uid), rr("IND4" + uid)], writes=[rr("GTF" + uid)])
                        p.op("tensor", lambda e: e.transpose(PT[0:64, 0:128], GTF[:, :],
                                                             ctx.sb["c_id_f32"][:, :]),
                             reads=[rr("GTF" + uid), rr("c_id_f32")], writes=[rr(kPT)])
                        dst = GATES[0:64, tql * SLAB + tt * 128: tql * SLAB + tt * 128 + 128]
                        p.op("vector", lambda e, dst=dst: e.tensor_copy(out=dst, in_=PT[0:64, 0:128]),
                             reads=[rr(kPT)], writes=[rr("GATES" + uid, tql * 4 + tt, tql * 4 + tt + 1)])
                for kb in range(16):
                    kbg = keq * 16 + kb
                    PHb, kPHb = (PH, kPH) if kb % 2 == 0 else (PT, kPT)
                    rsc = RSC[:, (kb % 2) * SLAB:(kb % 2 + 1) * SLAB]
                    def mk_h(e, kb=kb, PHb=PHb, tql=tql):
                        ins = []
                        for kt in range(NPT):
                            ins.append(e.matmul(
                                PHb[:, :], KQ[:, kt * 2048 + kb * 128: kt * 2048 + kb * 128 + 128],
                                XNT[:, tql * NPT * SLAB + kt * SLAB: tql * NPT * SLAB + (kt + 1) * SLAB],
                                start=(kt == 0), stop=(kt == NPT - 1)))
                        return ins
                    p.op("tensor", mk_h, reads=[rr("KQ" + uid), rr("XNT" + uid, tql, tql + 1)],
                         writes=[rr(kPHb)])
                    p.op("tensor", lambda e, kbg=kbg, tql=tql: e.matmul(
                        PG[:, :], oh64[:, kbg * 128:(kbg + 1) * 128],
                        GATES[0:64, tql * SLAB:(tql + 1) * SLAB], start=True, stop=True),
                         reads=[rr("GATES" + uid, tql * 4, tql * 4 + 4), rr("c_oh64")],
                         writes=[rr(kPG)])
                    p.op("scalar", lambda e, PHb=PHb, rsc=rsc: e.activation(
                        out=rsc, in_=PHb[:, :], func=AF.Relu),
                         reads=[rr(kPHb)], writes=[rr("RSC" + uid, kb % 2, kb % 2 + 1)])
                    p.op("vector", lambda e, kb=kb, rsc=rsc: e.tensor_tensor(
                        out=SCOR[:, kb * SLAB:(kb + 1) * SLAB], in0=rsc, in1=PG[:, :],
                        op=ALU.mult),
                         reads=[rr("RSC" + uid, kb % 2, kb % 2 + 1), rr(kPG)],
                         writes=[rr("SCOR" + uid, kb, kb + 1)])
                for pt in range(NPT):
                    POb, kPOb = (PO, kPO) if pt % 2 == 0 else (PH, kPH)
                    def mk_o(e, pt=pt, POb=POb):
                        ins = []
                        for kt in range(16):
                            ins.append(e.matmul(
                                POb[:, :], VQ[:, kt * 1024 + pt * 128: kt * 1024 + pt * 128 + 128],
                                SCOR[:, kt * SLAB:(kt + 1) * SLAB],
                                start=(kt == 0), stop=(kt == 15)))
                        return ins
                    p.op("tensor", mk_o,
                         reads=[rr("VQ" + uid)] + [rr("SCOR" + uid, kt, kt + 1) for kt in range(16)],
                         writes=[rr(kPOb)])
                    oa = OUTA[:, pt * 1024 + tql * SLAB: pt * 1024 + (tql + 1) * SLAB]
                    okey = rr("OUTA" + uid, pt * 2 + tql, pt * 2 + tql + 1)
                    if keq == 0:
                        p.op("vector", lambda e, oa=oa, POb=POb: e.tensor_copy(out=oa, in_=POb[:, :]),
                             reads=[rr(kPOb)], writes=[okey])
                    else:
                        p.op("vector", lambda e, oa=oa, POb=POb: e.tensor_tensor(
                            out=oa, in0=oa, in1=POb[:, :], op=ALU.add),
                             reads=[rr(kPOb), okey], writes=[okey])
                    if keq == 3:
                        p.op("vector", lambda e, oa=oa, pt=pt: e.tensor_tensor(
                            out=ctx.XS[:, pt * SLAB:(pt + 1) * SLAB],
                            in0=ctx.XS[:, pt * SLAB:(pt + 1) * SLAB], in1=oa, op=ALU.add),
                             reads=[okey, rr("XS")], writes=[rr("XS")])
                if keq == 3:
                    store_x_slab(ctx, x_out, kout, s)



def emit_l1attn_a(ctx, x_in, x_out, kin, kout, w):
    p = ctx.p
    tgm = ctx.use_const("c_tgmask")
    tgs = ctx.use_const("c_tgsel")
    QKVW = p.sbuf("QKVW", [128, 8 * 3072], BF16)
    P1W = p.sbuf("P1W", [128, 8 * 1024], BF16)
    P2W = p.sbuf("P2W", [128, 8 * 1024], BF16)
    QB = p.sbuf("QBc", [128, 24], F32)
    P1B = p.sbuf("P1Bc", [128, 8], F32)
    P2B = p.sbuf("P2Bc", [128, 8], F32)
    KVBB = p.sbuf("KVBB", [128, 2048], BF16)
    GBKV = p.sbuf("GBKV", [128, 4 * 1024], BF16)
    TGRID = p.sbuf("TGRID", [128, 512], F32)
    BGRID = p.sbuf("BGRID", [128, 512], F32)
    KVSB = p.sbuf("KVSB", [128, 512], F32)
    KV2 = p.sbuf("KV2", [128, 512], BF16)
    KT = p.sbuf("KT", [128, 1024], F32)
    KNT = p.sbuf("KNT", [128, 512], BF16)
    VNT = p.sbuf("VNT", [128, 512], BF16)
    TMPB = p.sbuf("TMPB", [128, 512], F32)
    ST16 = p.sbuf("ST16", [128, 32], F32)
    QSL = p.sbuf("QSL", [128, NPT * SLAB], BF16)
    AXN = p.sbuf("AXN", [128, NPT * SLAB], BF16)
    H1 = p.sbuf("H1", [128, NPT * SLAB], BF16)
    MT = p.sbuf("MT", [1, T], F32)
    RT = p.sbuf("RT", [1, T], F32)
    PK = p.psum("PK5", [128, 512], F32)
    PKV = p.psum("PKV5", [128, 512], F32)
    PQ = p.psum("PQ5", [128, 512], F32)
    PM = p.psum("PM5", [128, 512], F32)
    kPK, kPKV, kPQ, kPM = "PK5", "PKV5", "PQ5", "PM5"

    p.dma("qgps", QKVW[:, :].rearrange("p (kt c) -> p kt c", c=3072),
          w["qkvwT"].ap().rearrange("(kt p) c -> p kt c", p=128), writes=[rr("QKVW")])
    p.dma("qgps", P1W[:, :].rearrange("p (kt c) -> p kt c", c=1024),
          w["p1wT"].ap().rearrange("(kt p) c -> p kt c", p=128), writes=[rr("P1W")])
    p.dma("qgps", P2W[:, :].rearrange("p (kt c) -> p kt c", c=1024),
          w["p2wT"].ap().rearrange("(kt p) c -> p kt c", p=128), writes=[rr("P2W")])
    p.dma("qsync", QB[:, :], w["qkvb_col"].ap(), writes=[rr("QBc")])
    p.dma("qsync", P1B[:, :], w["p1b_col"].ap(), writes=[rr("P1Bc")])
    p.dma("qsync", P2B[:, :], w["p2b_col"].ap(), writes=[rr("P2Bc")])
    p.dma("qsync", BGRID[:, :], w["biasgrid"].ap(), writes=[rr("BGRID")])
    p.dma("qsync", ctx.COLS[0:16, 8:9], w["tempcol"].ap(), writes=[rr("TEMPC")])
    p.dma("qsync", ctx.ROWS[0:1, 0:1024], w["qkvb_row"].ap()[0:1, 1024:2048],
          writes=[rr("ROWS", 0, 1024)])
    p.dma("qsync", ctx.ROWS[0:1, 1024:2048], w["qkvb_row"].ap()[0:1, 2048:3072],
          writes=[rr("ROWS", 1024, 2048)])
    for c2 in range(4):
        p.op("tensor", lambda e, c2=c2: e.matmul(PM[:, :], ctx.sb["c_ones_row"][:, :],
                                                 ctx.ROWS[0:1, c2 * 512:(c2 + 1) * 512],
                                                 start=True, stop=True),
             reads=[rr("ROWS", c2 * 512, (c2 + 1) * 512), rr("c_ones_row")], writes=[rr(kPM)])
        p.op("vector", lambda e, c2=c2: e.tensor_copy(out=KVBB[:, c2 * 512:(c2 + 1) * 512],
                                                      in_=PM[:, :]),
             reads=[rr(kPM)], writes=[rr("KVBB", c2, c2 + 1)])
    for i, nm in enumerate(("kng_row", "knb_row", "vng_row", "vnb_row")):
        p.dma("qsync", ctx.ROWS[0:1, 0:1024], w[nm].ap(), writes=[rr("ROWS", 0, 1024)])
        for c2 in range(2):
            p.op("tensor", lambda e, c2=c2: e.matmul(PM[:, :], ctx.sb["c_ones_row"][:, :],
                                                     ctx.ROWS[0:1, c2 * 512:(c2 + 1) * 512],
                                                     start=True, stop=True),
                 reads=[rr("ROWS", c2 * 512, (c2 + 1) * 512), rr("c_ones_row")],
                 writes=[rr(kPM)])
            p.op("vector", lambda e, i=i, c2=c2: e.tensor_copy(
                out=GBKV[:, i * 1024 + c2 * 512: i * 1024 + (c2 + 1) * 512], in_=PM[:, :]),
                 reads=[rr(kPM)], writes=[rr("GBKV", i * 2 + c2, i * 2 + c2 + 1)])
    p.op("vector", lambda e: e.tensor_scalar(out=ctx.TMP[0:16, 0:512], in0=tgm[:, :],
                                             scalar1=ctx.COLS[0:16, 8:9], scalar2=1.0 / T,
                                             op0=ALU.mult, op1=ALU.mult),
         reads=[rr("c_tgmask"), rr("TEMPC")], writes=[rr("TMP")])
    p.op("tensor", lambda e: e.matmul(PM[:, :], tgs[:, :], ctx.TMP[0:16, 0:512],
                                      start=True, stop=True),
         reads=[rr("TMP"), rr("c_tgsel")], writes=[rr(kPM)])
    p.op("vector", lambda e: e.tensor_copy(out=TGRID[:, :], in_=PM[:, :]),
         reads=[rr(kPM)], writes=[rr("TGRID")])
    p.op("vector", lambda e: e.memset(KVSB[:, :], 0.0), writes=[rr("KVSB")])

    for s in range(NSLAB):
        rows = (MT[0:1, s * SLAB:(s + 1) * SLAB], RT[0:1, s * SLAB:(s + 1) * SLAB],
                rr("MT", s, s + 1), rr("RT", s, s + 1))
        load_x_slab(ctx, x_in, kin, s, stats=True, rowbufs=rows)
        for tt in range(4):
            for ch in range(2):
                for which, dstT, dk in ((0, KNT, "KNT"), (1, VNT, "VNT")):
                    col0 = 1024 + which * 1024 + ch * 512
                    PKb, kPKb = (PK, kPK) if which == 0 else (PM, kPM)
                    def mk_kv(e, tt=tt, col0=col0, PKb=PKb):
                        ins = []
                        for kt in range(NPT):
                            ins.append(e.matmul(
                                PKb[:, :],
                                ctx.XNB[:, kt * SLAB + tt * 128: kt * SLAB + tt * 128 + 128],
                                QKVW[:, kt * 3072 + col0: kt * 3072 + col0 + 512],
                                start=(kt == 0), stop=(kt == NPT - 1)))
                        return ins
                    p.op("tensor", mk_kv, reads=[rr("XNB"), rr("QKVW")], writes=[rr(kPKb)])
                    kk = rr("KT", which, which + 1)
                    dst = KT[:, which * 512:(which + 1) * 512]
                    p.op("vector", lambda e, dst=dst, which=which, ch=ch, PKb=PKb: e.tensor_tensor(
                        out=dst, in0=PKb[:, :],
                        in1=KVBB[:, which * 1024 + ch * 512: which * 1024 + ch * 512 + 512],
                        op=ALU.add),
                         reads=[rr(kPKb), rr("KVBB", which * 2 + ch, which * 2 + ch + 1)],
                         writes=[kk])
                    src3 = dst.rearrange("p (h d) -> p h d", d=64)
                    p.op("vector", lambda e, src3=src3: e.tensor_reduce(
                        out=ST16[:, 0:8], in_=src3, axis=AX.X, op=ALU.add),
                         reads=[kk], writes=[rr("ST16", 0, 8)])
                    p.op("vector", lambda e: e.tensor_scalar(
                        out=ST16[:, 0:8], in0=ST16[:, 0:8], scalar1=1.0 / 64, scalar2=None,
                        op0=ALU.mult),
                         reads=[rr("ST16", 0, 8)], writes=[rr("ST16", 0, 8)])
                    p.op("vector", lambda e, dst=dst: e.tensor_tensor(
                        out=ctx.TMP[:, :], in0=dst, in1=dst, op=ALU.mult),
                         reads=[kk], writes=[rr("TMP")])
                    p.op("vector", lambda e: e.tensor_reduce(
                        out=ST16[:, 8:16], in_=ctx.TMP[:, :].rearrange("p (h d) -> p h d", d=64),
                        axis=AX.X, op=ALU.add),
                         reads=[rr("TMP")], writes=[rr("ST16", 8, 16)])
                    p.op("vector", lambda e: e.tensor_scalar(
                        out=ST16[:, 8:16], in0=ST16[:, 8:16], scalar1=1.0 / 64, scalar2=None,
                        op0=ALU.mult),
                         reads=[rr("ST16", 8, 16)], writes=[rr("ST16", 8, 16)])
                    p.op("vector", lambda e: e.scalar_tensor_tensor(
                        out=ST16[:, 16:24], in0=ST16[:, 0:8], scalar=-1.0, in1=ST16[:, 0:8],
                        op0=ALU.mult, op1=ALU.mult),
                         reads=[rr("ST16", 0, 8)], writes=[rr("ST16", 16, 24)])
                    p.op("vector", lambda e: e.tensor_tensor(
                        out=ST16[:, 16:24], in0=ST16[:, 16:24], in1=ST16[:, 8:16], op=ALU.add),
                         reads=[rr("ST16", 8, 24)], writes=[rr("ST16", 16, 24)])
                    p.op("scalar", lambda e: e.activation(
                        out=ST16[:, 16:24], in_=ST16[:, 16:24], func=AF.Ln,
                        bias=ctx.COLS[:, 0:1]),
                         reads=[rr("ST16", 16, 24), rr("EPSC")], writes=[rr("ST16", 16, 24)])
                    p.op("scalar", lambda e: e.activation(
                        out=ST16[:, 24:32], in_=ST16[:, 16:24], func=AF.Exp, scale=-0.5),
                         reads=[rr("ST16", 16, 24)], writes=[rr("ST16", 24, 32)])
                    m3 = ST16[:, 0:8][:, :, None].broadcast_to((128, 8, 64))
                    r3 = ST16[:, 24:32][:, :, None].broadcast_to((128, 8, 64))
                    p.op("vector", lambda e, src3=src3, m3=m3: e.tensor_tensor(
                        out=ctx.TMP[:, :].rearrange("p (h d) -> p h d", d=64), in0=src3,
                        in1=m3, op=ALU.subtract),
                         reads=[kk, rr("ST16", 0, 8)], writes=[rr("TMP")])
                    p.op("vector", lambda e, r3=r3, dstT=dstT: e.tensor_tensor(
                        out=dstT[:, :].rearrange("p (h d) -> p h d", d=64),
                        in0=ctx.TMP[:, :].rearrange("p (h d) -> p h d", d=64),
                        in1=r3, op=ALU.mult),
                         reads=[rr("TMP"), rr("ST16", 24, 32)], writes=[rr(dk)])
                    g0 = (0 if which == 0 else 2) * 1024 + ch * 512
                    b0 = (1 if which == 0 else 3) * 1024 + ch * 512
                    p.op("vector", lambda e, dstT=dstT, g0=g0: e.tensor_tensor(
                        out=dstT[:, :], in0=dstT[:, :], in1=GBKV[:, g0:g0 + 512], op=ALU.mult),
                         reads=[rr(dk), rr("GBKV")], writes=[rr(dk)])
                    p.op("vector", lambda e, dstT=dstT, b0=b0: e.tensor_tensor(
                        out=dstT[:, :], in0=dstT[:, :], in1=GBKV[:, b0:b0 + 512], op=ALU.add),
                         reads=[rr(dk), rr("GBKV")], writes=[rr(dk)])
                for hl in range(8):
                    h = ch * 8 + hl
                    prow = (h % 2) * 64
                    pcol = (h // 2) * 64
                    p.op("tensor", lambda e, hl=hl, prow=prow, pcol=pcol: e.matmul(
                        PKV[prow:prow + 64, pcol:pcol + 64],
                        KNT[:, hl * 64:(hl + 1) * 64],
                        VNT[:, hl * 64:(hl + 1) * 64], start=True, stop=True),
                         reads=[rr("KNT"), rr("VNT")], writes=[rr(kPKV)])
                    p.op("vector", lambda e, prow=prow, pcol=pcol: e.tensor_tensor(
                        out=KVSB[prow:prow + 64, pcol:pcol + 64],
                        in0=KVSB[prow:prow + 64, pcol:pcol + 64],
                        in1=PKV[prow:prow + 64, pcol:pcol + 64], op=ALU.add),
                         reads=[rr(kPKV), rr("KVSB")], writes=[rr("KVSB")])
    p.op("vector", lambda e: e.tensor_tensor(out=KVSB[:, :], in0=KVSB[:, :], in1=TGRID[:, :],
                                             op=ALU.mult),
         reads=[rr("KVSB"), rr("TGRID")], writes=[rr("KVSB")])
    p.op("vector", lambda e: e.tensor_tensor(out=KV2[:, :], in0=KVSB[:, :], in1=BGRID[:, :],
                                             op=ALU.add),
         reads=[rr("KVSB"), rr("BGRID")], writes=[rr("KV2")])

    for s in range(NSLAB):
        rows = (MT[0:1, s * SLAB:(s + 1) * SLAB], RT[0:1, s * SLAB:(s + 1) * SLAB],
                rr("MT", s, s + 1), rr("RT", s, s + 1))
        load_x_slab(ctx, x_in, kin, s, stats=False, rowbufs=rows)
        for pt in range(NPT):
            def mk_q(e, pt=pt):
                ins = []
                for kt in range(NPT):
                    ins.append(e.matmul(
                        PQ[:, :], QKVW[:, kt * 3072 + pt * 128: kt * 3072 + pt * 128 + 128],
                        ctx.XNB[:, kt * SLAB:(kt + 1) * SLAB],
                        start=(kt == 0), stop=(kt == NPT - 1)))
                return ins
            p.op("tensor", mk_q, reads=[rr("XNB"), rr("QKVW")], writes=[rr(kPQ)])
            p.op("scalar", lambda e, pt=pt: e.activation(
                out=QSL[:, pt * SLAB:(pt + 1) * SLAB], in_=PQ[:, :], func=AF.Identity,
                bias=QB[:, pt:pt + 1]),
                 reads=[rr(kPQ), rr("QBc")], writes=[rr("QSL", pt, pt + 1)])
        for h in range(16):
            prow = (h % 2) * 64
            pcol = (h // 2) * 64
            qsl = QSL[prow:prow + 64, (h // 2) * SLAB:(h // 2 + 1) * SLAB]
            p.op("tensor", lambda e, prow=prow, pcol=pcol, qsl=qsl: e.matmul(
                PM[prow:prow + 64, :], KV2[prow:prow + 64, pcol:pcol + 64], qsl,
                start=True, stop=True),
                 reads=[rr("KV2"), rr("QSL", h // 2, h // 2 + 1)], writes=[rr(kPM)])
            xsl = ctx.XNB[prow:prow + 64, (h // 2) * SLAB:(h // 2 + 1) * SLAB]
            asl = AXN[prow:prow + 64, (h // 2) * SLAB:(h // 2 + 1) * SLAB]
            p.op("vector", lambda e, prow=prow, xsl=xsl, asl=asl: e.tensor_tensor(
                out=asl, in0=PM[prow:prow + 64, :], in1=xsl, op=ALU.add),
                 reads=[rr(kPM), rr("XNB")], writes=[rr("AXN", h, h + 1)])
        for pt in range(NPT):
            def mk_p1(e, pt=pt):
                ins = []
                for kt in range(NPT):
                    ins.append(e.matmul(
                        PQ[:, :], P1W[:, kt * 1024 + pt * 128: kt * 1024 + pt * 128 + 128],
                        AXN[:, kt * SLAB:(kt + 1) * SLAB],
                        start=(kt == 0), stop=(kt == NPT - 1)))
                return ins
            p.op("tensor", mk_p1, reads=[rr("AXN", 0, 16), rr("P1W")], writes=[rr(kPQ)])
            p.op("scalar", lambda e, pt=pt: e.activation(
                out=H1[:, pt * SLAB:(pt + 1) * SLAB], in_=PQ[:, :], func=AF.Gelu,
                bias=P1B[:, pt:pt + 1]),
                 reads=[rr(kPQ), rr("P1Bc")], writes=[rr("H1", pt, pt + 1)])
        for pt in range(NPT):
            def mk_p2(e, pt=pt):
                ins = []
                for kt in range(NPT):
                    ins.append(e.matmul(
                        PQ[:, :], P2W[:, kt * 1024 + pt * 128: kt * 1024 + pt * 128 + 128],
                        H1[:, kt * SLAB:(kt + 1) * SLAB],
                        start=(kt == 0), stop=(kt == NPT - 1)))
                return ins
            p.op("tensor", mk_p2, reads=[rr("H1", 0, 8), rr("P2W")], writes=[rr(kPQ)])
            xsp = ctx.XS[:, pt * SLAB:(pt + 1) * SLAB]
            p.op("vector", lambda e, pt=pt, xsp=xsp: e.scalar_tensor_tensor(
                out=xsp, in0=PQ[:, :], scalar=P2B[:, pt:pt + 1], in1=xsp,
                op0=ALU.add, op1=ALU.add),
                 reads=[rr(kPQ), rr("P2Bc"), rr("XS")], writes=[rr("XS")])
            p.op("vector", lambda e, pt=pt, xsp=xsp: e.tensor_tensor(
                out=xsp, in0=xsp, in1=ctx.XNB[:, pt * SLAB:(pt + 1) * SLAB], op=ALU.add),
                 reads=[rr("XS"), rr("XNB")], writes=[rr("XS")])
        store_x_slab(ctx, x_out, kout, s)



def emit_attn_passB_slab(p, ctx, s, names, KAR, QAR, VAR, PTB, OUTS, CTXE, RL,
                         BCOLS, PSQ, PCTX, PP):
    for pf in range(4):
        banks = {}
        for hp in range(2):
            h = 2 * pf + hp
            pr = hp * 64
            PC, kPC = names["pctx"][hp]
            PBC, kPBC = names["pbc"][hp]
            banks[hp] = (h, pr, PC, kPC, PBC, kPBC)

        def emit_qk(kq, hps=(0, 1)):
            par = kq % 2
            for hp in hps:
                h, pr, PC, kPC, PBC, kPBC = banks[hp]
                PSQb, kPSQ = names["psq"][hp * 2 + par]
                def mk_qk(e, kq=kq, pr=pr, PSQb=PSQb, pf=pf, s=s):
                    return e.matmul(
                        PSQb[:, :],
                        KAR[pr:pr + 64, pf * T + kq * 128: pf * T + kq * 128 + 128],
                        QAR[pr:pr + 64, pf * T + s * SLAB: pf * T + (s + 1) * SLAB],
                        start=True, stop=True)
                p.op("tensor", mk_qk,
                     reads=[rr(names["kar"], pf * 8, pf * 8 + 8),
                            rr(names["qar"], pf * 8 + s, pf * 8 + s + 1)],
                     writes=[rr(kPSQ)])
            for hp in hps:
                h, pr, PC, kPC, PBC, kPBC = banks[hp]
                PSQb, kPSQ = names["psq"][hp * 2 + par]
                dstp = PTB[:, (hp * 2 + par) * 512:(hp * 2 + par) * 512 + 512]
                p.op("scalar", lambda e, h=h, PSQb=PSQb, dstp=dstp: e.activation(
                    out=dstp, in_=PSQb[:, :], func=AF.Exp, scale=0.125,
                    bias=BCOLS[:, h:h + 1]),
                     reads=[rr(kPSQ), rr(names["bcols"])],
                     writes=[rr(names["ptb"], hp * 2 + par, hp * 2 + par + 1)])

        def emit_pv(kq, hps=(0, 1)):
            par = kq % 2
            for hp in hps:
                h, pr, PC, kPC, PBC, kPBC = banks[hp]
                dstp = PTB[:, (hp * 2 + par) * 512:(hp * 2 + par) * 512 + 512]
                def mk_pv(e, kq=kq, h=h, PC=PC, dstp=dstp):
                    return e.matmul(
                        PC[0:65, :], VAR[:, kq * 520 + h * 65: kq * 520 + h * 65 + 65],
                        dstp, start=(kq == 0), stop=(kq == T // 128 - 1))
                p.op("tensor", mk_pv,
                     reads=[rr(names["ptb"], hp * 2 + par, hp * 2 + par + 1),
                            rr(names["varv"], 0, 32), rr(names["varone"], h, h + 1)],
                     writes=[rr(kPC)])

        nkq = T // 128
        if ATTN_PAIRED:
            for kq in range(nkq):
                emit_qk(kq)
                if kq > 0:
                    emit_pv(kq - 1)
            emit_pv(nkq - 1)
        else:
            for kq in range(nkq):
                emit_qk(kq, (0,))
                if kq > 0:
                    emit_pv(kq - 1, (0,))
            emit_pv(nkq - 1, (0,))
            for kq in range(nkq):
                emit_qk(kq, (1,))
                if kq > 0:
                    emit_pv(kq - 1, (1,))
            emit_pv(nkq - 1, (1,))
        for hp in range(2):
            h, pr, PC, kPC, PBC, kPBC = banks[hp]
            ctxe = CTXE[:, hp * 512:(hp + 1) * 512]
            rl = RL[0:1, 0:512] if RL.shape[1] < 1024 else RL[0:1, hp * 512:(hp + 1) * 512]
            rlk = rr(names["rl"], hp, hp + 1) if RL.shape[1] >= 1024 else rr(names["rl"])
            p.op("vector", lambda e, PC=PC, rl=rl: e.reciprocal(out=rl, in_=PC[64:65, 0:512]),
                 reads=[rr(kPC)], writes=[rlk])
            p.op("tensor", lambda e, PBC=PBC, rl=rl: e.matmul(
                PBC[0:64, :], ctx.sb["c_ones_row"][0:1, 0:64], rl, start=True, stop=True),
                 reads=[rlk, rr("c_ones_row")], writes=[rr(kPBC)])
            p.op("scalar", lambda e, PC=PC, ctxe=ctxe: e.copy(out=ctxe, in_=PC[0:64, 0:512]),
                 reads=[rr(kPC)], writes=[rr(names["ctxe"], hp, hp + 1)])
            dsto = OUTS[pr:pr + 64, pf * 512:(pf + 1) * 512]
            p.op("vector", lambda e, dsto=dsto, ctxe=ctxe, PBC=PBC: e.tensor_tensor(
                out=dsto, in0=ctxe, in1=PBC[0:64, :], op=ALU.mult),
                 reads=[rr(names["ctxe"], hp, hp + 1), rr(kPBC)],
                 writes=[rr(names["outs"], h, h + 1)])


def emit_l0attn_b(ctx, x_in, x_out, kin, kout, w, attout):
    p = ctx.p
    erep = ctx.use_const("c_erep2")
    hmask = ctx.use_const("c_hmask")
    WQG = p.sbuf("WQG", [128, 8 * 512], BF16)
    WKG = p.sbuf("WKG", [128, 8 * 512], BF16)
    WVG = p.sbuf("WVG", [128, 8 * 512], BF16)
    WSG = p.sbuf("WSG", [128, 8 * 128], BF16)
    BQ = p.sbuf("BQ3", [128, 8], F32)
    BK = p.sbuf("BK3", [128, 8], F32)
    BS = p.sbuf("BS3", [128, 2], F32)
    VBCG = p.sbuf("VBCG", [128, 512], BF16)
    QAR = p.sbuf("QAR", [128, 4 * T], BF16)
    KAR = p.sbuf("KAR", [128, 4 * T], BF16)
    VAR = p.sbuf("VAR", [128, 32 * 520], BF16)
    SIGG = p.sbuf("SIGG", [128, 512], BF16)
    SEXPB = p.sbuf("SEXPB", [128, 512], BF16)
    SQT = p.sbuf("SQT", [128, 4 * 512], BF16)
    PTB = p.sbuf("PTB", [128, 2 * 1024], BF16)
    OUTS = p.sbuf("OUTS", [128, 4 * 512], BF16)
    XGRP = p.sbuf("XGRP", [128, 4 * 512], F32)
    CTXE = p.sbuf("CTXE", [64, 2 * 512], BF16)
    NRM = p.sbuf("NRM", [8, 4], F32)
    BROW = p.sbuf("BROW", [1, 8], F32)
    BCOLS = p.sbuf("BCOLS", [128, 8], F32)
    RL = p.sbuf("RL", [1, 2 * 512], F32)
    PSQ = [p.psum("PSQ3%d" % i, [128, 512], F32) for i in range(4)]
    PP = p.psum("PP3", [128, 512], F32)
    PCTX = p.psum("PCTX3", [128, 512], F32)
    kPP, kPCTX = "PP3", "PCTX3"
    p.dma("qsync", BQ[:, :], w["bq_col"].ap(), writes=[rr("BQ3")])
    p.dma("qsync", BK[:, :], w["bk_col"].ap(), writes=[rr("BK3")])
    p.dma("qsync", BS[:, :], w["bs_col"].ap(), writes=[rr("BS3")])

    for g in range(2):
        fc = g * 512
        p.dma("qgps", WQG[:, :].rearrange("p (kt c) -> p kt c", c=512),
              w["wqT"].ap()[:, fc:fc + 512].rearrange("(kt p) c -> p kt c", p=128),
              writes=[rr("WQG")])
        p.dma("qgps", WKG[:, :].rearrange("p (kt c) -> p kt c", c=512),
              w["wkT"].ap()[:, fc:fc + 512].rearrange("(kt p) c -> p kt c", p=128),
              writes=[rr("WKG")])
        p.dma("qgps", WVG[:, :].rearrange("p (kt c) -> p kt c", c=512),
              w["wvT"].ap()[:, fc:fc + 512].rearrange("(kt p) c -> p kt c", p=128),
              writes=[rr("WVG")])
        p.dma("qgps", WSG[:, :].rearrange("p (kt c) -> p kt c", c=128),
              w["wsT"].ap()[:, g * 128:(g + 1) * 128].rearrange("(kt p) c -> p kt c", p=128),
              writes=[rr("WSG")])
        p.dma("qsync", ctx.ROWS[0:1, 0:512], w["bv_row"].ap()[0:1, fc:fc + 512],
              writes=[rr("ROWS", 0, 512)])
        p.op("tensor", lambda e: e.matmul(PP[:, :], ctx.sb["c_ones_row"][:, :],
                                          ctx.ROWS[0:1, 0:512], start=True, stop=True),
             reads=[rr("ROWS", 0, 512), rr("c_ones_row")], writes=[rr(kPP)])
        p.op("vector", lambda e: e.tensor_copy(out=VBCG[:, :], in_=PP[:, :]),
             reads=[rr(kPP)], writes=[rr("VBCG")])
        for hh in range(8):
            ap1 = VAR[:, :].rearrange("p (kt hc) -> p kt hc", hc=520)[:, :, hh * 65 + 64: hh * 65 + 65]
            p.op("vector", lambda e, ap1=ap1: e.memset(ap1, 1.0), writes=[rr("VARONE", hh, hh + 1)])

        for s in range(NSLAB):
            load_x_slab(ctx, x_in, kin, s, stats=True)
            def mk_sig(e):
                ins = []
                for kt in range(NPT):
                    ins.append(e.matmul(PCTX[:, :], WSG[:, kt * 128:(kt + 1) * 128],
                                        ctx.XNB[:, kt * SLAB:(kt + 1) * SLAB],
                                        start=(kt == 0), stop=(kt == NPT - 1)))
                return ins
            p.op("tensor", mk_sig, reads=[rr("XNB"), rr("WSG")], writes=[rr(kPCTX)])
            p.op("scalar", lambda e, g=g: e.activation(out=SIGG[:, :], in_=PCTX[:, :],
                                                       func=AF.Sigmoid, bias=BS[:, g:g + 1]),
                 reads=[rr(kPCTX), rr("BS3")], writes=[rr("SIGG")])
            for pf in range(4):
                def mk_q(e, pf=pf):
                    ins = []
                    for kt in range(NPT):
                        ins.append(e.matmul(
                            PP[:, :], WQG[:, kt * 512 + pf * 128: kt * 512 + pf * 128 + 128],
                            ctx.XNB[:, kt * SLAB:(kt + 1) * SLAB],
                            start=(kt == 0), stop=(kt == NPT - 1)))
                    return ins
                p.op("tensor", mk_q, reads=[rr("XNB"), rr("WQG")], writes=[rr(kPP)])
                p.op("vector", lambda e, g=g, pf=pf: e.tensor_scalar(
                    out=ctx.TMP[:, :], in0=PP[:, :], scalar1=BQ[:, g * 4 + pf: g * 4 + pf + 1],
                    scalar2=None, op0=ALU.add),
                     reads=[rr(kPP), rr("BQ3")], writes=[rr("TMP")])
                p.op("tensor", lambda e, g=g, pf=pf: e.matmul(
                    PCTX[:, :], erep[:, g * 512 + pf * 128: g * 512 + pf * 128 + 128],
                    SIGG[:, :], start=True, stop=True),
                     reads=[rr("SIGG"), rr("c_erep2")], writes=[rr(kPCTX)])
                p.op("vector", lambda e: e.tensor_scalar(out=SEXPB[:, :], in0=PCTX[:, :],
                                                         scalar1=0.95, scalar2=None, op0=ALU.add),
                     reads=[rr(kPCTX)], writes=[rr("SEXPB")])
                dq = QAR[:, pf * T + s * SLAB: pf * T + (s + 1) * SLAB]
                p.op("vector", lambda e, dq=dq: e.tensor_tensor(out=dq, in0=ctx.TMP[:, :],
                                                                in1=SEXPB[:, :], op=ALU.mult),
                     reads=[rr("TMP"), rr("SEXPB")], writes=[rr("QAR", pf * 8 + s, pf * 8 + s + 1)])
                def mk_k(e, pf=pf):
                    ins = []
                    for kt in range(NPT):
                        ins.append(e.matmul(
                            PP[:, :], WKG[:, kt * 512 + pf * 128: kt * 512 + pf * 128 + 128],
                            ctx.XNB[:, kt * SLAB:(kt + 1) * SLAB],
                            start=(kt == 0), stop=(kt == NPT - 1)))
                    return ins
                p.op("tensor", mk_k, reads=[rr("XNB"), rr("WKG")], writes=[rr(kPP)])
                dk = KAR[:, pf * T + s * SLAB: pf * T + (s + 1) * SLAB]
                p.op("vector", lambda e, g=g, pf=pf, dk=dk: e.tensor_scalar(
                    out=dk, in0=PP[:, :], scalar1=BK[:, g * 4 + pf: g * 4 + pf + 1],
                    scalar2=None, op0=ALU.add),
                     reads=[rr(kPP), rr("BK3")], writes=[rr("KAR", pf * 8 + s, pf * 8 + s + 1)])
            for tt in range(4):
                def mk_v(e, tt=tt):
                    ins = []
                    for kt in range(NPT):
                        ins.append(e.matmul(
                            PP[:, :], ctx.XNB[:, kt * SLAB + tt * 128: kt * SLAB + tt * 128 + 128],
                            WVG[:, kt * 512:(kt + 1) * 512],
                            start=(kt == 0), stop=(kt == NPT - 1)))
                    return ins
                p.op("tensor", mk_v, reads=[rr("XNB"), rr("WVG")], writes=[rr(kPP)])
                ktg = s * 4 + tt
                dv = VAR[:, ktg * 520: ktg * 520 + 520].rearrange("p (h c) -> p h c", c=65)[:, :, 0:64]
                p.op("vector", lambda e, dv=dv: e.tensor_tensor(out=dv, in0=PP[:, :],
                                                                in1=VBCG[:, :], op=ALU.add),
                     reads=[rr(kPP), rr("VBCG")], writes=[rr("VARV", ktg, ktg + 1)])
            for which, AR, kk in ((0, QAR, "QAR"), (1, KAR, "KAR")):
                for pf in range(4):
                    src = AR[:, pf * T + s * SLAB: pf * T + (s + 1) * SLAB]
                    p.op("vector", lambda e, src=src, pf=pf: e.tensor_tensor(
                        out=SQT[:, pf * 512:(pf + 1) * 512], in0=src, in1=src, op=ALU.mult),
                         reads=[rr(kk, pf * 8 + s, pf * 8 + s + 1)],
                         writes=[rr("SQT", pf, pf + 1)])
                def mk_n(e):
                    ins = []
                    for pf in range(4):
                        ins.append(e.matmul(PSQ[1][0:8, 0:512], hmask[:, pf * 8:(pf + 1) * 8],
                                            SQT[:, pf * 512:(pf + 1) * 512],
                                            start=(pf == 0), stop=(pf == 3)))
                    return ins
                p.op("tensor", mk_n, reads=[rr("SQT", 0, 4), rr("c_hmask")], writes=[rr("PSQ31")])
                p.op("vector", lambda e: e.tensor_reduce(out=NRM[0:8, 2:3], in_=PSQ[1][0:8, 0:512],
                                                         axis=AX.X, op=ALU.max),
                     reads=[rr("PSQ31")], writes=[rr("NRM", 2, 3)])
                if s == 0:
                    p.op("vector", lambda e, which=which: e.tensor_copy(
                        out=NRM[0:8, which:which + 1], in_=NRM[0:8, 2:3]),
                         reads=[rr("NRM", 2, 3)], writes=[rr("NRM", which, which + 1)])
                else:
                    p.op("vector", lambda e, which=which: e.tensor_tensor(
                        out=NRM[0:8, which:which + 1], in0=NRM[0:8, which:which + 1],
                        in1=NRM[0:8, 2:3], op=ALU.max),
                         reads=[rr("NRM", 2, 3), rr("NRM", which, which + 1)],
                         writes=[rr("NRM", which, which + 1)])
        p.op("scalar", lambda e: e.activation(out=NRM[0:8, 0:1], in_=NRM[0:8, 0:1], func=AF.Ln),
             reads=[rr("NRM", 0, 1)], writes=[rr("NRM", 0, 1)])
        p.op("scalar", lambda e: e.activation(out=NRM[0:8, 1:2], in_=NRM[0:8, 1:2], func=AF.Ln),
             reads=[rr("NRM", 1, 2)], writes=[rr("NRM", 1, 2)])
        p.op("vector", lambda e: e.tensor_tensor(out=NRM[0:8, 3:4], in0=NRM[0:8, 0:1],
                                                 in1=NRM[0:8, 1:2], op=ALU.add),
             reads=[rr("NRM", 0, 2)], writes=[rr("NRM", 3, 4)])
        p.op("scalar", lambda e: e.activation(out=NRM[0:8, 3:4], in_=NRM[0:8, 3:4],
                                              func=AF.Exp, scale=0.25),
             reads=[rr("NRM", 3, 4)], writes=[rr("NRM", 3, 4)])
        p.op("vector", lambda e: e.tensor_scalar(out=NRM[0:8, 3:4], in0=NRM[0:8, 3:4],
                                                 scalar1=-0.125, scalar2=None, op0=ALU.mult),
             reads=[rr("NRM", 3, 4)], writes=[rr("NRM", 3, 4)])
        p.op("tensor", lambda e: e.transpose(PP[0:1, 0:8], NRM[0:8, 3:4],
                                             ctx.sb["c_id_f32"][0:8, 0:8]),
             reads=[rr("NRM", 3, 4), rr("c_id_f32")], writes=[rr(kPP)])
        p.op("vector", lambda e: e.tensor_copy(out=BROW[0:1, 0:8], in_=PP[0:1, 0:8]),
             reads=[rr(kPP)], writes=[rr("BROW")])
        for hh in range(8):
            p.op("tensor", lambda e, hh=hh: e.matmul(PP[:, hh:hh + 1],
                                                     ctx.sb["c_ones_row"][:, :],
                                                     BROW[0:1, hh:hh + 1], start=True, stop=True),
                 reads=[rr("BROW"), rr("c_ones_row")], writes=[rr(kPP)])
        p.op("vector", lambda e: e.tensor_copy(out=BCOLS[:, 0:8], in_=PP[:, 0:8]),
             reads=[rr(kPP)], writes=[rr("BCOLS")])

        names3 = {"kar": "KAR", "qar": "QAR", "varv": "VARV", "varone": "VARONE",
                  "ptb": "PTB", "outs": "OUTS", "ctxe": "CTXE", "rl": "RL",
                  "bcols": "BCOLS",
                  "psq": [(PSQ[i], "PSQ3%d" % i) for i in range(4)],
                  "pctx": [(PCTX, "PCTX3"), (ctx.PS[0], "PSB0")],
                  "pbc": [(PP, "PP3"), (ctx.PS[1], "PSB1")]}
        for s in range(NSLAB):
            emit_attn_passB_slab(p, ctx, s, names3, KAR, QAR, VAR, PTB, OUTS, CTXE,
                                 RL, BCOLS, PSQ, PCTX, PP)
            ao = attout.ap()[fc:fc + 512, s * SLAB:(s + 1) * SLAB].rearrange(
                "(pt p) t -> p pt t", p=128)
            o3 = OUTS[:, :].rearrange("p (pt t) -> p pt t", t=SLAB)
            p.dma("qscal", ao, o3, reads=[rr("OUTS", 0, 8)],
                  writes=[rr("attout", g * 8 + s, g * 8 + s + 1)])

    AOS = p.sbuf("AOS", [128, NPT * SLAB], BF16)
    for s in range(NSLAB):
        xa = x_in.ap()[:, s * SLAB:(s + 1) * SLAB].rearrange("(pt p) t -> p pt t", p=128)
        xs3 = ctx.XS[:, :].rearrange("p (pt t) -> p pt t", t=SLAB)
        p.dma("qsync", xs3, xa, reads=[rr(kin, s, s + 1)], writes=[rr("XS")])
        aoa = attout.ap()[:, s * SLAB:(s + 1) * SLAB].rearrange("(pt p) t -> p pt t", p=128)
        ao3 = AOS[:, :].rearrange("p (pt t) -> p pt t", t=SLAB)
        p.dma("qsync", ao3, aoa, reads=[rr("attout", s, s + 1), rr("attout", 8 + s, 9 + s)],
              writes=[rr("AOS")])
        for pt in range(NPT):
            p.op("vector", lambda e, pt=pt: e.tensor_tensor(
                out=ctx.XS[:, pt * SLAB:(pt + 1) * SLAB],
                in0=ctx.XS[:, pt * SLAB:(pt + 1) * SLAB],
                in1=AOS[:, pt * SLAB:(pt + 1) * SLAB], op=ALU.add),
                 reads=[rr("XS"), rr("AOS")], writes=[rr("XS")])
        store_x_slab(ctx, x_out, kout, s)



def emit_l1attn_b(ctx, x_in, x_out, kin, kout, w, attout):
    p = ctx.p
    hmask = ctx.use_const("c_hmask")
    swap = ctx.use_const("c_swap32")
    ropec = ctx.use_const("c_ropec4")
    ropes = ctx.use_const("c_ropes4")
    WQG = p.sbuf("WQG7", [128, 8 * 512], BF16)
    WKG = p.sbuf("WKG7", [128, 8 * 512], BF16)
    WVG = p.sbuf("WVG7", [128, 8 * 512], BF16)
    WOG = p.sbuf("WOG7", [128, 4 * 1024], BF16)
    BO = p.sbuf("BO7", [128, 8], F32)
    GBC = p.sbuf("GBC7", [128, 16], F32)
    QAR = p.sbuf("QAR7", [128, 4 * T], BF16)
    KAR = p.sbuf("KAR7", [128, 4 * T], BF16)
    VAR = p.sbuf("VAR7", [128, 32 * 520], BF16)
    SQT = p.sbuf("SQT7", [128, 4 * 512], BF16)
    PTB = p.sbuf("PTB7", [128, 2 * 1024], BF16)
    OUTS = p.sbuf("OUTS7", [128, 4 * 512], BF16)
    AOS7 = p.sbuf("AOS7", [128, NPT * SLAB], BF16)
    CTXE = p.sbuf("CTXE7", [64, 2 * 512], BF16)
    RT1 = p.sbuf("RT17", [128, 512], BF16)
    RT2 = p.sbuf("RT27", [128, 512], BF16)
    NRM = p.sbuf("NRM7", [8, 4], F32)
    BROW = p.sbuf("BROW7", [1, 8], F32)
    BCOLS = p.sbuf("BCOLS7", [128, 8], F32)
    RL = p.sbuf("RL7", [1, 512], F32)
    PSQ = [p.psum("PSQ7%d" % i, [128, 512], F32) for i in range(4)]
    PP = p.psum("PP7", [128, 512], F32)
    PCTX = p.psum("PCTX7", [128, 512], F32)
    kPP, kPCTX = "PP7", "PCTX7"
    p.dma("qsync", BO[:, :], w["bo_col"].ap(), writes=[rr("BO7")])
    p.dma("qsync", GBC[:, :], w["gbcols"].ap(), writes=[rr("GBC7")])

    for g in range(2):
        fc = g * 512
        for which, Wt, wk in ((0, WQG, "WQG7"), (1, WKG, "WKG7"), (2, WVG, "WVG7")):
            p.dma("qgps", Wt[:, :].rearrange("p (kt c) -> p kt c", c=512),
                  w["wqkvT"].ap()[:, which * 1024 + fc: which * 1024 + fc + 512]
                  .rearrange("(kt p) c -> p kt c", p=128), writes=[rr(wk)])
        p.dma("qgps", WOG[:, :].rearrange("p (kt c) -> p kt c", c=1024),
              w["woT"].ap()[fc:fc + 512, :].rearrange("(kt p) c -> p kt c", p=128),
              writes=[rr("WOG7")])
        for hh in range(8):
            ap1 = VAR[:, :].rearrange("p (kt hc) -> p kt hc", hc=520)[:, :, hh * 65 + 64: hh * 65 + 65]
            p.op("vector", lambda e, ap1=ap1: e.memset(ap1, 1.0), writes=[rr("VARONE7", hh, hh + 1)])

        for s in range(NSLAB):
            load_x_slab(ctx, x_in, kin, s, stats=True, gb=(GBC, rr("GBC7")))
            for which, Wt, wk, AR, ak in ((0, WQG, "WQG7", QAR, "QAR7"),
                                          (1, WKG, "WKG7", KAR, "KAR7")):
                for pf in range(4):
                    def mk_p(e, pf=pf, Wt=Wt):
                        ins = []
                        for kt in range(NPT):
                            ins.append(e.matmul(
                                PP[:, :], Wt[:, kt * 512 + pf * 128: kt * 512 + pf * 128 + 128],
                                ctx.XNB[:, kt * SLAB:(kt + 1) * SLAB],
                                start=(kt == 0), stop=(kt == NPT - 1)))
                        return ins
                    p.op("tensor", mk_p, reads=[rr("XNB"), rr(wk)], writes=[rr(kPP)])
                    p.op("vector", lambda e: e.tensor_copy(out=RT1[:, :], in_=PP[:, :]),
                         reads=[rr(kPP)], writes=[rr("RT17")])
                    p.op("tensor", lambda e: e.matmul(PCTX[:, :], swap[:, :], RT1[:, :],
                                                      start=True, stop=True),
                         reads=[rr("RT17"), rr("c_swap32")], writes=[rr(kPCTX)])
                    cs = ropec[:, s * SLAB:(s + 1) * SLAB]
                    sn = ropes[:, s * SLAB:(s + 1) * SLAB]
                    p.op("vector", lambda e, sn=sn: e.tensor_tensor(
                        out=RT2[:, :], in0=PCTX[:, :], in1=sn, op=ALU.mult),
                         reads=[rr(kPCTX), rr("c_ropes4")], writes=[rr("RT27")])
                    p.op("vector", lambda e, cs=cs: e.tensor_tensor(
                        out=RT1[:, :], in0=RT1[:, :], in1=cs, op=ALU.mult),
                         reads=[rr("RT17"), rr("c_ropec4")], writes=[rr("RT17")])
                    dst = AR[:, pf * T + s * SLAB: pf * T + (s + 1) * SLAB]
                    p.op("vector", lambda e, dst=dst: e.tensor_tensor(
                        out=dst, in0=RT1[:, :], in1=RT2[:, :], op=ALU.add),
                         reads=[rr("RT17"), rr("RT27")], writes=[rr(ak, pf * 8 + s, pf * 8 + s + 1)])
            for tt in range(4):
                def mk_v(e, tt=tt):
                    ins = []
                    for kt in range(NPT):
                        ins.append(e.matmul(
                            PP[:, :], ctx.XNB[:, kt * SLAB + tt * 128: kt * SLAB + tt * 128 + 128],
                            WVG[:, kt * 512:(kt + 1) * 512],
                            start=(kt == 0), stop=(kt == NPT - 1)))
                    return ins
                p.op("tensor", mk_v, reads=[rr("XNB"), rr("WVG7")], writes=[rr(kPP)])
                ktg = s * 4 + tt
                dv = VAR[:, ktg * 520: ktg * 520 + 520].rearrange("p (h c) -> p h c", c=65)[:, :, 0:64]
                p.op("vector", lambda e, dv=dv: e.tensor_copy(out=dv, in_=PP[:, :]),
                     reads=[rr(kPP)], writes=[rr("VARV7", ktg, ktg + 1)])
            for which, AR, kk in ((0, QAR, "QAR7"), (1, KAR, "KAR7")):
                for pf in range(4):
                    src = AR[:, pf * T + s * SLAB: pf * T + (s + 1) * SLAB]
                    p.op("vector", lambda e, src=src, pf=pf: e.tensor_tensor(
                        out=SQT[:, pf * 512:(pf + 1) * 512], in0=src, in1=src, op=ALU.mult),
                         reads=[rr(kk, pf * 8 + s, pf * 8 + s + 1)],
                         writes=[rr("SQT7", pf, pf + 1)])
                def mk_n(e):
                    ins = []
                    for pf in range(4):
                        ins.append(e.matmul(PSQ[1][0:8, 0:512], hmask[:, pf * 8:(pf + 1) * 8],
                                            SQT[:, pf * 512:(pf + 1) * 512],
                                            start=(pf == 0), stop=(pf == 3)))
                    return ins
                p.op("tensor", mk_n, reads=[rr("SQT7", 0, 4), rr("c_hmask")], writes=[rr("PSQ71")])
                p.op("vector", lambda e: e.tensor_reduce(out=NRM[0:8, 2:3], in_=PSQ[1][0:8, 0:512],
                                                         axis=AX.X, op=ALU.max),
                     reads=[rr("PSQ71")], writes=[rr("NRM7", 2, 3)])
                if s == 0:
                    p.op("vector", lambda e, which=which: e.tensor_copy(
                        out=NRM[0:8, which:which + 1], in_=NRM[0:8, 2:3]),
                         reads=[rr("NRM7", 2, 3)], writes=[rr("NRM7", which, which + 1)])
                else:
                    p.op("vector", lambda e, which=which: e.tensor_tensor(
                        out=NRM[0:8, which:which + 1], in0=NRM[0:8, which:which + 1],
                        in1=NRM[0:8, 2:3], op=ALU.max),
                         reads=[rr("NRM7", 2, 3), rr("NRM7", which, which + 1)],
                         writes=[rr("NRM7", which, which + 1)])
        p.op("scalar", lambda e: e.activation(out=NRM[0:8, 0:1], in_=NRM[0:8, 0:1], func=AF.Ln),
             reads=[rr("NRM7", 0, 1)], writes=[rr("NRM7", 0, 1)])
        p.op("scalar", lambda e: e.activation(out=NRM[0:8, 1:2], in_=NRM[0:8, 1:2], func=AF.Ln),
             reads=[rr("NRM7", 1, 2)], writes=[rr("NRM7", 1, 2)])
        p.op("vector", lambda e: e.tensor_tensor(out=NRM[0:8, 3:4], in0=NRM[0:8, 0:1],
                                                 in1=NRM[0:8, 1:2], op=ALU.add),
             reads=[rr("NRM7", 0, 2)], writes=[rr("NRM7", 3, 4)])
        p.op("scalar", lambda e: e.activation(out=NRM[0:8, 3:4], in_=NRM[0:8, 3:4],
                                              func=AF.Exp, scale=0.25),
             reads=[rr("NRM7", 3, 4)], writes=[rr("NRM7", 3, 4)])
        p.op("vector", lambda e: e.tensor_scalar(out=NRM[0:8, 3:4], in0=NRM[0:8, 3:4],
                                                 scalar1=-0.125, scalar2=None, op0=ALU.mult),
             reads=[rr("NRM7", 3, 4)], writes=[rr("NRM7", 3, 4)])
        p.op("tensor", lambda e: e.transpose(PP[0:1, 0:8], NRM[0:8, 3:4],
                                             ctx.sb["c_id_f32"][0:8, 0:8]),
             reads=[rr("NRM7", 3, 4), rr("c_id_f32")], writes=[rr(kPP)])
        p.op("vector", lambda e: e.tensor_copy(out=BROW[0:1, 0:8], in_=PP[0:1, 0:8]),
             reads=[rr(kPP)], writes=[rr("BROW7")])
        for hh in range(8):
            p.op("tensor", lambda e, hh=hh: e.matmul(PP[:, hh:hh + 1],
                                                     ctx.sb["c_ones_row"][:, :],
                                                     BROW[0:1, hh:hh + 1], start=True, stop=True),
                 reads=[rr("BROW7"), rr("c_ones_row")], writes=[rr(kPP)])
        p.op("vector", lambda e: e.tensor_copy(out=BCOLS[:, 0:8], in_=PP[:, 0:8]),
             reads=[rr(kPP)], writes=[rr("BCOLS7")])

        names7 = {"kar": "KAR7", "qar": "QAR7", "varv": "VARV7", "varone": "VARONE7",
                  "ptb": "PTB7", "outs": "OUTS7", "ctxe": "CTXE7", "rl": "RL7",
                  "bcols": "BCOLS7",
                  "psq": [(PSQ[i], "PSQ7%d" % i) for i in range(4)],
                  "pctx": [(PCTX, "PCTX7"), (ctx.PS[0], "PSB0")],
                  "pbc": [(PP, "PP7"), (ctx.PS[1], "PSB1")]}
        for s in range(NSLAB):
            emit_attn_passB_slab(p, ctx, s, names7, KAR, QAR, VAR, PTB, OUTS, CTXE,
                                 RL, BCOLS, PSQ, PCTX, PP)
            if g == 0:
                for pt in range(NPT):
                    def mk_o(e, pt=pt):
                        ins = []
                        for kt in range(4):
                            ins.append(e.matmul(
                                PP[:, :], WOG[:, kt * 1024 + pt * 128: kt * 1024 + pt * 128 + 128],
                                OUTS[:, kt * 512:(kt + 1) * 512],
                                start=(kt == 0), stop=(kt == 3)))
                        return ins
                    p.op("tensor", mk_o, reads=[rr("OUTS7", 0, 8), rr("WOG7")], writes=[rr(kPP)])
                    p.op("scalar", lambda e, pt=pt: e.activation(
                        out=AOS7[:, pt * SLAB:(pt + 1) * SLAB], in_=PP[:, :],
                        func=AF.Identity, bias=BO[:, pt:pt + 1]),
                         reads=[rr(kPP), rr("BO7")], writes=[rr("AOS7s", pt, pt + 1)])
                ao = attout.ap()[:, s * SLAB:(s + 1) * SLAB].rearrange("(pt p) t -> p pt t", p=128)
                a3 = AOS7[:, :].rearrange("p (pt t) -> p pt t", t=SLAB)
                p.dma("qscal", ao, a3, reads=[rr("AOS7s", 0, 8)], writes=[rr("attout7", s, s + 1)])
            else:
                ao = attout.ap()[:, s * SLAB:(s + 1) * SLAB].rearrange("(pt p) t -> p pt t", p=128)
                a3 = AOS7[:, :].rearrange("p (pt t) -> p pt t", t=SLAB)
                p.dma("qsync", a3, ao, reads=[rr("attout7", s, s + 1)], writes=[rr("AOS7s", 0, 8)])
                xa = x_in.ap()[:, s * SLAB:(s + 1) * SLAB].rearrange("(pt p) t -> p pt t", p=128)
                xs3 = ctx.XS[:, :].rearrange("p (pt t) -> p pt t", t=SLAB)
                p.dma("qsync", xs3, xa, reads=[rr(kin, s, s + 1)], writes=[rr("XS")])
                for pt in range(NPT):
                    def mk_o(e, pt=pt):
                        ins = []
                        for kt in range(4):
                            ins.append(e.matmul(
                                PP[:, :], WOG[:, kt * 1024 + pt * 128: kt * 1024 + pt * 128 + 128],
                                OUTS[:, kt * 512:(kt + 1) * 512],
                                start=(kt == 0), stop=(kt == 3)))
                        return ins
                    p.op("tensor", mk_o, reads=[rr("OUTS7", 0, 8), rr("WOG7")], writes=[rr(kPP)])
                    sl = slice(pt * SLAB, (pt + 1) * SLAB)
                    p.op("vector", lambda e, sl=sl: e.tensor_tensor(
                        out=ctx.XS[:, sl], in0=ctx.XS[:, sl], in1=AOS7[:, sl], op=ALU.add),
                         reads=[rr("AOS7s", 0, 8), rr("XS")], writes=[rr("XS")])
                    p.op("vector", lambda e, sl=sl: e.tensor_tensor(
                        out=ctx.XS[:, sl], in0=ctx.XS[:, sl], in1=PP[:, :], op=ALU.add),
                         reads=[rr(kPP), rr("XS")], writes=[rr("XS")])
                store_x_slab(ctx, x_out, kout, s)


def emit_precast(ctx, src_dram, dst_dram, rows, cols, uid, BB=None):
    p = ctx.p
    rchunk = max(128, rows // 8)
    sa = src_dram.ap()
    da = dst_dram.ap()
    for i, r0 in enumerate(range(0, rows, rchunk)):
        r1 = min(rows, r0 + rchunk)
        p.dma("qgps", da[r0:r1, :], sa[r0:r1, :],
              writes=[rr("pcdst" + uid, i, i + 1)], tag="pc")



def emit_mamba(ctx, x_in, x_out, kin, kout, w):
    p = ctx.p
    oh16 = ctx.use_const("c_oh16")
    idb = ctx.use_const("c_id_bf16")
    ones128 = ctx.use_const("c_ones128")
    utri = ctx.use_const("c_utri")
    ltris = ctx.use_const("c_ltris")
    cmask = ctx.use_const("c_causmask")
    WXC = p.sbuf("WXC1", [128, 8 * 2048], BF16)
    WZ = p.sbuf("WZ1", [128, 8 * 2048], BF16)
    WX = p.sbuf("WX1", [128, 16 * 33], BF16)
    WOUT = p.sbuf("WOUT1", [128, 16 * 1024], BF16)
    CVW = p.sbuf("CVW1", [128, 64], F32)
    CVB = p.sbuf("CVB1", [128, 16], F32)
    DCS = p.sbuf("DCS1", [128, 16], F32)
    ALC = p.sbuf("ALC1", [16, 1], F32)
    ANC = p.sbuf("ANC1", [16, 1], F32)
    ABC = p.sbuf("ABC1", [128, 16], F32)
    XCB = p.sbuf("XCB1", [128, 515], BF16)
    if MAMBA_FP32D:
        WXDF = p.sbuf("WXDF1", [128, 16], F32)
        XSF32T = p.sbuf("XSF32T", [128, 128], F32)
    HALO = p.sbuf("HALO1", [128, 48], BF16)
    XSFM = p.sbuf("XSFM1", [128, 16 * 512], BF16)
    XST = p.sbuf("XST1", [128, 2048], BF16)
    ZS = p.sbuf("ZS1", [128, 16 * 512], BF16)
    YB = p.sbuf("YB1", [128, 16 * 512], BF16)
    HF = p.sbuf("HF1", [16, 2048], F32)
    HBF = p.sbuf("HBF1", [16, 2048], BF16)
    DELT = p.sbuf("DELT1", [128, 8], F32)
    T16 = p.sbuf("T16A", [128, 16], F32)
    E1 = p.sbuf("E1B", [128, 16], F32)
    EB = p.sbuf("EBB", [128, 16], F32)
    CEX = p.sbuf("CEX1", [128, 16], BF16)
    BEX = p.sbuf("BEX1", [128, 16], BF16)
    CEXT = p.sbuf("CEXT1", [16, 128], BF16)
    BBAR = p.sbuf("BBAR1", [128, 16], BF16)
    CPB = p.sbuf("CPB1", [128, 16], BF16)
    CPT = p.sbuf("CPT1", [16, 128], BF16)
    DEC = p.sbuf("DEC1", [16, 1], F32)
    TDA = p.sbuf("TDA1", [128, 128], F32)
    TDB = p.sbuf("TDB1", [128, 128], F32)
    GB1 = p.sbuf("GB1", [128, 128], BF16)
    MTA = p.sbuf("MTA1", [128, 128], F32)
    MTB = p.sbuf("MTB1", [128, 128], BF16)
    PA = p.psum("PA1", [128, 512], F32)
    PB = p.psum("PB1", [128, 512], F32)
    PD = p.psum("PD1", [128, 512], F32)
    PST = p.psum("PST1", [128, 512], F32)
    PT1 = p.psum("PT1", [128, 512], F32)
    kPA, kPB, kPD, kPST, kPT = "PA1", "PB1", "PD1", "PST1", "PT1"

    for q in range(2):
        sl = slice(q * 4, (q + 1) * 4)
        p.dma("qgps", WXC[:, :].rearrange("p (kt c) -> p kt c", c=2048)[:, sl, :],
              w["winxc"].ap().rearrange("(kt p) c -> p kt c", p=128)[:, sl, :],
              writes=[rr("WXC1", q, q + 1)])
        p.dma("qgps", WZ[:, :].rearrange("p (kt c) -> p kt c", c=2048)[:, sl, :],
              w["winz"].ap().rearrange("(kt p) c -> p kt c", p=128)[:, sl, :],
              writes=[rr("WZ1", q, q + 1)])
    p.dma("qgps", WX[:, :].rearrange("p (kt c) -> p kt c", c=33),
          w["wx"].ap().rearrange("(kt p) c -> p kt c", p=128), writes=[rr("WX1")])
    for q in range(4):
        sl = slice(q * 4, (q + 1) * 4)
        p.dma("qgps", WOUT[:, :].rearrange("p (kt c) -> p kt c", c=1024)[:, sl, :],
              w["wout"].ap().rearrange("(kt p) c -> p kt c", p=128)[:, sl, :],
              writes=[rr("WOUT1", q, q + 1)])
    p.dma("qsync", CVW[:, :], w["convcols"].ap(), writes=[rr("CVW1")])
    if MAMBA_FP32D:
        p.dma("qsync", WXDF[:, :], w["wxd_cols"].ap(), writes=[rr("WXDF1")])
    p.dma("qsync", CVB[:, :], w["convbcol"].ap(), writes=[rr("CVB1")])
    p.dma("qsync", DCS[:, :], w["dcols"].ap(), writes=[rr("DCS1")])
    p.dma("qsync", ALC[:, :], w["alogcol"].ap(), writes=[rr("ALC1")])
    p.op("scalar", lambda e: e.activation(out=ANC[:, :], in_=ALC[:, :], func=AF.Exp),
         reads=[rr("ALC1")], writes=[rr("ANC1")])
    p.op("vector", lambda e: e.tensor_scalar(out=ANC[:, :], in0=ANC[:, :], scalar1=-1.0,
                                             scalar2=None, op0=ALU.mult),
         reads=[rr("ANC1")], writes=[rr("ANC1")])
    p.op("tensor", lambda e: e.transpose(PT1[0:1, 0:16], ANC[:, :],
                                         ctx.sb["c_id_f32"][0:16, 0:16]),
         reads=[rr("ANC1"), rr("c_id_f32")], writes=[rr(kPT)])
    p.op("vector", lambda e: e.tensor_copy(out=ctx.ROWS[0:1, 0:16], in_=PT1[0:1, 0:16]),
         reads=[rr(kPT)], writes=[rr("ROWS", 0, 16)])
    p.op("tensor", lambda e: e.matmul(PT1[:, 0:16], ctx.sb["c_ones_row"][:, :],
                                      ctx.ROWS[0:1, 0:16], start=True, stop=True),
         reads=[rr("ROWS", 0, 16), rr("c_ones_row")], writes=[rr(kPT)])
    p.op("vector", lambda e: e.tensor_copy(out=ABC[:, :], in_=PT1[:, 0:16]),
         reads=[rr(kPT)], writes=[rr("ABC1")])
    p.op("vector", lambda e: e.memset(HF[:, :], 0.0), writes=[rr("HF1")])
    p.op("vector", lambda e: e.memset(HBF[:, :], 0.0), writes=[rr("HBF1")])
    p.op("vector", lambda e: e.memset(HALO[:, :], 0.0), writes=[rr("HALO1", 0, 16)])

    DPS = [(PD, rr(kPD)), (ctx.PS[1], rr("PSB1")), (PT1, rr(kPT)),
           (ctx.PS[0], rr("PSB0"))]
    for s in range(NSLAB):
        load_x_slab(ctx, x_in, kin, s, stats=True)
        for ft in range(16):
            xb = 0
            p.op("vector", lambda e, ft=ft, xb=xb: e.tensor_copy(
                out=XCB[:, xb: xb + 3], in_=HALO[:, ft * 3: ft * 3 + 3]),
                 reads=[rr("HALO1", ft, ft + 1)], writes=[rr("XCB1", ft % 2, ft % 2 + 1)])
            PXCb, kXCb = (PA, kPA) if ft % 2 == 0 else (PD, kPD)
            def mk_xc(e, ft=ft, PXCb=PXCb):
                ins = []
                for kt in range(NPT):
                    ins.append(e.matmul(
                        PXCb[:, :], WXC[:, kt * 2048 + ft * 128: kt * 2048 + ft * 128 + 128],
                        ctx.XNB[:, kt * SLAB:(kt + 1) * SLAB],
                        start=(kt == 0), stop=(kt == NPT - 1)))
                return ins
            p.op("tensor", mk_xc, reads=[rr("XNB"), rr("WXC1")], writes=[rr(kXCb)])
            p.op("vector", lambda e, xb=xb, PXCb=PXCb: e.tensor_copy(
                out=XCB[:, xb + 3: xb + 515], in_=PXCb[:, :]),
                 reads=[rr(kXCb)], writes=[rr("XCB1", ft % 2, ft % 2 + 1)])
            p.op("vector", lambda e, ft=ft, xb=xb: e.tensor_copy(
                out=HALO[:, ft * 3: ft * 3 + 3], in_=XCB[:, xb + 512: xb + 515]),
                 reads=[rr("XCB1", ft % 2, ft % 2 + 1)], writes=[rr("HALO1", ft, ft + 1)])
            p.op("vector", lambda e, ft=ft, xb=xb: e.tensor_scalar(
                out=ctx.TMP[:, :], in0=XCB[:, xb + 0: xb + 512],
                scalar1=CVW[:, ft * 4: ft * 4 + 1], scalar2=None, op0=ALU.mult),
                 reads=[rr("XCB1", ft % 2, ft % 2 + 1), rr("CVW1")], writes=[rr("TMP")])
            for j in range(1, 4):
                p.op("vector", lambda e, ft=ft, j=j, xb=xb: e.scalar_tensor_tensor(
                    out=ctx.TMP[:, :], in0=XCB[:, xb + j: xb + j + 512],
                    scalar=CVW[:, ft * 4 + j: ft * 4 + j + 1], in1=ctx.TMP[:, :],
                    op0=ALU.mult, op1=ALU.add),
                     reads=[rr("XCB1", ft % 2, ft % 2 + 1), rr("CVW1"), rr("TMP")],
                     writes=[rr("TMP")])
            p.op("scalar", lambda e, ft=ft: e.activation(
                out=XSFM[:, ft * 512:(ft + 1) * 512], in_=ctx.TMP[:, :], func=AF.Silu,
                bias=CVB[:, ft:ft + 1]),
                 reads=[rr("TMP"), rr("CVB1")], writes=[rr("XSFM1", ft, ft + 1)])
            for cc in (range(4) if MAMBA_FP32D else []):
                dps, dkey = DPS[cc]
                p.op("scalar", lambda e, cc=cc: e.activation(
                    out=XSF32T[:, :], in_=ctx.TMP[:, cc * 128:(cc + 1) * 128], func=AF.Silu,
                    bias=CVB[:, ft:ft + 1]),
                     reads=[rr("TMP"), rr("CVB1")], writes=[rr("XSF32T1")])
                p.op("tensor", lambda e, ft=ft, dps=dps: e.matmul(
                    dps[:, 384:385], XSF32T[:, :], WXDF[:, ft:ft + 1],
                    start=(ft == 0), stop=(ft == 15)),
                     reads=[rr("XSF32T1"), rr("WXDF1")], writes=[dkey])
            PZb, kZb = (PB, kPB) if ft % 2 == 0 else (PT1, kPT)
            def mk_z(e, ft=ft, PZb=PZb):
                ins = []
                for kt in range(NPT):
                    ins.append(e.matmul(
                        PZb[:, :], WZ[:, kt * 2048 + ft * 128: kt * 2048 + ft * 128 + 128],
                        ctx.XNB[:, kt * SLAB:(kt + 1) * SLAB],
                        start=(kt == 0), stop=(kt == NPT - 1)))
                return ins
            p.op("tensor", mk_z, reads=[rr("XNB"), rr("WZ1")], writes=[rr(kZb)])
            p.op("scalar", lambda e, ft=ft, PZb=PZb: e.activation(
                out=ZS[:, ft * 512:(ft + 1) * 512], in_=PZb[:, :], func=AF.Silu),
                 reads=[rr(kZb)], writes=[rr("ZS1", ft, ft + 1)])
        for c in range(4):
            par = c % 2
            for ft in range(16):
                pbf = PT1[:, :].bitcast(BF16)
                p.op("tensor", lambda e, ft=ft, c=c, pbf=pbf: e.transpose(
                    pbf[:, 0:128], XSFM[:, ft * 512 + c * 128: ft * 512 + c * 128 + 128],
                    idb[:, :]),
                     reads=[rr("XSFM1", ft, ft + 1), rr("c_id_bf16")], writes=[rr(kPT)])
                p.op("vector", lambda e, ft=ft, pbf=pbf: e.tensor_copy(
                    out=XST[:, ft * 128: ft * 128 + 128], in_=pbf[:, 0:128]),
                     reads=[rr(kPT)], writes=[rr("XST1", ft, ft + 1)])
            def mk_xd(e, c=c):
                ins = []
                for ft in range(16):
                    ins.append(e.matmul(
                        PD[0:128, 0:33], XSFM[:, ft * 512 + c * 128: ft * 512 + c * 128 + 128],
                        WX[:, ft * 33:(ft + 1) * 33],
                        start=(ft == 0), stop=(ft == 15)))
                return ins
            p.op("tensor", mk_xd, reads=[rr("XSFM1", 0, 16), rr("WX1")], writes=[rr(kPD)])
            if MAMBA_FP32D:
                dps, dkey = DPS[c]
                dsrc = dps[:, 384:385]
            else:
                dps, dkey = PD, rr(kPD)
                dsrc = PD[:, 0:1]
            p.op("scalar", lambda e, dsrc=dsrc: e.activation(out=DELT[:, 5:6], in_=dsrc,
                                                  func=AF.Abs),
                 reads=[dkey], writes=[rr("DELT1", 5, 6)])
            p.op("scalar", lambda e: e.activation(out=DELT[:, 6:7], in_=DELT[:, 5:6],
                                                  func=AF.Exp, scale=-1.0),
                 reads=[rr("DELT1", 5, 6)], writes=[rr("DELT1", 6, 7)])
            p.op("scalar", lambda e: e.activation(out=DELT[:, 6:7], in_=DELT[:, 6:7],
                                                  func=AF.Ln, bias=ctx.COLS[:, 1:2]),
                 reads=[rr("DELT1", 6, 7), rr("ONEC")], writes=[rr("DELT1", 6, 7)])
            p.op("scalar", lambda e, dsrc=dsrc: e.activation(out=DELT[:, 7:8], in_=dsrc,
                                                  func=AF.Relu),
                 reads=[dkey], writes=[rr("DELT1", 7, 8)])
            p.op("vector", lambda e: e.tensor_tensor(out=DELT[:, 0:1], in0=DELT[:, 7:8],
                                                     in1=DELT[:, 6:7], op=ALU.add),
                 reads=[rr("DELT1", 6, 8)], writes=[rr("DELT1", 0, 1)])
            p.op("vector", lambda e: e.tensor_scalar(out=DELT[:, 1:2], in0=DELT[:, 0:1],
                                                     scalar1=-1.0, scalar2=None, op0=ALU.mult),
                 reads=[rr("DELT1", 0, 1)], writes=[rr("DELT1", 1, 2)])
            p.op("tensor", lambda e: e.matmul(PT1[:, 0:1], utri[:, :], DELT[:, 0:1],
                                              start=True, stop=True),
                 reads=[rr("DELT1", 0, 1), rr("c_utri")], writes=[rr(kPT)])
            p.op("vector", lambda e: e.tensor_copy(out=DELT[:, 2:3], in_=PT1[:, 0:1]),
                 reads=[rr(kPT)], writes=[rr("DELT1", 2, 3)])
            p.op("tensor", lambda e: e.matmul(PT1[:, 0:1], ltris[:, :], DELT[:, 0:1],
                                              start=True, stop=True),
                 reads=[rr("DELT1", 0, 1), rr("c_ltris")], writes=[rr(kPT)])
            p.op("vector", lambda e: e.tensor_copy(out=DELT[:, 3:4], in_=PT1[:, 0:1]),
                 reads=[rr(kPT)], writes=[rr("DELT1", 3, 4)])
            p.op("vector", lambda e: e.tensor_tensor(out=DELT[:, 4:5], in0=DELT[:, 2:3],
                                                     in1=DELT[:, 3:4], op=ALU.add),
                 reads=[rr("DELT1", 2, 4)], writes=[rr("DELT1", 4, 5)])
            p.op("vector", lambda e: e.tensor_scalar(out=T16[:, :], in0=ABC[:, :],
                                                     scalar1=DELT[:, 2:3], scalar2=None,
                                                     op0=ALU.mult),
                 reads=[rr("ABC1"), rr("DELT1", 2, 3)], writes=[rr("T16A")])
            p.op("scalar", lambda e: e.activation(out=E1[:, :], in_=T16[:, :], func=AF.Exp),
                 reads=[rr("T16A")], writes=[rr("E1B")])
            p.op("vector", lambda e: e.tensor_tensor(out=CEX[:, :], in0=PD[:, 17:33],
                                                     in1=E1[:, :], op=ALU.mult),
                 reads=[rr(kPD), rr("E1B")], writes=[rr("CEX1")])
            p.op("vector", lambda e: e.tensor_scalar(out=T16[:, :], in0=ABC[:, :],
                                                     scalar1=DELT[:, 3:4], scalar2=None,
                                                     op0=ALU.mult),
                 reads=[rr("ABC1"), rr("DELT1", 3, 4)], writes=[rr("T16A")])
            p.op("scalar", lambda e: e.activation(out=EB[:, :], in_=T16[:, :], func=AF.Exp),
                 reads=[rr("T16A")], writes=[rr("EBB")])
            p.op("vector", lambda e: e.tensor_scalar(out=T16[:, :], in0=PD[:, 1:17],
                                                     scalar1=DELT[:, 0:1], scalar2=None,
                                                     op0=ALU.mult),
                 reads=[rr(kPD), rr("DELT1", 0, 1)], writes=[rr("T16A")])
            p.op("vector", lambda e: e.tensor_copy(out=BBAR[:, :], in_=T16[:, :]),
                 reads=[rr("T16A")], writes=[rr("BBAR1")])
            p.op("vector", lambda e: e.tensor_tensor(out=BEX[:, :], in0=T16[:, :],
                                                     in1=EB[:, :], op=ALU.mult),
                 reads=[rr("T16A"), rr("EBB")], writes=[rr("BEX1")])
            p.op("vector", lambda e: e.tensor_scalar(out=DEC[:, :], in0=ANC[:, :],
                                                     scalar1=DELT[0:16, 4:5], scalar2=None,
                                                     op0=ALU.mult),
                 reads=[rr("ANC1"), rr("DELT1", 4, 5)], writes=[rr("DEC1")])
            p.op("scalar", lambda e: e.activation(out=DEC[:, :], in_=DEC[:, :], func=AF.Exp),
                 reads=[rr("DEC1")], writes=[rr("DEC1")])
            pbf = PT1[:, :].bitcast(BF16)
            p.op("tensor", lambda e, pbf=pbf: e.transpose(pbf[0:16, 0:128], CEX[:, :],
                                                          idb[:, :]),
                 reads=[rr("CEX1"), rr("c_id_bf16")], writes=[rr(kPT)])
            p.op("vector", lambda e, pbf=pbf: e.tensor_copy(out=CEXT[:, :], in_=pbf[0:16, 0:128]),
                 reads=[rr(kPT)], writes=[rr("CEXT1")])
            p.op("vector", lambda e: e.tensor_copy(out=CPB[:, :], in_=PD[:, 17:33]),
                 reads=[rr(kPD)], writes=[rr("CPB1")])
            p.op("tensor", lambda e, pbf=pbf: e.transpose(pbf[0:16, 0:128], CPB[:, :],
                                                          idb[:, :]),
                 reads=[rr("CPB1"), rr("c_id_bf16")], writes=[rr(kPT)])
            p.op("vector", lambda e, pbf=pbf: e.tensor_copy(out=CPT[:, :], in_=pbf[0:16, 0:128]),
                 reads=[rr(kPT)], writes=[rr("CPT1")])
            p.op("vector", lambda e: e.tensor_scalar(out=TDA[:, :], in0=utri[:, :],
                                                     scalar1=DELT[:, 1:2], scalar2=None,
                                                     op0=ALU.mult),
                 reads=[rr("c_utri"), rr("DELT1", 1, 2)], writes=[rr("TDA1")])
            p.op("vector", lambda e: e.tensor_scalar(out=TDB[:, :], in0=ones128[:, :],
                                                     scalar1=DELT[:, 0:1], scalar2=None,
                                                     op0=ALU.mult),
                 reads=[rr("c_ones128"), rr("DELT1", 0, 1)], writes=[rr("TDB1")])
            def mk_td(e):
                i1 = e.matmul(PD[:, 128:256], TDA[:, :], ones128[:, :], start=True, stop=False)
                i2 = e.matmul(PD[:, 128:256], TDB[:, :], utri[:, :], start=False, stop=True)
                return [i1, i2]
            p.op("tensor", mk_td, reads=[rr("TDA1"), rr("TDB1"), rr("c_ones128"), rr("c_utri")],
                 writes=[rr(kPD)])
            p.op("vector", lambda e: e.tensor_tensor(out=PST[:, 0:128], in0=PD[:, 128:256],
                                                     in1=cmask[:, :], op=ALU.add),
                 reads=[rr(kPD), rr("c_causmask")], writes=[rr(kPST)])
            for n in range(16):
                p.op("scalar", lambda e, n=n: e.activation(out=GB1[:, :], in_=PST[:, 0:128],
                                                           func=AF.Exp, scale=ABC[:, n:n + 1]),
                     reads=[rr(kPST), rr("ABC1")], writes=[rr("GB1k")])
                p.op("tensor", lambda e, n=n: e.matmul(
                    PT1[:, 0:128], oh16[:, n * 128:(n + 1) * 128], CPT[:, :],
                    start=True, stop=True),
                     reads=[rr("CPT1"), rr("c_oh16")], writes=[rr(kPT)])
                if n == 0:
                    p.op("vector", lambda e, n=n: e.scalar_tensor_tensor(
                        out=MTA[:, :], in0=GB1[:, :], scalar=BBAR[:, n:n + 1],
                        in1=PT1[:, 0:128], op0=ALU.mult, op1=ALU.mult),
                         reads=[rr("GB1k"), rr("BBAR1"), rr(kPT)], writes=[rr("MTA1")])
                else:
                    p.op("vector", lambda e, n=n: e.scalar_tensor_tensor(
                        out=PD[:, 0:128], in0=GB1[:, :], scalar=BBAR[:, n:n + 1],
                        in1=PT1[:, 0:128], op0=ALU.mult, op1=ALU.mult),
                         reads=[rr("GB1k"), rr("BBAR1"), rr(kPT)], writes=[rr(kPD)])
                    p.op("vector", lambda e: e.tensor_tensor(out=MTA[:, :], in0=MTA[:, :],
                                                             in1=PD[:, 0:128], op=ALU.add),
                         reads=[rr(kPD), rr("MTA1")], writes=[rr("MTA1")])
            p.op("vector", lambda e: e.tensor_copy(out=MTB[:, :], in_=MTA[:, :]),
                 reads=[rr("MTA1")], writes=[rr("MTB1")])
            for ft in range(16):
                pw = PA if ft % 2 == 0 else PB
                kpw = kPA if ft % 2 == 0 else kPB
                def mk_y(e, ft=ft, pw=pw):
                    i1 = e.matmul(pw[:, 0:128],
                                  XST[:, ft * 128: ft * 128 + 128],
                                  MTB[:, :], start=True, stop=False)
                    i2 = e.matmul(pw[:, 0:128], HBF[:, ft * 128:(ft + 1) * 128],
                                  CEXT[:, :], start=False, stop=True)
                    return [i1, i2]
                p.op("tensor", mk_y,
                     reads=[rr("XST1", ft, ft + 1), rr("MTB1"), rr("HBF1"), rr("CEXT1")],
                     writes=[rr(kpw)])
                p.op("vector", lambda e, ft=ft, c=c, pw=pw: e.scalar_tensor_tensor(
                    out=PD[:, 0:128], in0=XSFM[:, ft * 512 + c * 128: ft * 512 + c * 128 + 128],
                    scalar=DCS[:, ft:ft + 1], in1=pw[:, 0:128], op0=ALU.mult, op1=ALU.add),
                     reads=[rr(kpw), rr("XSFM1", ft, ft + 1), rr("DCS1")], writes=[rr(kPD)])
                p.op("vector", lambda e, ft=ft, c=c: e.tensor_tensor(
                    out=YB[:, ft * 512 + c * 128: ft * 512 + c * 128 + 128], in0=PD[:, 0:128],
                    in1=ZS[:, ft * 512 + c * 128: ft * 512 + c * 128 + 128], op=ALU.mult),
                     reads=[rr(kPD), rr("ZS1", ft, ft + 1)], writes=[rr("YB1", ft, ft + 1)])
            for dc in range(4):
                def mk_h(e, dc=dc):
                    return e.matmul(PST[0:16, :], BEX[:, :],
                                    XST[:, dc * 512:(dc + 1) * 512],
                                    start=True, stop=True)
                p.op("tensor", mk_h,
                     reads=[rr("BEX1"), rr("XST1", 0, 16)], writes=[rr(kPST)])
                p.op("vector", lambda e, dc=dc: e.scalar_tensor_tensor(
                    out=HF[:, dc * 512:(dc + 1) * 512], in0=HF[:, dc * 512:(dc + 1) * 512],
                    scalar=DEC[:, 0:1], in1=PST[0:16, :], op0=ALU.mult, op1=ALU.add),
                     reads=[rr(kPST), rr("HF1"), rr("DEC1")], writes=[rr("HF1")])
                p.op("vector", lambda e, dc=dc: e.tensor_copy(
                    out=HBF[:, dc * 512:(dc + 1) * 512], in_=HF[:, dc * 512:(dc + 1) * 512]),
                     reads=[rr("HF1")], writes=[rr("HBF1")])
        for dot in range(NPT):
            def mk_o(e, dot=dot):
                ins = []
                for kt in range(16):
                    ins.append(e.matmul(
                        PA[:, :], WOUT[:, kt * 1024 + dot * 128: kt * 1024 + dot * 128 + 128],
                        YB[:, kt * 512:(kt + 1) * 512],
                        start=(kt == 0), stop=(kt == 15)))
                return ins
            p.op("tensor", mk_o, reads=[rr("YB1", 0, 16), rr("WOUT1")], writes=[rr(kPA)])
            p.op("vector", lambda e, dot=dot: e.tensor_tensor(
                out=ctx.XS[:, dot * SLAB:(dot + 1) * SLAB],
                in0=ctx.XS[:, dot * SLAB:(dot + 1) * SLAB], in1=PA[:, :], op=ALU.add),
                 reads=[rr(kPA), rr("XS")], writes=[rr("XS")])
        store_x_slab(ctx, x_out, kout, s)



def build_l0ffn_test():
    nc = bass.Bass("TRN2", target_bir_lowering=False)
    xin = nc.dram_tensor("xin", [D, T], F32, kind="ExternalInput")
    xout = nc.dram_tensor("xout", [D, T], F32, kind="ExternalOutput")
    win = nc.dram_tensor("w_in", [8, 4096, 1024], F32, kind="ExternalInput")
    wotT = nc.dram_tensor("w_outT", [8, 4096, 1024], F32, kind="ExternalInput")
    wgT = nc.dram_tensor("w_gateT", [D, 8], F32, kind="ExternalInput")
    me = nc.dram_tensor("me", [1024, 8192], BF16, kind="Internal")
    stack = ExitStack()
    p = Prog(nc, stack)
    ctx = Ctx(p)
    wg_sb = p.sbuf("WGsb", [128, NPT * 8], BF16)
    p.dma("qgps", wg_sb[:, :].rearrange("p (pt e8) -> p pt e8", e8=8),
          wgT.ap().rearrange("(pt p) e8 -> p pt e8", p=128),
          writes=[rr("WGsb")])
    keyscb = nc.dram_tensor("keyscb", [D, 8192], BF16, kind="Internal")
    valscb = nc.dram_tensor("valscb", [8192, D], BF16, kind="Internal")
    with p.phase():
        emit_me_build(ctx, win, wotT, me)
        emit_precast(ctx, keysc, keyscb, D, 8192, "K")
        emit_precast(ctx, valsc, valscb, 8192, D, "V")
    with p.phase():
        emit_l0ffn(ctx, xin, xout, "xin", "xout", me, "A", wg_sb)
    p.finalize()
    stack.close()
    return nc


S5_W_SHAPES = [("qkvwT", (1024, 3072)), ("p1wT", (1024, 1024)), ("p2wT", (1024, 1024)),
               ("qkvb_col", (128, 24)), ("qkvb_row", (1, 3072)), ("p1b_col", (128, 8)),
               ("p2b_col", (128, 8)), ("kng_row", (1, 1024)), ("knb_row", (1, 1024)),
               ("vng_row", (1, 1024)), ("vnb_row", (1, 1024)), ("tempcol", (16, 1)),
               ("biasgrid", (128, 512))]


def prep_s5_weights(qkv_w, qkv_b, kn_g, kn_b, vn_g, vn_b, temperature, attn_bias,
                    p1_w, p1_b, p2_w, p2_b, prefix=""):
    d = {}
    d[prefix + "qkvwT"] = np.ascontiguousarray(qkv_w.T)
    d[prefix + "p1wT"] = np.ascontiguousarray(p1_w.T)
    d[prefix + "p2wT"] = np.ascontiguousarray(p2_w.T)
    d[prefix + "qkvb_col"] = np.ascontiguousarray(qkv_b.reshape(24, 128).T)
    d[prefix + "qkvb_row"] = np.ascontiguousarray(qkv_b.reshape(1, 3072))
    d[prefix + "p1b_col"] = np.ascontiguousarray(p1_b.reshape(8, 128).T)
    d[prefix + "p2b_col"] = np.ascontiguousarray(p2_b.reshape(8, 128).T)
    d[prefix + "kng_row"] = np.tile(kn_g, 16).reshape(1, 1024)
    d[prefix + "knb_row"] = np.tile(kn_b, 16).reshape(1, 1024)
    d[prefix + "vng_row"] = np.tile(vn_g, 16).reshape(1, 1024)
    d[prefix + "vnb_row"] = np.tile(vn_b, 16).reshape(1, 1024)
    d[prefix + "tempcol"] = temperature.reshape(16, 1)
    bg = np.zeros((128, 512), np.float32)
    for h in range(16):
        bg[(h % 2) * 64:(h % 2) * 64 + 64, (h // 2) * 64:(h // 2) * 64 + 64] = attn_bias[h]
    d[prefix + "biasgrid"] = bg
    return {k: np.ascontiguousarray(v).astype(np.float32) for k, v in d.items()}


def build_l1attn_a_test():
    nc = bass.Bass("TRN2", target_bir_lowering=False)
    xin = nc.dram_tensor("xin", [D, T], F32, kind="ExternalInput")
    xout = nc.dram_tensor("xout", [D, T], F32, kind="ExternalOutput")
    w = {}
    for n, shp in S5_W_SHAPES:
        w[n] = nc.dram_tensor(n, list(shp), F32, kind="ExternalInput")
    stack = ExitStack()
    p = Prog(nc, stack)
    ctx = Ctx(p)
    with p.phase():
        emit_l1attn_a(ctx, xin, xout, "xin", "xout", w)
    p.finalize()
    stack.close()
    return nc


S3_W_SHAPES = [("wqT", (1024, 1024)), ("wkT", (1024, 1024)), ("wvT", (1024, 1024)),
               ("wsT", (1024, 256)), ("bq_col", (128, 8)), ("bk_col", (128, 8)),
               ("bv_row", (1, 1024)), ("bs_col", (128, 2))]


def prep_s3_weights(Wq, bq, Wk, bk, Wv, bv, Ws, bs, prefix=""):
    d = {}
    d[prefix + "wqT"] = Wq.T
    d[prefix + "wkT"] = Wk.T
    d[prefix + "wvT"] = Wv.T
    d[prefix + "wsT"] = Ws.T
    d[prefix + "bq_col"] = bq.reshape(8, 128).T
    d[prefix + "bk_col"] = bk.reshape(8, 128).T
    d[prefix + "bv_row"] = bv.reshape(1, 1024)
    d[prefix + "bs_col"] = bs.reshape(2, 128).T
    return {k: np.ascontiguousarray(v).astype(np.float32) for k, v in d.items()}


def build_l0attn_b_test():
    nc = bass.Bass("TRN2", target_bir_lowering=False)
    xin = nc.dram_tensor("xin", [D, T], F32, kind="ExternalInput")
    xout = nc.dram_tensor("xout", [D, T], F32, kind="ExternalOutput")
    attout = nc.dram_tensor("attout", [D, T], BF16, kind="Internal")
    w = {}
    for n, shp in S3_W_SHAPES:
        w[n] = nc.dram_tensor(n, list(shp), F32, kind="ExternalInput")
    stack = ExitStack()
    p = Prog(nc, stack)
    ctx = Ctx(p)
    with p.phase():
        emit_l0attn_b(ctx, xin, xout, "xin", "xout", w, attout)
    p.finalize()
    stack.close()
    return nc


S7_W_SHAPES = [("wqkvT", (1024, 3072)), ("woT", (1024, 1024)),
               ("bo_col", (128, 8)), ("gbcols", (128, 16))]


def prep_s7_weights(ln_g, ln_b, w_qkv, w_out, b_out, prefix=""):
    d = {}
    d[prefix + "wqkvT"] = w_qkv.T
    d[prefix + "woT"] = w_out.T
    d[prefix + "bo_col"] = b_out.reshape(8, 128).T
    d[prefix + "gbcols"] = np.concatenate(
        [ln_g.reshape(8, 128).T, ln_b.reshape(8, 128).T], axis=1)
    return {k: np.ascontiguousarray(v).astype(np.float32) for k, v in d.items()}


def build_l1attn_b_test():
    nc = bass.Bass("TRN2", target_bir_lowering=False)
    xin = nc.dram_tensor("xin", [D, T], F32, kind="ExternalInput")
    xout = nc.dram_tensor("xout", [D, T], F32, kind="ExternalOutput")
    attout = nc.dram_tensor("attout", [D, T], BF16, kind="Internal")
    w = {}
    for n, shp in S7_W_SHAPES:
        w[n] = nc.dram_tensor(n, list(shp), F32, kind="ExternalInput")
    stack = ExitStack()
    p = Prog(nc, stack)
    ctx = Ctx(p)
    with p.phase():
        emit_l1attn_b(ctx, xin, xout, "xin", "xout", w, attout)
    p.finalize()
    stack.close()
    return nc


S1_W_SHAPES = [("winxc", (1024, 2048)), ("winz", (1024, 2048)), ("wx", (2048, 33)),
               ("wxd_cols", (128, 16)),
               ("wout", (2048, 1024)), ("convcols", (128, 64)), ("convbcol", (128, 16)),
               ("dcols", (128, 16)), ("alogcol", (16, 1))]


def prep_s1_weights(W_in, conv_w, conv_b, W_x, A_log, D_, W_out, prefix=""):
    d = {}
    d[prefix + "winxc"] = W_in[:, 0:2048]
    d[prefix + "winz"] = W_in[:, 2048:4096]
    d[prefix + "wx"] = W_x
    d[prefix + "wxd_cols"] = W_x[:, 0].reshape(16, 128).T
    d[prefix + "wout"] = W_out
    d[prefix + "convcols"] = conv_w.reshape(16, 128, 4).transpose(1, 0, 2).reshape(128, 64)
    d[prefix + "convbcol"] = conv_b.reshape(16, 128).T
    d[prefix + "dcols"] = D_.reshape(16, 128).T
    d[prefix + "alogcol"] = A_log.reshape(16, 1)
    return {k: np.ascontiguousarray(v).astype(np.float32) for k, v in d.items()}


def build_mamba_test():
    nc = bass.Bass("TRN2", target_bir_lowering=False)
    xin = nc.dram_tensor("xin", [D, T], F32, kind="ExternalInput")
    xout = nc.dram_tensor("xout", [D, T], F32, kind="ExternalOutput")
    w = {}
    for n, shp in S1_W_SHAPES:
        w[n] = nc.dram_tensor(n, list(shp), F32, kind="ExternalInput")
    stack = ExitStack()
    p = Prog(nc, stack)
    ctx = Ctx(p)
    with p.phase():
        emit_mamba(ctx, xin, xout, "xin", "xout", w)
    p.finalize()
    stack.close()
    return nc


def build_l1ffn_test():
    nc = bass.Bass("TRN2", target_bir_lowering=False)
    xin = nc.dram_tensor("xin", [D, T], F32, kind="ExternalInput")
    xout = nc.dram_tensor("xout", [D, T], F32, kind="ExternalOutput")
    eselT = nc.dram_tensor("eselT", [D, 64], F32, kind="ExternalInput")
    keysc = nc.dram_tensor("keysc", [D, 8192], F32, kind="ExternalInput")
    valsc = nc.dram_tensor("valsc", [8192, D], F32, kind="ExternalInput")
    stack = ExitStack()
    p = Prog(nc, stack)
    ctx = Ctx(p)
    with p.phase():
        emit_l1ffn(ctx, xin, xout, "xin", "xout", eselT, keysc, valsc, "A")
    p.finalize()
    stack.close()
    return nc



S2_CONSTS = ("c_ones_col", "c_ones_colf", "c_ones_row", "c_id_f32", "c_oh8")
S1_CONSTS = ("c_ones_col", "c_ones_colf", "c_ones_row", "c_id_f32", "c_id_bf16",
             "c_oh16", "c_ones128", "c_utri", "c_ltris", "c_causmask")
S3_CONSTS = ("c_ones_col", "c_ones_colf", "c_ones_row", "c_id_f32", "c_erep2", "c_hmask")
S5_CONSTS = ("c_ones_col", "c_ones_colf", "c_ones_row", "c_id_f32", "c_tgmask", "c_tgsel")
S6_CONSTS = ("c_ones_col", "c_ones_colf", "c_ones_row", "c_id_f32", "c_oh64")
S7_CONSTS = ("c_ones_col", "c_ones_colf", "c_ones_row", "c_id_f32", "c_hmask",
             "c_swap32", "c_ropec4", "c_ropes4")


def _build_s1():
    return build_mamba_test(), S1_CONSTS


def _build_s2():
    return build_l0ffn_test(), S2_CONSTS


def _build_s3():
    return build_l0attn_b_test(), S3_CONSTS


def _build_s5():
    return build_l1attn_a_test(), S5_CONSTS


def _build_s6():
    return build_l1ffn_test(), S6_CONSTS


def _build_s7():
    return build_l1attn_b_test(), S7_CONSTS


_PROG_CACHE = {}


def _get_prog(name):
    if name not in _PROG_CACHE:
        _PROG_CACHE[name] = {"s1": _build_s1, "s2": _build_s2, "s3": _build_s3,
                             "s5": _build_s5, "s6": _build_s6, "s7": _build_s7}[name]()
    return _PROG_CACHE[name]


FUSED = True
_FUSED_CACHE = []


def build_fused():
    nc = bass.Bass("TRN2", target_bir_lowering=False)
    xin = nc.dram_tensor("xin", [D, T], F32, kind="ExternalInput")
    xout = nc.dram_tensor("xout", [D, T], F32, kind="ExternalOutput")
    xm = nc.dram_tensor("xm", [D, T], F32, kind="Internal")
    att3 = nc.dram_tensor("att3", [D, T], BF16, kind="Internal")
    att7 = nc.dram_tensor("att7", [D, T], BF16, kind="Internal")
    me = nc.dram_tensor("me", [1024, 8192], BF16, kind="Internal")
    w1 = {n: nc.dram_tensor(n, list(s), F32, kind="ExternalInput") for n, s in S1_W_SHAPES}
    win = nc.dram_tensor("w_in", [8, 4096, 1024], F32, kind="ExternalInput")
    wotT = nc.dram_tensor("w_outT", [8, 4096, 1024], F32, kind="ExternalInput")
    wgT = nc.dram_tensor("w_gateT", [D, 8], F32, kind="ExternalInput")
    w3 = {n: nc.dram_tensor(n, list(s), F32, kind="ExternalInput") for n, s in S3_W_SHAPES}
    w5 = {n: nc.dram_tensor(n, list(s), F32, kind="ExternalInput") for n, s in S5_W_SHAPES}
    eselT = nc.dram_tensor("eselT", [D, 64], F32, kind="ExternalInput")
    keysc = nc.dram_tensor("keysc", [D, 8192], F32, kind="ExternalInput")
    valsc = nc.dram_tensor("valsc", [8192, D], F32, kind="ExternalInput")
    w7 = {n: nc.dram_tensor(n, list(s), F32, kind="ExternalInput") for n, s in S7_W_SHAPES}
    stack = ExitStack()
    p = Prog(nc, stack)
    ctx = Ctx(p)
    wg_sb = p.sbuf("WGsb", [128, NPT * 8], BF16)
    p.dma("qgps", wg_sb[:, :].rearrange("p (pt e8) -> p pt e8", e8=8),
          wgT.ap().rearrange("(pt p) e8 -> p pt e8", p=128), writes=[rr("WGsb")])
    with p.phase():
        emit_mamba(ctx, xin, xm, "xin", "xm", w1)
    keyscb = nc.dram_tensor("keyscb", [D, 8192], BF16, kind="Internal")
    valscb = nc.dram_tensor("valscb", [8192, D], BF16, kind="Internal")
    with p.phase():
        emit_me_build(ctx, win, wotT, me)
        emit_precast(ctx, keysc, keyscb, D, 8192, "K")
        emit_precast(ctx, valsc, valscb, 8192, D, "V")
    with p.phase():
        emit_l0ffn(ctx, xm, xm, "xm", "xm", me, "A", wg_sb)
    with p.phase():
        emit_l0attn_b(ctx, xm, xm, "xm", "xm", w3, att3)
    with p.phase():
        emit_l0ffn(ctx, xm, xm, "xm", "xm", me, "B", wg_sb)
    with p.phase():
        emit_l1attn_a(ctx, xm, xm, "xm", "xm", w5)
    with p.phase():
        emit_l1ffn(ctx, xm, xm, "xm", "xm", eselT, keyscb, valscb, "C")
    with p.phase():
        emit_l1attn_b(ctx, xm, xm, "xm", "xm", w7, att7)
    with p.phase():
        emit_l1ffn(ctx, xm, xout, "xm", "xout", eselT, keyscb, valscb, "D")
    p.finalize()
    stack.close()
    return nc


ALL_CONSTS = ("c_ones_col", "c_ones_colf", "c_ones_row", "c_id_f32", "c_id_bf16",
              "c_oh8", "c_oh16", "c_oh64", "c_ones128", "c_utri", "c_ltris",
              "c_causmask", "c_erep2", "c_hmask", "c_tgmask", "c_tgsel",
              "c_swap32", "c_ropec4", "c_ropes4")


def kernel(**inputs):
    consts = build_consts()
    w1 = prep_s1_weights(inputs["L0_attn_a_W_in"], inputs["L0_attn_a_conv_w"],
                         inputs["L0_attn_a_conv_b"], inputs["L0_attn_a_W_x"],
                         inputs["L0_attn_a_A_log"], inputs["L0_attn_a_D"],
                         inputs["L0_attn_a_W_out"])
    w2 = {"w_in": np.ascontiguousarray(inputs["L0_ffn_w_in"], np.float32),
          "w_outT": np.ascontiguousarray(np.transpose(inputs["L0_ffn_w_out"], (0, 2, 1)).astype(np.float32)),
          "w_gateT": np.ascontiguousarray(np.asarray(inputs["L0_ffn_w_gate"]).T.astype(np.float32))}
    w3 = prep_s3_weights(inputs["L0_attn_b_Wq"], inputs["L0_attn_b_bq"],
                         inputs["L0_attn_b_Wk"], inputs["L0_attn_b_bk"],
                         inputs["L0_attn_b_Wv"], inputs["L0_attn_b_bv"],
                         inputs["L0_attn_b_Ws"], inputs["L0_attn_b_bs"])
    w5 = prep_s5_weights(inputs["L1_attn_a_qkv_w"], inputs["L1_attn_a_qkv_b"],
                         inputs["L1_attn_a_kn_g"], inputs["L1_attn_a_kn_b"],
                         inputs["L1_attn_a_vn_g"], inputs["L1_attn_a_vn_b"],
                         inputs["L1_attn_a_temperature"], inputs["L1_attn_a_attn_bias"],
                         inputs["L1_attn_a_p1_w"], inputs["L1_attn_a_p1_b"],
                         inputs["L1_attn_a_p2_w"], inputs["L1_attn_a_p2_b"])
    keys = np.asarray(inputs["L1_ffn_keys"], np.float32)
    vals = np.asarray(inputs["L1_ffn_values"], np.float32)
    w6 = {"eselT": np.ascontiguousarray(np.asarray(inputs["L1_ffn_expert_sel"], np.float32).T),
          "keysc": np.ascontiguousarray(keys.transpose(1, 0, 2).reshape(D, 8192)),
          "valsc": np.ascontiguousarray(vals.reshape(8192, D))}
    w7 = prep_s7_weights(inputs["L1_attn_b_ln_g"], inputs["L1_attn_b_ln_b"],
                         inputs["L1_attn_b_w_qkv"], inputs["L1_attn_b_w_out"],
                         inputs["L1_attn_b_b_out"])

    x = np.asarray(inputs["L0_attn_a_x"], np.float32)
    xs = [np.ascontiguousarray(x[c].T) for c in range(8)]

    if FUSED:
        if not _FUSED_CACHE:
            _FUSED_CACHE.append(build_fused())
        nc = _FUSED_CACHE[0]
        wall = {}
        for wd in (w1, w2, w3, w5, w6, w7):
            wall.update(wd)
        for n in ALL_CONSTS:
            wall[n] = consts[n]
        in_maps = []
        for c in range(8):
            m = dict(wall)
            m["xin"] = xs[c]
            in_maps.append(m)
        res = run_bass_kernel_spmd(nc, in_maps, core_ids=list(range(8)))
        xs = [res.results[c]["xout"] for c in range(8)]
        return np.stack([xs[c].T for c in range(8)]).astype(np.float32)

    seq = [("s1", w1), ("s2", w2), ("s3", w3), ("s2", w2),
           ("s5", w5), ("s6", w6), ("s7", w7), ("s6", w6)]
    for name, wd in seq:
        nc, cns = _get_prog(name)
        in_maps = []
        for c in range(8):
            m = dict(wd)
            m["xin"] = xs[c]
            for n in cns:
                m[n] = consts[n]
            in_maps.append(m)
        res = run_bass_kernel_spmd(nc, in_maps, core_ids=list(range(8)))
        xs = [np.ascontiguousarray(res.results[c]["xout"]) for c in range(8)]
    out = np.stack([xs[c].T for c in range(8)]).astype(np.float32)
    return out



